# version 14
# baseline (speedup 1.0000x reference)
import numpy as np
import concourse.bass as bass
import concourse.mybir as mybir
from concourse.bass_utils import run_bass_kernel_spmd
from contextlib import ExitStack
from types import SimpleNamespace

F32 = mybir.dt.float32
BF16 = mybir.dt.bfloat16
I8 = mybir.dt.int8
ALU = mybir.AluOpType
ACT = mybir.ActivationFunctionType
AX = mybir.AxisListType

B, T, D = 4, 2048, 2048
H, HKV, DS, DG, DV = 16, 4, 32, 32, 64
MB, LW = 64, 1024
NBLK, NK = 16, 1040
NEG = -1.0e9
NCN = 8
TOWN, CHT, NSLOT = 1024, 512, 8
IFF, M_I, KT_D = 8192, 64, 16
EPS = float(np.float32(1e-5))
C8192 = float(np.float32(1.0 / np.sqrt(8192.0)))
SLOT_LOC = [0, 0, 0, 0, 2, 4, 6, 8]
GSL = 128 * 2048
OWN_G = 8 * GSL
OWN_D = 8 * 128 * 8 * 256
OWN_N = 2 * OWN_G + OWN_D

DEBUG = False
PART = 3
DBG_FFN_CK = 0
STAGE = 99
SUBST = 99
POOL_BF = True
POOL_XI = True
ATTN_IL = True


def _bf16():
    import ml_dtypes
    return ml_dtypes.bfloat16


def _host_consts():
    bf16 = _bf16()
    c = {}
    c["ident"] = np.eye(128, dtype=bf16)
    c["identf"] = np.eye(128, dtype=np.float32)
    i = np.arange(128)
    a = i[:, None] & i[None, :]
    pop = np.zeros((128, 128), np.int64)
    for bb in range(7):
        pop += (a >> bb) & 1
    c["h128"] = np.where(pop % 2 == 0, 1.0, -1.0).astype(bf16)
    pm = np.zeros((128, 128), np.float32)
    for hh in range(4):
        for f in range(16):
            pm[hh * 32 + 16 + f, hh * 32 + f] = 1.0
            pm[hh * 32 + f, hh * 32 + 16 + f] = 1.0
    c["permrope"] = pm.astype(bf16)
    poolm = np.zeros((128, 128), np.float32)
    for tt in range(8):
        poolm[0:64, tt * 16 + 2 * tt] = 1.0 / 64.0
        poolm[64:128, tt * 16 + 2 * tt + 1] = 1.0 / 64.0
    c["poolm"] = poolm.astype(bf16)
    return c


def _rope_tables(pos):
    d = 32
    inv = 1.0 / (10000.0 ** (np.arange(0, d, 2, dtype=np.float64) / d))
    ang = pos.astype(np.float64)[None, :] * inv[:, None]
    cos64, sin64 = np.cos(ang), np.sin(ang)
    cf = np.tile(np.concatenate([cos64, cos64], 0), (4, 1))
    sf = np.tile(np.concatenate([-sin64, sin64], 0), (4, 1))

    def hilo(x):
        hi = x.astype(np.float32)
        return hi, (x - hi.astype(np.float64)).astype(np.float32)
    ch, cl = hilo(cf)
    sh, sl = hilo(sf)
    return ch, cl, sh, sl


def _own_positions(p):
    out = np.empty(TOWN, np.int64)
    for s in range(NSLOT):
        t = 2 * s + p
        out[s * 128:(s + 1) * 128] = np.arange(t * 128, (t + 1) * 128)
    return out


def _masks(p):
    bf16 = _bf16()
    kpp = np.arange(NBLK) * MB + (MB - 1)
    smaskp = np.zeros((128, 4 * 16), np.float32)
    tmaskp = np.zeros((16, 4 * 128), np.float32)
    for s in range(4):
        t = 2 * s + p
        q = np.arange(t * 128, (t + 1) * 128)
        vis = kpp[None, :] <= q[:, None]
        smaskp[:, s * 16:(s + 1) * 16] = np.where(vis, 0.0, NEG)
        tmaskp[:, s * 128:(s + 1) * 128] = np.where(vis.T, 0.0, NEG)
    smaskl = np.zeros((128, 4 * 2 * 128), np.float32)
    tmaskl = np.zeros((128, 4 * 2 * 128), np.float32)
    for js, s in enumerate(range(4, 8)):
        t = 2 * s + p
        q = np.arange(t * 128, (t + 1) * 128)
        for w in range(2):
            kt = SLOT_LOC[s] - 2 + w
            kidx = LW + np.arange(kt * 128, (kt + 1) * 128)
            vis = kidx[None, :] <= q[:, None]
            sl = slice((js * 2 + w) * 128, (js * 2 + w + 1) * 128)
            smaskl[:, sl] = np.where(vis, 0.0, NEG)
            tmaskl[:, sl] = np.where(vis.T, 0.0, NEG)
    blend = np.zeros((1, 128), np.float32)
    if p == 0:
        blend[0, :63] = 1.0
    return (smaskp.astype(bf16), tmaskp.astype(bf16), smaskl.astype(bf16),
            tmaskl.astype(bf16), blend)


ENGS = ("sync", "vector", "scalar", "tensor", "gpsimd")

DATA_OPS = {
    "tensor_tensor", "tensor_add", "tensor_sub", "tensor_mul", "tensor_max",
    "tensor_scalar", "tensor_single_scalar", "tensor_scalar_add",
    "tensor_scalar_sub", "tensor_scalar_mul", "tensor_scalar_max",
    "tensor_scalar_min", "tensor_copy", "tensor_reduce", "reduce_max",
    "reduce_sum", "scalar_tensor_tensor", "tensor_tensor_scan", "memset",
    "memzero", "reciprocal", "select", "activation", "copy", "add", "mul",
    "sign", "sqrt", "square", "iota", "transpose",
}
TENSOR_OPS = {"matmul", "transpose", "ldweights"}


class POp:
    __slots__ = ("kind", "name", "args", "kwargs", "incs", "val")

    def __init__(self, kind, name=None, args=(), kwargs=None, val=None):
        self.kind, self.name = kind, name
        self.args, self.kwargs = args, (kwargs or {})
        self.incs, self.val = [], val


class EngProxy:
    def __init__(self, pl, eng, chain=None):
        self._pl, self._eng, self._chain = pl, eng, chain

    def __getattr__(self, name):
        pl, eng, chain = self._pl, self._eng, self._chain

        def emit(*a, **kw):
            nochain = kw.pop("nochain", False)
            op = POp("op", name, a, kw)
            if chain is not None and name in DATA_OPS:
                if pl.cnt[chain] > 0 and not nochain:
                    pl.ops[eng].append(POp("wait", chain, val=pl.cnt[chain]))
                op.incs.append((chain, 1))
                pl.cnt[chain] += 1
            elif eng == "tensor" and name in TENSOR_OPS:
                pl.tcnt += 1
                pl.last_top = op
            pl.ops[eng].append(op)
            return op
        return emit


class Pl:
    def __init__(self):
        self.ops = {e: [] for e in ENGS}
        self.cnt = {}
        self.h = {}
        self.tcnt = 0
        self.last_top = None
        self.v = EngProxy(self, "vector", "vch")
        self.s = EngProxy(self, "scalar", "sch")
        self.t = EngProxy(self, "tensor")
        self.g = EngProxy(self, "gpsimd")
        self.p = EngProxy(self, "gpsimd", "pch")
        self.q = EngProxy(self, "sync")

    def newsem(self, name, handle):
        self.h[name] = handle
        self.cnt[name] = 0

    def dma(self, eng, out, in_, sem, **kw):
        op = POp("op", "dma_start", (), {"out": out, "in_": in_, **kw})
        op.incs.append((sem, 16))
        self.cnt[sem] += 16
        self.ops[eng].append(op)
        return self.cnt[sem]

    def wait(self, eng, sem, val=None):
        v = self.cnt[sem] if val is None else val
        if v > 0:
            self.ops[eng].append(POp("wait", sem, val=v))
        return v

    def vmark(self):
        return self.cnt["vch"]

    def smark(self):
        return self.cnt["sch"]

    def pmark(self):
        return self.cnt["pch"]

    def tmark(self):
        assert self.last_top is not None
        for (snm, k) in self.last_top.incs:
            if snm == "tms":
                return self.cnt["tms"]
        self.last_top.incs.append(("tms", 1))
        self.cnt["tms"] += 1
        return self.cnt["tms"]

    def bump(self, eng, sem, k=1):
        op = POp("op", "nop", (), {})
        op.incs.append((sem, k))
        self.cnt[sem] += k
        self.ops[eng].append(op)
        return self.cnt[sem]

    def barrier(self, name):
        tv = self.tmark()
        vv = self.cnt["vch"]
        sv = self.cnt["sch"]
        pv = self.cnt["pch"]
        for eng in ("sync", "gpsimd"):
            self.bump(eng, name)
        bn = self.cnt[name]
        for eng in ENGS:
            self.wait(eng, "tms", tv)
            self.wait(eng, "vch", vv)
            self.wait(eng, "sch", sv)
            self.wait(eng, "pch", pv)
            self.wait(eng, name, bn)
        return bn

    def replay(self, block):
        for eng_name in ENGS:
            def _mk(name):
                def _f(eng):
                    for op in self.ops[name]:
                        if op.kind == "wait":
                            eng.wait_ge(self.h[op.name], op.val)
                        else:
                            instr = getattr(eng, op.name)(*op.args, **op.kwargs)
                            for (snm, k) in op.incs:
                                instr.then_inc(self.h[snm], k)
                return _f
            getattr(block, eng_name)(_mk(eng_name))


R_RS32, R_TMP, R_R1040 = 0, 2, 4
R_LS, R_CH = 32, 48
R_WS = 64


def build_nc(debug=False, part=None):
    if part is None:
        part = PART
    nc = bass.Bass("TRN2", target_bir_lowering=False)
    pl = Pl()
    C = SimpleNamespace(nc=nc, pl=pl, debug=debug)

    def din(name, shape, dt=F32):
        return nc.dram_tensor(name, shape, dt, kind="ExternalInput").ap()

    C.xkv = din("xkv", [D, T])
    C.xq = din("xq", [D, TOWN])
    C.wqsem = din("wqsem", [D, H * DS])
    C.wqgeo = din("wqgeo", [D, H * DG])
    C.wksem = din("wksem", [D, HKV * DS])
    C.wkgeo = din("wkgeo", [D, HKV * DG])
    C.wv = din("wv", [D, HKV * DV])
    C.wo = din("wo", [H * DV, D])
    C.lsc = din("lsc", [1, H])
    C.wgch = din("wgch", [TOWN, D])
    C.wuch = din("wuch", [TOWN, D])
    C.wdch = din("wdch", [D, TOWN])
    C.ident = din("ident", [128, 128], BF16)
    C.identf = din("identf", [128, 128])
    C.h128 = din("h128", [128, 128], BF16)
    C.permrope = din("permrope", [128, 128], BF16)
    C.poolm = din("poolm", [128, 128], BF16)
    C.kcos_h = din("kcos_h", [128, T]); C.kcos_l = din("kcos_l", [128, T])
    C.ksin_h = din("ksin_h", [128, T]); C.ksin_l = din("ksin_l", [128, T])
    C.qcos_h = din("qcos_h", [128, TOWN]); C.qcos_l = din("qcos_l", [128, TOWN])
    C.qsin_h = din("qsin_h", [128, TOWN]); C.qsin_l = din("qsin_l", [128, TOWN])
    C.smaskp = din("smaskp", [128, 64], BF16)
    C.tmaskp = din("tmaskp", [16, 512], BF16)
    C.smaskl = din("smaskl", [128, 1024], BF16)
    C.tmaskl = din("tmaskl", [128, 1024], BF16)
    C.blendrow = din("blendrow", [1, 128])

    C.out = nc.dram_tensor("out", [D, TOWN], F32, kind="ExternalOutput").ap()
    C.dbg = {}
    if debug:
        for nm, sh, dt in [
            ("d_kct", [128, 4 * 1152], BF16), ("d_qct", [128, H * TOWN], BF16),
            ("d_v2t", [128, 9 * 260], BF16), ("d_meanv", [128, 4], F32),
            ("d_att", [128, 2 * 8 * CHT], BF16),
            ("d_x1", [128, 2 * 16 * CHT], F32),
            ("d_rows", [1, 128], F32), ("d_scr", [1, 16384], BF16),
            ("d_y", [128, M_I * CHT], BF16), ("d_xq", [128, 16 * CHT], BF16),
        ]:
            C.dbg[nm] = nc.dram_tensor(nm, sh, dt, kind="ExternalOutput").ap()

    C.mbounce = nc.dram_tensor("mbounce", [64, 128], BF16).ap()
    C.ccs_in = nc.dram_tensor("ccs_in", [1, 4], F32).ap()
    C.ccs_out = nc.dram_tensor("ccs_out", [1, 4], F32).ap()
    C.scr_own = nc.dram_tensor("scr_own", [1, OWN_N], BF16).ap()
    C.scr_all = nc.dram_tensor("scr_all", [NCN, OWN_N], BF16,
                               addr_space="Shared").ap()
    C.r2b = nc.dram_tensor("r2b", [D, CHT], F32).ap()
    C.x1d = nc.dram_tensor("x1d", [D, TOWN], F32).ap()
    C.xqd = nc.dram_tensor("xqd", [2 * D, CHT], BF16).ap()
    C.qbounce = nc.dram_tensor("qbounce", [32, 16], F32).ap()

    with ExitStack() as est:
        e = est.enter_context
        for nm in ("vch", "sch", "tms", "pch",
                   "ldcst", "ldmsk", "ldwA", "ldxq", "ldm", "ldwo",
                   "pw0", "pw1", "pq0", "pq1", "arb", "sw0", "sw1",
                   "agc", "arc", "boun", "xr0", "xr1",
                   "wg0", "wg1", "wu0", "wu1", "wd0", "wd1", "wd2", "wd3",
                   "x1r0", "x1r1", "r2w", "r2r0", "r2r1", "outw", "qbn",
                   "xqw",
                   "bar1", "bar2"):
            pl.newsem(nm, e(nc.semaphore(nm)))

        C.pb = [e(nc.psum_tensor("pb%d" % i, [128, 512], F32)) for i in range(6)]
        C.pst = e(nc.psum_tensor("pst", [128, 512], F32))
        C.ptr = e(nc.psum_tensor("ptr", [128, 1024], BF16))

        C.sident = e(nc.sbuf_tensor("sident", [128, 128], BF16))
        C.sidentf = e(nc.sbuf_tensor("sidentf", [128, 128], F32))
        C.sh128 = e(nc.sbuf_tensor("sh128", [128, 128], BF16))
        C.sperm = e(nc.sbuf_tensor("sperm", [128, 128], BF16))
        C.spoolm = e(nc.sbuf_tensor("spoolm", [128, 128], BF16))
        C.onescol_b = e(nc.sbuf_tensor("onescol_b", [128, 1], BF16))
        C.onescol_f = e(nc.sbuf_tensor("onescol_f", [128, 1], F32))
        C.onesrow_f = e(nc.sbuf_tensor("onesrow_f", [1, 128], F32))
        C.rows = e(nc.sbuf_tensor("rows", [1, 128], F32))
        C.rcol = e(nc.sbuf_tensor("rcol", [128, 16], F32))
        C.MEANV = e(nc.sbuf_tensor("MEANV", [128, 4], F32))
        C.RCONST = e(nc.sbuf_tensor("RCONST", [128, 4], F32))
        C.CVALL = e(nc.sbuf_tensor("CVALL", [128, 16], F32))
        C.s_blend = e(nc.sbuf_tensor("s_blend", [1, 128], F32))
        C.TROWS = e(nc.sbuf_tensor("TROWS", [1, 2048], F32))
        C.PMROW = e(nc.sbuf_tensor("PMROW", [1, 2048], F32))
        C.aq_v = {}
        C.xqw_done = {}
        C.qs_v = {}
        C.xres_free = {}
        C.WMC = e(nc.sbuf_tensor("WMC", [128, 4], F32))

        _emit_base_loads(C)
        _emit_derived(C)

        with ExitStack() as estAB:
            eab = estAB.enter_context
            C.KCT = eab(nc.sbuf_tensor("KCT", [128, 4 * 1152], BF16))
            C.V2T = eab(nc.sbuf_tensor("V2T", [128, 9 * 260], BF16))
            C.QCT = eab(nc.sbuf_tensor("QCT", [128, H * TOWN], BF16))
            C.s_smaskp = eab(nc.sbuf_tensor("s_smaskp", [128, 64], BF16))
            C.s_tmaskp = eab(nc.sbuf_tensor("s_tmaskp", [16, 512], BF16))
            C.s_smaskl = eab(nc.sbuf_tensor("s_smaskl", [128, 1024], BF16))
            C.s_tmaskl = eab(nc.sbuf_tensor("s_tmaskl", [128, 1024], BF16))
            _emit_ab_loads(C)
            with ExitStack() as estA:
                if STAGE >= 4:
                    gA = _gen_phaseA(C, estA.enter_context)
                    gP = _gen_prepass(C, estA.enter_context)

                    def pump(g, n):
                        for _ in range(n):
                            try:
                                next(g)
                            except StopIteration:
                                return False
                        return True

                    pump(gA, 1)
                    for _ in range(4):
                        pump(gA, 1)
                        pump(gP, 2)
                    pump(gP, 2)
                    pump(gA, 1)
                    while True:
                        alive = pump(gA, 1)
                        pump(gP, 2)
                        if not alive:
                            break
                    while pump(gP, 8):
                        pass
                elif STAGE >= 1:
                    for _ in _gen_phaseA(C, estA.enter_context):
                        pass
                if debug:
                    _emit_dbgA(C)
            if part >= 2:
                pl.barrier("bar1")
                with ExitStack() as estB:
                    _emit_attn_all(C, estB.enter_context)
        if part >= 3:
            pl.barrier("bar2")
            with ExitStack() as estF:
                _emit_ffn_all(C, estF.enter_context)
        pl.wait("gpsimd", "outw")
        pl.wait("gpsimd", "r2w")

        block = e(nc.Block())
        pl.replay(block)
    return nc


def _emit_base_loads(C):
    pl = C.pl
    g = lambda t, s: pl.dma("gpsimd", t, s, "ldcst")
    g(C.sident[:], C.ident)
    g(C.sidentf[:], C.identf)
    g(C.sh128[:], C.h128)
    g(C.sperm[:], C.permrope)
    g(C.spoolm[:], C.poolm)
    g(C.s_blend[:], C.blendrow)
    g(C.rows[0:1, R_LS:R_LS + 16], C.lsc)
    C.BASELOAD = pl.cnt["ldcst"]


def _emit_ab_loads(C):
    pl = C.pl
    g = lambda t, s: pl.dma("gpsimd", t, s, "ldmsk")
    g(C.s_smaskp[:], C.smaskp)
    g(C.s_tmaskp[:], C.tmaskp)
    g(C.s_smaskl[:], C.smaskl)
    g(C.s_tmaskl[:], C.tmaskl)
    C.MSKLOAD = pl.cnt["ldmsk"]


def _emit_derived(C):
    pl, rows = C.pl, C.rows
    pl.wait("vector", "ldcst", C.BASELOAD)
    pl.v.memset(rows[0:1, 0:32], 0.0)
    pl.v.memset(rows[0:1, 48:128], 0.0, nochain=True)
    pl.v.memset(C.onescol_b[:], 1.0)
    pl.v.memset(C.onescol_f[:], 1.0)
    pl.v.memset(C.onesrow_f[:], 1.0)
    pl.v.memset(rows[0:1, R_RS32:R_RS32 + 1], 0.1767766952966369)
    for _ in range(2):
        pl.v.tensor_tensor(rows[0:1, R_TMP:R_TMP + 1],
                           rows[0:1, R_RS32:R_RS32 + 1],
                           rows[0:1, R_RS32:R_RS32 + 1], ALU.mult)
        pl.v.tensor_scalar(rows[0:1, R_TMP:R_TMP + 1],
                           rows[0:1, R_TMP:R_TMP + 1],
                           -16.0, 1.5, ALU.mult, ALU.add)
        pl.v.tensor_tensor(rows[0:1, R_RS32:R_RS32 + 1],
                           rows[0:1, R_RS32:R_RS32 + 1],
                           rows[0:1, R_TMP:R_TMP + 1], ALU.mult)
    pl.v.memset(rows[0:1, R_R1040:R_R1040 + 1], 1040.0)
    pl.v.reciprocal(rows[0:1, R_R1040:R_R1040 + 1],
                    rows[0:1, R_R1040:R_R1040 + 1])
    pl.wait("scalar", "vch")
    pl.s.activation(rows[0:1, R_CH:R_CH + 16], rows[0:1, R_LS:R_LS + 16],
                    ACT.Exp)
    pl.wait("vector", "sch")
    pl.v.tensor_single_scalar(rows[0:1, R_CH:R_CH + 16],
                              rows[0:1, R_CH:R_CH + 16],
                              rows[0:1, R_RS32:R_RS32 + 1], ALU.mult)
    pl.wait("tensor", "vch")
    pl.t.matmul(C.pb[5][:, 0:16], C.onesrow_f[:], rows[0:1, R_CH:R_CH + 16],
                start=True, stop=True)
    pl.t.matmul(C.pb[5][:, 16:20], C.onesrow_f[:],
                rows[0:1, R_R1040:R_R1040 + 4], start=True, stop=True)
    tv = pl.tmark()
    pl.wait("vector", "tms", tv)
    pl.v.tensor_copy(C.CVALL[:], C.pb[5][:, 0:16])
    pl.v.tensor_copy(C.RCONST[:], C.pb[5][:, 16:20])
    C.CVREADY = pl.vmark()


def _gen_phaseA(C, e):
    nc, pl = C.nc, C.pl
    QT = 512
    XTB = e(nc.sbuf_tensor("XTB", [128, 16 * QT], BF16))
    C.WQS = e(nc.sbuf_tensor("WQS", [128, 16 * 512], BF16))
    C.WQG = e(nc.sbuf_tensor("WQG", [128, 16 * 512], BF16))
    WKS = e(nc.sbuf_tensor("WKS", [128, 16 * 128], BF16))
    WKG = e(nc.sbuf_tensor("WKG", [128, 16 * 128], BF16))
    WVV = e(nc.sbuf_tensor("WVV", [128, 16 * 256], BF16))
    tabs = {nm: e(nc.sbuf_tensor("tab_" + nm, [128, QT], F32))
            for nm in ("ch", "cl", "sh", "sl")}
    KG = e(nc.sbuf_tensor("KG", [128, QT], F32))
    KGR = e(nc.sbuf_tensor("KGR", [128, QT], F32))
    KTMP = e(nc.sbuf_tensor("KTMP", [128, 2 * QT], F32))
    KGB = e(nc.sbuf_tensor("KGB", [128, QT], BF16))
    VTQ = e(nc.sbuf_tensor("VTQ", [128, 8 * 256], BF16))
    QSM = e(nc.sbuf_tensor("QSM", [128, 4 * 512], F32))
    QGO = e(nc.sbuf_tensor("QGO", [128, 4 * 512], F32))

    for (wbuf, src) in ((WKS, C.wksem), (WKG, C.wkgeo), (WVV, C.wv),
                        (C.WQS, C.wqsem), (C.WQG, C.wqgeo)):
        pl.dma("gpsimd", wbuf[:].rearrange("p (k c) -> p k c", k=16),
               src.rearrange("(k p) c -> p k c", p=128), "ldwA")
    C.WLOAD2 = pl.cnt["ldwA"]
    pl.v.memset(C.KCT[:], 0.0)
    vev = {}
    xtb_tms = 0
    yield

    for qq in range(4):
        tok0 = qq * QT
        if qq > 0:
            pl.wait("gpsimd", "tms", xtb_tms)
        pl.dma("gpsimd", XTB[:].rearrange("p (k c) -> p k c", k=16),
               C.xkv[:, tok0:tok0 + QT].rearrange("(k p) c -> p k c", p=128),
               "ldxq")
        for nm, src in (("ch", C.kcos_h), ("cl", C.kcos_l),
                        ("sh", C.ksin_h), ("sl", C.ksin_l)):
            if qq > 0:
                pl.wait("scalar", "vch", pl.vmark())
            pl.dma("scalar", tabs[nm][:], src[:, tok0:tok0 + QT], "ldm")
        XL = pl.cnt["ldxq"]
        TL = pl.cnt["ldm"]
        pl.wait("tensor", "ldxq", XL)
        pl.wait("tensor", "ldwA", C.WLOAD2)
        ev = {}
        for wi, wbuf in enumerate((WKS, WKG)):
            bi = wi
            if bi in vev:
                pl.wait("tensor", "vch", vev[bi])
            for kt in range(16):
                pl.t.matmul(C.pb[bi][:, 0:512],
                            wbuf[:, kt * 128:(kt + 1) * 128],
                            XTB[:, kt * QT:(kt + 1) * QT],
                            start=(kt == 0), stop=(kt == 15))
            tv = pl.tmark()
            pl.wait("vector", "tms", tv)
            if wi == 0:
                if qq < 2:
                    for g in range(4):
                        base = g * 1152
                        pl.v.reduce_sum(C.rcol[0:32, 0:8],
                                        C.pb[bi][g * 32:(g + 1) * 32, 0:512]
                                        .rearrange("p (b w) -> p b w", w=64),
                                        AX.X)
                        pl.v.tensor_scalar(
                            C.KCT[0:32, base + qq * 8:base + qq * 8 + 8],
                            C.rcol[0:32, 0:8], 1.0 / 64.0, None, ALU.mult)
                else:
                    for g in range(4):
                        base = g * 1152
                        pl.v.tensor_copy(
                            C.KCT[0:32, base + 16 + (qq - 2) * 512:
                                  base + 16 + (qq - 1) * 512],
                            C.pb[bi][g * 32:(g + 1) * 32, 0:512],
                            nochain=(g > 0))
            else:
                pl.v.tensor_copy(KG[:], C.pb[bi][:, 0:512])
            ev[wi] = pl.vmark()
            vev[bi] = ev[wi]
        pl.v.tensor_copy(KGB[:], KG[:])
        kgb_v = pl.vmark()
        pl.wait("tensor", "vch", kgb_v)
        pl.t.matmul(C.pb[2][:, 0:512], C.sperm[:], KGB[:], start=True,
                    stop=True)
        tv = pl.tmark()
        pl.wait("vector", "tms", tv)
        pl.wait("vector", "ldm", TL)
        pl.v.tensor_tensor(KGR[:], KG[:], tabs["ch"][:], ALU.mult)
        pl.v.tensor_tensor(KTMP[:, 0:512], KG[:], tabs["cl"][:], ALU.mult,
                           nochain=True)
        pl.v.tensor_tensor(KGR[:], KGR[:], KTMP[:, 0:512], ALU.add)
        pl.v.tensor_copy(KTMP[:, 0:512], C.pb[2][:, 0:512])
        pl.v.tensor_tensor(KTMP[:, 0:512], KTMP[:, 0:512], tabs["sh"][:],
                           ALU.mult)
        pl.v.tensor_tensor(KGR[:], KGR[:], KTMP[:, 0:512], ALU.add)
        pl.v.tensor_copy(KTMP[:, 0:512], C.pb[2][:, 0:512])
        pl.v.tensor_tensor(KTMP[:, 0:512], KTMP[:, 0:512], tabs["sl"][:],
                           ALU.mult)
        pl.v.tensor_tensor(KGR[:], KGR[:], KTMP[:, 0:512], ALU.add)
        if qq == 0:
            for g in range(4):
                pl.v.memset(C.KCT[64:68, g * 1152:g * 1152 + 1040], 1.0,
                            nochain=(g > 0))
        if qq < 2:
            for g in range(4):
                base = g * 1152
                pl.v.reduce_sum(C.rcol[0:32, 0:8],
                                KGR[g * 32:(g + 1) * 32, :]
                                .rearrange("p (b w) -> p b w", w=64), AX.X)
                pl.v.tensor_scalar(
                    C.KCT[32:64, base + qq * 8:base + qq * 8 + 8],
                    C.rcol[0:32, 0:8], 1.0 / 64.0, None, ALU.mult)
        else:
            for g in range(4):
                base = g * 1152
                pl.v.tensor_copy(C.KCT[32:64, base + 16 + (qq - 2) * 512:
                                       base + 16 + (qq - 1) * 512],
                                 KGR[g * 32:(g + 1) * 32, :],
                                 nochain=(g > 0))
        kct_v = pl.vmark()

        for ti in range(4):
            tt = qq * 4 + ti
            bi = 3 + tt % 2
            if bi in vev:
                pl.wait("tensor", "vch", vev[bi])
            for kt in range(16):
                pl.t.matmul(C.pb[bi][:, 0:256],
                            XTB[:, kt * QT + ti * 128:kt * QT + (ti + 1) * 128],
                            WVV[:, kt * 256:(kt + 1) * 256],
                            start=(kt == 0), stop=(kt == 15))
            tv = pl.tmark()
            pl.wait("vector", "tms", tv)
            if tt < 8:
                pl.v.tensor_copy(VTQ[:, tt * 256:(tt + 1) * 256],
                                 C.pb[bi][:, 0:256])
            else:
                kt2 = tt - 7
                for g in range(4):
                    pl.v.tensor_copy(
                        C.V2T[:, kt2 * 260 + g * 65:kt2 * 260 + g * 65 + 64],
                        C.pb[bi][:, g * 64:(g + 1) * 64], nochain=(g > 0))
            vev[bi] = pl.vmark()
        xtb_tms = tv
        yield

    pl.wait("tensor", "vch", pl.vmark())
    for tt in range(8):
        pl.t.matmul(C.pb[5][0:16, 0:256], C.spoolm[:, tt * 16:(tt + 1) * 16],
                    VTQ[:, tt * 256:(tt + 1) * 256],
                    start=(tt == 0), stop=(tt == 7))
    tv = pl.tmark()
    pl.wait("vector", "tms", tv)
    pl.v.memset(C.V2T[:, 0:260], 0.0)
    for g in range(4):
        pl.v.tensor_copy(C.V2T[0:16, g * 65:g * 65 + 64],
                         C.pb[5][0:16, g * 64:(g + 1) * 64], nochain=(g > 0))
    for kt in range(9):
        pp = 16 if kt == 0 else 128
        pl.v.memset(C.V2T[0:pp, kt * 260:(kt + 1) * 260]
                    .rearrange("p (g c) -> p g c", g=4)[:, :, 64:65], 1.0,
                    nochain=(kt > 0))
    C.V2T_V = pl.vmark()
    pl.wait("tensor", "vch", C.V2T_V)
    for g in range(4):
        for kt in range(9):
            pp = 16 if kt == 0 else 128
            pl.t.matmul(C.pb[5][0:65, g:g + 1],
                        C.V2T[0:pp, kt * 260 + g * 65:kt * 260 + (g + 1) * 65],
                        C.onescol_b[0:pp, 0:1], start=(kt == 0), stop=(kt == 8))
    tv = pl.tmark()
    pl.wait("vector", "tms", tv)
    pl.v.tensor_single_scalar(C.MEANV[0:64, 0:4], C.pb[5][0:64, 0:4],
                              C.RCONST[0:64, 0:1], ALU.mult)
    pl.v.tensor_single_scalar(C.MEANV[64:128, 0:4], C.pb[5][0:64, 0:4],
                              C.RCONST[0:64, 0:1], ALU.mult, nochain=True)
    C.MEANV_V = pl.vmark()
    yield

    for hf in range(2):
        tk0 = hf * 512
        pl.wait("gpsimd", "tms", xtb_tms)
        pl.dma("gpsimd", XTB[:].rearrange("p (k c) -> p k c", k=16),
               C.xq[:, tk0:tk0 + 512].rearrange("(k p) c -> p k c", p=128),
               "ldxq")
        for nm, src in (("ch", C.qcos_h), ("cl", C.qcos_l),
                        ("sh", C.qsin_h), ("sl", C.qsin_l)):
            pl.wait("scalar", "vch", pl.vmark())
            pl.dma("scalar", tabs[nm][:], src[:, tk0:tk0 + 512], "ldm")
        XL = pl.cnt["ldxq"]
        TL = pl.cnt["ldm"]
        yield
        pl.wait("tensor", "ldxq", XL)
        QSEM, QGEO = QSM, QGO
        ev = {}
        for wi, (wbuf, dstq) in enumerate(((C.WQS, QSEM), (C.WQG, QGEO))):
            for mt in range(4):
                bi = (wi * 4 + mt) % 2
                if bi in ev:
                    pl.wait("tensor", "vch", ev[bi])
                elif bi in vev:
                    pl.wait("tensor", "vch", vev[bi])
                for kt in range(16):
                    pl.t.matmul(C.pb[bi][:, 0:512],
                                wbuf[:, kt * 512 + mt * 128:
                                     kt * 512 + (mt + 1) * 128],
                                XTB[:, kt * 512:(kt + 1) * 512],
                                start=(kt == 0), stop=(kt == 15))
                tv = pl.tmark()
                pl.wait("vector", "tms", tv)
                pl.wait("vector", "vch", C.CVREADY)
                for hh in range(4):
                    h = mt * 4 + hh
                    pl.v.tensor_scalar(dstq[hh * 32:(hh + 1) * 32,
                                            mt * 512:(mt + 1) * 512],
                                       C.pb[bi][hh * 32:(hh + 1) * 32, 0:512],
                                       C.CVALL[hh * 32:hh * 32 + 32, h:h + 1],
                                       None, ALU.mult, nochain=(hh > 0))
                ev[bi] = pl.vmark()
            yield
        xtb_tms = tv
        for mt in range(4):
            pl.v.tensor_copy(KGB[:], QGEO[:, mt * 512:(mt + 1) * 512])
            qgb_v = pl.vmark()
            pl.wait("tensor", "vch", qgb_v)
            pl.t.matmul(C.pb[2][:, 0:512], C.sperm[:], KGB[:],
                        start=True, stop=True)
            tv = pl.tmark()
            pl.wait("vector", "tms", tv)
            pl.wait("vector", "ldm", TL)
            sl = slice(mt * 512, (mt + 1) * 512)
            pl.v.tensor_tensor(KTMP[:, 0:512], QGEO[:, sl], tabs["ch"][:],
                               ALU.mult)
            pl.v.tensor_tensor(KTMP[:, 512:1024], QGEO[:, sl], tabs["cl"][:],
                               ALU.mult, nochain=True)
            pl.v.tensor_tensor(KTMP[:, 0:512], KTMP[:, 0:512],
                               KTMP[:, 512:1024], ALU.add)
            pl.v.tensor_copy(KTMP[:, 512:1024], C.pb[2][:, 0:512])
            pl.v.tensor_tensor(KTMP[:, 512:1024], KTMP[:, 512:1024],
                               tabs["sh"][:], ALU.mult)
            pl.v.tensor_tensor(KTMP[:, 0:512], KTMP[:, 0:512],
                               KTMP[:, 512:1024], ALU.add)
            pl.v.tensor_copy(KTMP[:, 512:1024], C.pb[2][:, 0:512])
            pl.v.tensor_tensor(KTMP[:, 512:1024], KTMP[:, 512:1024],
                               tabs["sl"][:], ALU.mult)
            pl.v.tensor_tensor(QGEO[:, sl], KTMP[:, 0:512],
                               KTMP[:, 512:1024], ALU.add)
            yield
        for h in range(16):
            mt, hh = h // 4, h % 4
            col = h * TOWN + hf * 512
            pl.v.tensor_copy(C.QCT[0:32, col:col + 512],
                             QSEM[hh * 32:(hh + 1) * 32,
                                  mt * 512:(mt + 1) * 512],
                             nochain=(h > 0))
            pl.v.tensor_copy(C.QCT[32:64, col:col + 512],
                             QGEO[hh * 32:(hh + 1) * 32,
                                  mt * 512:(mt + 1) * 512], nochain=True)
    pl.v.memset(C.QCT[64:128, :], 0.0)
    C.QCT_V = pl.vmark()


def _gen_prepass(C, e):
    nc, pl, rows = C.nc, C.pl, C.rows
    WFP = e(nc.sbuf_tensor("WFP", [128, 2 * 2048], F32))
    WQB = e(nc.sbuf_tensor("WQB", [128, 2048], BF16))
    SLAB = e(nc.sbuf_tensor("SLAB", [128, 2 * 2048], BF16))
    PSUMS = e(nc.sbuf_tensor("PSUMS", [128, 32], F32))
    WSCOL = e(nc.sbuf_tensor("WSCOL", [128, 8], F32))

    wtiles = [("g", C.wgch, i, 2048) for i in range(8)] + \
             [("u", C.wuch, i, 2048) for i in range(8)] + \
             [("d", C.wdch, i, 1024) for i in range(16)]

    red_vs = []
    for ti, (kind, ap, i, ncol) in enumerate(wtiles):
        buf = WFP[:, (ti % 2) * 2048:(ti % 2) * 2048 + ncol]
        sem = "pw%d" % (ti % 2)
        if ti >= 2:
            pl.wait("sync", "vch", red_vs[ti - 2])
        pl.dma("sync", buf, ap[i * 128:(i + 1) * 128, 0:ncol], sem)
        pl.wait("vector", sem, pl.cnt[sem])
        pl.v.reduce_sum(PSUMS[:, ti:ti + 1], buf, AX.X,
                        apply_absolute_value=True, nochain=True)
        red_vs.append(pl.vmark())
        if ti % 4 == 3:
            yield
    if SUBST < 1:
        return
    pl.wait("tensor", "vch", red_vs[-1])
    pl.t.matmul(C.pst[0:1, 0:32], C.onescol_f[:], PSUMS[:, 0:32],
                start=True, stop=True)
    tv = pl.tmark()
    pl.wait("vector", "tms", tv)
    pl.v.reduce_sum(rows[0:1, R_WS + 0:R_WS + 1], C.pst[0:1, 0:8], AX.X)
    pl.v.reduce_sum(rows[0:1, R_WS + 1:R_WS + 2], C.pst[0:1, 8:16], AX.X,
                    nochain=True)
    pl.v.reduce_sum(rows[0:1, R_WS + 2:R_WS + 3], C.pst[0:1, 16:32], AX.X,
                    nochain=True)
    pl.v.memset(rows[0:1, R_WS + 3:R_WS + 4], 0.0, nochain=True)
    if SUBST < 2:
        return
    pl.wait("gpsimd", "vch", pl.vmark())
    pl.dma("gpsimd", C.ccs_in, rows[0:1, R_WS:R_WS + 4], "arb")
    pl.wait("gpsimd", "arb", pl.cnt["arb"])
    op = pl.g.collective_compute(
        "AllReduce", ALU.add, replica_groups=[list(range(NCN))],
        ins=[C.ccs_in], outs=[C.ccs_out])
    op.incs.append(("arc", 1))
    pl.cnt["arc"] += 1
    pl.wait("gpsimd", "arc")
    pl.dma("gpsimd", rows[0:1, R_WS + 4:R_WS + 8], C.ccs_out, "arb")
    pl.wait("vector", "arb", pl.cnt["arb"])
    pl.v.tensor_scalar(rows[0:1, R_WS + 8:R_WS + 11],
                       rows[0:1, R_WS + 4:R_WS + 7],
                       1.0 / 16777216.0, EPS, ALU.mult, ALU.max)
    pl.v.reciprocal(rows[0:1, R_WS + 11:R_WS + 14],
                    rows[0:1, R_WS + 8:R_WS + 11])
    pl.v.reciprocal(rows[0:1, R_WS + 14:R_WS + 17],
                    rows[0:1, R_WS + 11:R_WS + 14])
    scl_v = pl.vmark()
    pl.wait("tensor", "vch", scl_v)
    pl.t.matmul(C.pst[:, 40:46], C.onesrow_f[:],
                rows[0:1, R_WS + 11:R_WS + 17], start=True, stop=True)
    tv = pl.tmark()
    pl.wait("vector", "tms", tv)
    pl.v.tensor_copy(WSCOL[:, 0:6], C.pst[:, 40:46])
    pl.v.tensor_copy(C.WMC[:, 0:3], C.pst[:, 43:46], nochain=True)
    scl_v = pl.vmark()
    yield
    if SUBST < 3:
        return

    tern_vs, tF_after, scrw_after = [], [], []
    evac_v = pl.vmark()
    down_base = 2 * OWN_G
    for jj, (kind, ap, i, ncol) in enumerate(wtiles):
        scol = {"g": 0, "u": 1, "d": 2}[kind]
        mcol = 3 + {"g": 0, "u": 1, "d": 2}[kind]
        buf = WFP[:, (jj % 2) * 2048:(jj % 2) * 2048 + ncol]
        sem = "pq%d" % (jj % 2)
        if jj >= 2:
            pl.wait("sync", "vch", tern_vs[jj - 2])
        else:
            pl.wait("sync", "vch", red_vs[30 + jj])
        pl.dma("sync", buf, ap[i * 128:(i + 1) * 128, 0:ncol], sem)
        pl.wait("vector", sem, pl.cnt[sem])
        pl.wait("vector", "vch", scl_v)
        if jj >= 1:
            pl.wait("vector", "tms", tF_after[jj - 1])
        if jj >= 2:
            pl.wait("vector", "sw%d" % (jj % 2), scrw_after[jj - 2])
        z = buf
        pl.v.tensor_single_scalar(z, buf, WSCOL[:, scol:scol + 1], ALU.mult,
                                  nochain=True)
        pl.v.tensor_scalar(WQB[:, 0:ncol], z, -0.5, None, ALU.is_lt)
        pl.v.scalar_tensor_tensor(WQB[:, 0:ncol], z, 0.5, WQB[:, 0:ncol],
                                  ALU.is_gt, ALU.subtract)
        tern_vs.append(pl.vmark())
        par = jj % 2
        if SUBST < 4:
            tF_after.append(pl.cnt["tms"])
            scrw_after.append(pl.cnt["sw%d" % par])
            continue
        pl.wait("tensor", "vch", tern_vs[-1])
        for grp in range(ncol // 1024):
            pl.wait("tensor", "vch", evac_v)
            for j in range(8):
                bkk = grp * 8 + j
                pl.t.transpose(C.ptr[:, j * 128:(j + 1) * 128],
                               WQB[:, bkk * 128:(bkk + 1) * 128], C.sident[:])
            tv = pl.tmark()
            pl.wait("vector", "tms", tv)
            pl.v.tensor_copy(SLAB[:, par * 2048 + grp * 1024:
                                  par * 2048 + (grp + 1) * 1024],
                             C.ptr[:, 0:1024], nochain=True)
            evac_v = pl.vmark()
        tF_after.append(pl.cnt["tms"])
        if SUBST < 5:
            scrw_after.append(pl.cnt["sw%d" % par])
            continue
        pl.wait("gpsimd", "vch", evac_v)
        ssem = "sw%d" % par
        if kind in ("g", "u"):
            off = (0 if kind == "g" else OWN_G) + i * GSL
            dst = C.scr_own[0, off:off + GSL].rearrange("(p n) -> p n", p=128)
            pl.dma("gpsimd", dst, SLAB[:, par * 2048:(par + 1) * 2048], ssem)
        else:
            otp, oh = i // 2, i % 2
            base_o = down_base + otp * (128 * 2048)
            dst = C.scr_own[0, base_o:base_o + 128 * 2048].rearrange(
                "(p k c) -> p k c", p=128, k=8)[:, :, oh * 128:(oh + 1) * 128]
            pl.dma("gpsimd", dst,
                   SLAB[:, par * 2048:par * 2048 + 1024]
                   .rearrange("p (k c) -> p k c", k=8), ssem)
        scrw_after.append(pl.cnt[ssem])
        yield
    if C.debug and STAGE >= 5:
        _emit_allgather(C)


def _emit_allgather(C):
    pl = C.pl
    pl.wait("gpsimd", "sw0", pl.cnt["sw0"])
    pl.wait("gpsimd", "sw1", pl.cnt["sw1"])
    pl.wait("gpsimd", "r2w", pl.cnt["r2w"])
    pl.wait("gpsimd", "xqw", pl.cnt["xqw"])
    pl.wait("gpsimd", "ldxq", pl.cnt["ldxq"])
    op = pl.g.collective_compute(
        "AllGather", ALU.bypass, replica_groups=[list(range(NCN))],
        ins=[C.scr_own], outs=[C.scr_all])
    op.incs.append(("agc", 1))
    pl.cnt["agc"] += 1
    C.AGDONE = pl.cnt["agc"]


def _emit_dbgA(C):
    pl = C.pl
    pl.wait("gpsimd", "vch", pl.vmark())
    if STAGE >= 1:
        pl.dma("gpsimd", C.dbg["d_kct"], C.KCT[:], "outw")
        pl.dma("gpsimd", C.dbg["d_qct"], C.QCT[:], "outw")
        pl.dma("gpsimd", C.dbg["d_v2t"], C.V2T[:], "outw")
        pl.dma("gpsimd", C.dbg["d_meanv"], C.MEANV[:], "outw")
    pl.dma("gpsimd", C.dbg["d_rows"], C.rows[:], "outw")
    if STAGE < 5:
        return
    pl.wait("gpsimd", "agc", C.AGDONE)
    pl.dma("gpsimd", C.dbg["d_scr"][0:1, 0:4096], C.scr_all[0:1, 0:4096],
           "outw")
    pl.dma("gpsimd", C.dbg["d_scr"][0:1, 4096:8192],
           C.scr_all[3:4, OWN_G:OWN_G + 4096], "outw")
    pl.dma("gpsimd", C.dbg["d_scr"][0:1, 8192:12288],
           C.scr_all[6:7, 2 * OWN_G:2 * OWN_G + 4096], "outw")


def _emit_attn_all(C, e):
    nc, pl = C.nc, C.pl
    WOO = e(nc.sbuf_tensor("WOO", [128, 8 * 2048], BF16))
    EXPB = e(nc.sbuf_tensor("EXPB", [128, 2 * 9 * 512], BF16))
    ATT = e(nc.sbuf_tensor("ATT", [128, 8 * 512], BF16))
    X1C = e(nc.sbuf_tensor("X1C", [128, 16 * CHT], F32))
    MCOL = e(nc.sbuf_tensor("MCOL", [128, 64], F32))
    MROW = e(nc.sbuf_tensor("MROW", [64, 128], BF16))
    XRES = e(nc.sbuf_tensor("XRES", [128, 2 * 512], F32))
    BCB = e(nc.sbuf_tensor("BCB", [128, 1024], F32))
    BL1 = e(nc.sbuf_tensor("BL1", [128, 128], F32))
    BLM = e(nc.sbuf_tensor("BLM", [128, 128], F32))
    R1B = e(nc.sbuf_tensor("R1B", [128, 512], BF16))
    SQT = e(nc.sbuf_tensor("SQT", [128, 512], BF16))
    QS1 = e(nc.sbuf_tensor("QS1", [128, 512], F32))
    QS2 = e(nc.sbuf_tensor("QS2", [128, 512], F32))
    ATI8 = e(nc.sbuf_tensor("ATI8", [128, 512], I8))
    AQB = e(nc.sbuf_tensor("AQB", [128, 2 * 512], BF16))
    M2A = e(nc.sbuf_tensor("M2A", [32, 16], F32))
    bufs = (WOO, EXPB, ATT, X1C, MCOL, MROW, XRES, BCB, BL1, BLM, R1B, SQT,
            QS1, QS2, ATI8, AQB, M2A)

    pl.dma("gpsimd", WOO[:].rearrange("p (k c) -> p k c", k=8),
           C.wo.rearrange("(k p) c -> p k c", p=128), "ldwo")
    C.WOL = pl.cnt["ldwo"]
    if not C.debug and STAGE >= 5:
        _emit_allgather(C)
    pl.wait("vector", "ldmsk", C.MSKLOAD)
    pl.wait("sync", "outw")

    pl.t.matmul(C.pb[2][:, 0:128], C.onesrow_f[:], C.s_blend[:],
                start=True, stop=True)
    tv = pl.tmark()
    pl.wait("vector", "tms", tv)
    pl.v.tensor_copy(BLM[:], C.pb[2][:, 0:128])
    pl.v.tensor_scalar(BL1[:], C.pb[2][:, 0:128], -1.0, 1.0,
                       ALU.mult, ALU.add, nochain=True)

    g0 = _gen_attn_chunk(C, 0, bufs)
    g1 = _gen_attn_chunk(C, 1, bufs)
    next(g0)
    next(g0)
    if ATTN_IL:
        next(g1)
        for _ in g0:
            pass
        for _ in g1:
            pass
    else:
        for _ in g0:
            pass
        for _ in g1:
            pass
    if C.debug:
        pl.wait("gpsimd", "r2w")
        pl.dma("gpsimd", C.dbg["d_x1"][:, 0:16 * CHT]
               .rearrange("p (k c) -> p k c", k=16),
               C.x1d[:, 0:CHT].rearrange("(k p) c -> p k c", p=128), "outw")
        pl.dma("gpsimd", C.dbg["d_x1"][:, 16 * CHT:]
               .rearrange("p (k c) -> p k c", k=16),
               C.x1d[:, CHT:].rearrange("(k p) c -> p k c", p=128), "outw")


def _gen_attn_chunk(C, ck, bufs):
    (WOO, EXPB, ATT, X1C, MCOL, MROW, XRES, BCB, BL1, BLM, R1B, SQT,
     QS1, QS2, ATI8, AQB, M2A) = bufs
    RECB = BCB[:, 0:512]
    RSB = BCB[:, 0:512]
    NBB = BCB[:, 512:1024]
    nc, pl = C.nc, C.pl
    LOCS = SLOT_LOC[ck * 4:(ck + 1) * 4]
    pl.wait("tensor", "vch", pl.vmark())
    if C.debug:
        pl.wait("vector", "outw")

    red_done = {}
    for s4 in range(4):
        loc = LOCS[s4]
        for h in range(16):
            g = h // 4
            par = h % 2
            lb = (C.pb[0], C.pb[1]) if par == 0 else (C.pb[2], C.pb[3])
            poolb = C.pb[4] if par == 0 else C.pb[5]
            qcol = h * TOWN + ck * 512 + s4 * 128
            if par in red_done:
                pl.wait("tensor", "vch", red_done[par])
            base = g * 1152
            pl.t.matmul(poolb[:, 0:16], C.QCT[0:64, qcol:qcol + 128],
                        C.KCT[0:64, base:base + 16], start=True, stop=True)
            for kt in range(loc):
                pl.t.matmul(lb[kt // 4][:, (kt % 4) * 128:(kt % 4 + 1) * 128],
                            C.QCT[0:64, qcol:qcol + 128],
                            C.KCT[0:64, base + 16 + kt * 128:
                                  base + 16 + (kt + 1) * 128],
                            start=True, stop=True)
            tv = pl.tmark()
            pl.wait("vector", "tms", tv)
            ps = poolb[:, 0:16]
            if ck == 0:
                pl.v.tensor_tensor(ps, ps,
                                   C.s_smaskp[:, s4 * 16:(s4 + 1) * 16],
                                   ALU.add, nochain=True)
            pl.v.reduce_max(C.rcol[:, 0:1], ps, AX.X)
            ncand = 1
            if loc > 0:
                for w in range(2):
                    kt = loc - 2 + w
                    sl = lb[kt // 4][:, (kt % 4) * 128:(kt % 4 + 1) * 128]
                    pl.v.tensor_tensor(sl, sl,
                                       C.s_smaskl[:, (s4 * 2 + w) * 128:
                                                  (s4 * 2 + w + 1) * 128],
                                       ALU.add, nochain=True)
                n0 = min(loc, 4)
                pl.v.reduce_max(C.rcol[:, 1:2], lb[0][:, 0:n0 * 128], AX.X)
                ncand = 2
                if loc > 4:
                    pl.v.reduce_max(C.rcol[:, 2:3],
                                    lb[1][:, 0:(loc - 4) * 128], AX.X)
                    ncand = 3
            pl.v.reduce_max(MCOL[:, s4 * 16 + h:s4 * 16 + h + 1],
                            C.rcol[:, 0:ncand], AX.X)
            red_done[par] = pl.vmark()

    pl.wait("tensor", "vch", max(red_done.values()))
    pl.t.transpose(C.pb[5][0:64, 0:128], MCOL[:], C.sidentf[:])
    tv = pl.tmark()
    pl.wait("vector", "tms", tv)
    pl.v.tensor_scalar(MROW[:], C.pb[5][0:64, 0:128], -1.0, None, ALU.mult)
    mr_v = pl.vmark()
    pl.wait("sync", "vch", mr_v)
    pl.dma("sync", C.mbounce, MROW[:], "boun")
    pl.wait("sync", "boun", pl.cnt["boun"])
    for s4 in range(4):
        dst = C.QCT[64 + s4:65 + s4, :].rearrange("p (h q) -> p h q", h=16)[
            :, :, ck * 512 + s4 * 128:ck * 512 + (s4 + 1) * 128]
        src = C.mbounce[:].rearrange("(s h) q -> s h q", s=4)[s4:s4 + 1]
        pl.dma("sync", dst, src, "boun")
    MB_DONE = pl.cnt["boun"]

    yield

    pl.wait("tensor", "boun", MB_DONE)
    pl.wait("tensor", "ldwo", C.WOL)
    if ck == 1:
        pl.wait("tensor", "vch", C.qs_v[0])
    pv_evac = {}
    st_free = {3: None, 4: None}
    prev_pv_tv = None
    pv_tvs = []
    for g in range(4):
        kbase = g * 1152
        for h4 in range(4):
            h = g * 4 + h4
            qcol = h * TOWN + ck * 512
            pvb = C.pb[0] if h % 2 == 0 else C.pb[1]
            eb = (h % 2) * 9 * 512
            if len(pv_tvs) >= 2:
                pl.wait("scalar", "tms", pv_tvs[-2])
            kts = [("p", 0)] + [("l", kt) for kt in range(LOCS[3])]
            for idx, (kind, kt) in enumerate(kts):
                stb = 3 + idx % 2
                if st_free[stb] is not None:
                    pl.wait("tensor", "sch", st_free[stb])
                if kind == "p":
                    pl.t.matmul(C.pb[stb][0:16, 0:512],
                                C.KCT[0:68, kbase:kbase + 16],
                                C.QCT[0:68, qcol:qcol + 512],
                                start=True, stop=True)
                else:
                    cs = 128 * sum(1 for l in LOCS if l <= kt)
                    pl.t.matmul(C.pb[stb][:, cs:512],
                                C.KCT[0:68, kbase + 16 + kt * 128:
                                      kbase + 16 + (kt + 1) * 128],
                                C.QCT[0:68, qcol + cs:qcol + 512],
                                start=True, stop=True)
                tv = pl.tmark()
                need_mask = (kind == "p" and ck == 0) or \
                            (kind == "l" and any(l - 2 <= kt < l for l in LOCS))
                if need_mask:
                    pl.wait("vector", "tms", tv)
                    if kind == "p":
                        for s4 in range(4):
                            pl.v.tensor_tensor(
                                C.pb[stb][0:16, s4 * 128:(s4 + 1) * 128],
                                C.pb[stb][0:16, s4 * 128:(s4 + 1) * 128],
                                C.s_tmaskp[:, s4 * 128:(s4 + 1) * 128],
                                ALU.add, nochain=(s4 > 0))
                    else:
                        firstm = True
                        for s4, l in enumerate(LOCS):
                            if l - 2 <= kt < l:
                                w = kt - (l - 2)
                                pl.v.tensor_tensor(
                                    C.pb[stb][:, s4 * 128:(s4 + 1) * 128],
                                    C.pb[stb][:, s4 * 128:(s4 + 1) * 128],
                                    C.s_tmaskl[:, (s4 * 2 + w) * 128:
                                               (s4 * 2 + w + 1) * 128],
                                    ALU.add, nochain=not firstm)
                                firstm = False
                    pl.wait("scalar", "vch", pl.vmark())
                else:
                    pl.wait("scalar", "tms", tv)
                if kind == "p":
                    pl.s.activation(EXPB[0:16, eb:eb + 512],
                                    C.pb[stb][0:16, 0:512], ACT.Exp)
                else:
                    cs = 128 * sum(1 for l in LOCS if l <= kt)
                    pl.s.activation(
                        EXPB[:, eb + (1 + kt) * 512 + cs:
                             eb + (1 + kt) * 512 + 512],
                        C.pb[stb][:, cs:512], ACT.Exp)
                st_free[stb] = pl.smark()
            pl.wait("tensor", "sch", max(x for x in st_free.values()
                                         if x is not None))
            if (h % 2) in pv_evac:
                pl.wait("tensor", "vch", pv_evac[h % 2])
            pl.t.matmul(pvb[0:65, 0:512], C.V2T[0:16, g * 65:g * 65 + 65],
                        EXPB[0:16, eb:eb + 512], start=True,
                        stop=(LOCS[3] == 0))
            for kt in range(LOCS[3]):
                cs = 128 * sum(1 for l in LOCS if l <= kt)
                pl.t.matmul(pvb[0:65, cs:512],
                            C.V2T[:, (1 + kt) * 260 + g * 65:
                                  (1 + kt) * 260 + (g + 1) * 65],
                            EXPB[:, eb + (1 + kt) * 512 + cs:
                                 eb + (1 + kt) * 512 + 512],
                            start=False, stop=(kt == LOCS[3] - 1))
            prev_pv_tv = pl.tmark()
            pv_tvs.append(prev_pv_tv)
            pl.wait("scalar", "tms", prev_pv_tv)
            pl.s.activation(C.TROWS[0:1, 1536:2048], pvb[64:65, 0:512],
                            ACT.Copy)
            dcp_s = pl.smark()
            pl.wait("tensor", "sch", dcp_s)
            pl.t.matmul(C.pb[2][:, 0:512], C.onesrow_f[:],
                        C.TROWS[0:1, 1536:2048], start=True, stop=True)
            tv2 = pl.tmark()
            pl.wait("scalar", "tms", tv2)
            pl.s.activation(RECB, C.pb[2][:, 0:512], ACT.Ln)
            pl.s.activation(RECB, RECB, ACT.Exp, scale=-1.0)
            rec_s = pl.smark()
            pl.wait("vector", "sch", rec_s)
            att = ATT[(h % 2) * 64:(h % 2) * 64 + 64,
                      (h // 2) * 512:(h // 2) * 512 + 512]
            pl.v.tensor_tensor(att, pvb[0:64, 0:512], BCB[0:64, 0:512],
                               ALU.mult)
            if ck == 0:
                pbase = (h % 2) * 64
                a0 = ATT[pbase:pbase + 64,
                         (h // 2) * 512:(h // 2) * 512 + 128]
                pl.v.tensor_tensor(a0, a0, BL1[pbase:pbase + 64, :], ALU.mult)
                pl.v.scalar_tensor_tensor(a0, BLM[pbase:pbase + 64, :],
                                          C.MEANV[pbase:pbase + 64, g:g + 1],
                                          a0, ALU.mult, ALU.add)
            pv_evac[h % 2] = pl.vmark()

    pl.wait("tensor", "vch", max(pv_evac.values()))
    pl.wait("tensor", "sch", pl.smark())
    if ck == 1:
        pl.wait("vector", "r2w", C.X1D_DONE)
        pl.wait("sync", "vch", C.xres_free[0])
    wo_free = {3: None, 4: None}
    r1_vs = []
    last_stat_tv = None
    for mt in range(16):
        wob = 3 + mt % 2
        xsem = "xr%d" % (mt % 2)
        if mt >= 2:
            pl.wait("sync", "vch", r1_vs[mt - 2])
        pl.dma("sync", XRES[:, (mt % 2) * 512:(mt % 2 + 1) * 512],
               C.xq[mt * 128:(mt + 1) * 128, ck * 512:(ck + 1) * 512], xsem)
        if wo_free[wob] is not None:
            pl.wait("tensor", "vch", wo_free[wob])
        for kt in range(8):
            pl.t.matmul(C.pb[wob][:, 0:512],
                        WOO[:, kt * 2048 + mt * 128:kt * 2048 + (mt + 1) * 128],
                        ATT[:, kt * 512:(kt + 1) * 512],
                        start=(kt == 0), stop=(kt == 7))
        tv = pl.tmark()
        pl.wait("vector", "tms", tv)
        pl.wait("vector", xsem, pl.cnt[xsem])
        if last_stat_tv is not None:
            pl.wait("vector", "tms", last_stat_tv)
        x1t = X1C[:, mt * 512:(mt + 1) * 512]
        pl.v.tensor_tensor(x1t, C.pb[wob][:, 0:512],
                           XRES[:, (mt % 2) * 512:(mt % 2 + 1) * 512],
                           ALU.add, nochain=True)
        pl.v.tensor_copy(R1B[:], x1t)
        pl.v.tensor_tensor(SQT[:], R1B[:], R1B[:], ALU.mult)
        wo_free[wob] = pl.vmark()
        r1_vs.append(wo_free[wob])
        pl.wait("tensor", "vch", wo_free[wob])
        pl.t.matmul(C.pst[0:1, 0:512], C.onescol_b[:], R1B[:],
                    start=(mt == 0), stop=(mt == 15))
        pl.t.matmul(C.pb[5][0:1, 0:512], C.onescol_b[:], SQT[:],
                    start=(mt == 0), stop=(mt == 15))
        last_stat_tv = pl.tmark()

    pl.wait("vector", "tms", last_stat_tv)
    mu = C.TROWS[0:1, 0:512]
    rs = C.TROWS[0:1, 512:1024]
    nb = C.TROWS[0:1, 1024:1536]
    tm = C.TROWS[0:1, 1536:2048]
    pl.v.tensor_scalar(mu, C.pst[0:1, 0:512], 1.0 / 2048.0, None, ALU.mult)
    pl.v.tensor_scalar(tm, C.pb[5][0:1, 0:512], 1.0 / 2048.0, None, ALU.mult,
                       nochain=True)
    pl.v.tensor_tensor(nb, mu, mu, ALU.mult)
    pl.v.tensor_tensor(tm, tm, nb, ALU.subtract)
    pl.v.tensor_scalar(tm, tm, EPS, None, ALU.add)
    pl.wait("scalar", "vch", pl.vmark())
    pl.s.activation(rs, tm, ACT.Sqrt)
    pl.wait("vector", "sch", pl.smark())
    pl.v.reciprocal(rs, rs)
    pl.v.tensor_tensor(nb, rs, rs, ALU.mult)
    pl.v.tensor_tensor(nb, nb, tm, ALU.mult)
    pl.v.tensor_scalar(nb, nb, -0.5, 1.5, ALU.mult, ALU.add)
    pl.v.tensor_tensor(rs, rs, nb, ALU.mult)
    pl.v.tensor_tensor(nb, mu, rs, ALU.mult)
    pl.v.tensor_scalar(nb, nb, -1.0, None, ALU.mult)
    rs_v = pl.vmark()
    pl.wait("tensor", "vch", rs_v)
    pl.t.matmul(C.pb[2][:, 0:512], C.onesrow_f[:], C.TROWS[0:1, 512:1024],
                start=True, stop=True)
    pl.t.matmul(C.pb[5][:, 0:512], C.onesrow_f[:], C.TROWS[0:1, 1024:1536],
                start=True, stop=True)
    tv = pl.tmark()
    pl.wait("vector", "tms", tv)
    pl.v.tensor_copy(RSB, C.pb[2][:, 0:512])
    pl.v.tensor_copy(NBB, C.pb[5][:, 0:512], nochain=True)
    for mt in range(16):
        x1t = X1C[:, mt * 512:(mt + 1) * 512]
        pl.v.tensor_tensor(x1t, x1t, RSB, ALU.mult, nochain=(mt > 0))
        pl.v.tensor_tensor(x1t, x1t, NBB, ALU.add)
    C.X1_V = pl.vmark()
    pl.wait("gpsimd", "vch", C.X1_V)
    pl.dma("gpsimd", C.x1d[:, ck * CHT:(ck + 1) * CHT]
           .rearrange("(k p) c -> p k c", p=128),
           X1C[:].rearrange("p (k c) -> p k c", k=16), "r2w")
    C.X1D_DONE = pl.cnt["r2w"]
    if not hasattr(C, "x1d_done"):
        C.x1d_done = {}
    C.x1d_done[ck] = C.X1D_DONE

    yield

    aslots = (XRES[:, 0:512], QS1[:])
    axm = XRES[:, 512:1024]
    pl.wait("scalar", "xr0", pl.cnt["xr0"])
    pl.wait("scalar", "xr1", pl.cnt["xr1"])
    pl.wait("scalar", "vch", C.X1_V)
    amx_vs = []
    for mt in range(16):
        x1t = X1C[:, mt * 512:(mt + 1) * 512]
        aslot = aslots[mt % 2]
        if mt >= 2:
            pl.wait("scalar", "vch", amx_vs[mt - 2])
        pl.s.activation(aslot, x1t, ACT.Abs)
        pl.wait("vector", "sch", pl.smark())
        if mt == 0:
            pl.v.tensor_copy(axm, aslot)
        else:
            pl.v.tensor_tensor(axm, axm, aslot, ALU.max)
        amx_vs.append(pl.vmark())
    mu = C.TROWS[0:1, 0:512]
    tm = C.TROWS[0:1, 1536:2048]
    _amax_row(C, aslots[0], M2A, axm, tm)
    C.xres_free[ck] = pl.vmark()
    pl.v.tensor_scalar(tm, tm, EPS, None, ALU.max)
    pl.v.reciprocal(mu, tm)
    pl.v.tensor_scalar(mu, mu, 128.0, None, ALU.mult)
    pl.v.tensor_scalar(tm, tm, 1.0 / 128.0, None, ALU.mult)
    qsc_v = pl.vmark()
    pl.wait("tensor", "vch", qsc_v)
    pl.t.matmul(C.pb[2][:, 0:512], C.onesrow_f[:], mu, start=True, stop=True)
    pl.t.matmul(C.pb[5][:, 0:512], C.onesrow_f[:], tm, start=True, stop=True)
    tv = pl.tmark()
    pl.wait("vector", "tms", tv)
    pl.v.tensor_copy(QS1[:], C.pb[2][:, 0:512])
    pl.v.tensor_copy(QS2[:], C.pb[5][:, 0:512], nochain=True)
    C.qs_v[ck] = pl.vmark()
    xqw_mks = []
    for mt in range(16):
        x1t = X1C[:, mt * 512:(mt + 1) * 512]
        sl = AQB[:, (mt % 2) * 512:(mt % 2 + 1) * 512]
        if mt >= 2:
            pl.wait("vector", "xqw", xqw_mks[mt - 2])
        pl.v.tensor_tensor(ATI8[:], x1t, QS1[:], ALU.mult)
        pl.v.tensor_tensor(sl, ATI8[:], QS2[:], ALU.mult)
        mk = pl.vmark()
        pl.wait("gpsimd", "vch", mk)
        pl.dma("gpsimd", C.xqd[ck * D + mt * 128:ck * D + (mt + 1) * 128, :],
               sl, "xqw")
        xqw_mks.append(pl.cnt["xqw"])
    C.aq_v[ck] = pl.vmark()
    C.xqw_done[ck] = pl.cnt["xqw"]

    if C.debug:
        pl.wait("gpsimd", "vch", C.X1_V)
        pl.dma("gpsimd", C.dbg["d_att"][:, ck * 8 * 512:(ck + 1) * 8 * 512],
               ATT[:], "outw")


def _emit_ffn_all(C, e):
    nc, pl = C.nc, C.pl
    XQ = e(nc.sbuf_tensor("XQ", [128, 16 * 512], BF16))
    Y = e(nc.sbuf_tensor("Y", [128, M_I * 512], BF16))
    X1T = e(nc.sbuf_tensor("X1T", [128, 2 * 512], F32))
    WG = e(nc.sbuf_tensor("WG", [128, 2 * 2048], BF16))
    WU = e(nc.sbuf_tensor("WU", [128, 2 * 2048], BF16))
    WD = e(nc.sbuf_tensor("WD", [128, 4 * 4096], BF16))
    T4 = e(nc.sbuf_tensor("T4", [128, 4 * 512], F32))
    T4P = e(nc.sbuf_tensor("T4P", [128, 4 * 512], F32))
    TI8P = e(nc.sbuf_tensor("TI8P", [128, 512], I8))
    SIG = e(nc.sbuf_tensor("SIG", [128, 2 * 512], BF16))
    AM = e(nc.sbuf_tensor("AM", [128, 512], F32))
    AMT = e(nc.sbuf_tensor("AMT", [128, 512], F32))
    M2 = e(nc.sbuf_tensor("M2", [32, 16], F32))
    SCB = e(nc.sbuf_tensor("SCB", [128, 512], F32))
    BCB2 = e(nc.sbuf_tensor("BCB2", [128, 2 * 512], F32))
    TQ = e(nc.sbuf_tensor("TQ", [128, 512], F32))
    TI8 = e(nc.sbuf_tensor("TI8", [128, 512], I8))
    RT = e(nc.sbuf_tensor("RT", [128, 2 * 512], F32))
    OUTT = e(nc.sbuf_tensor("OUTT", [128, 2 * 512], F32))
    R2B2 = e(nc.sbuf_tensor("R2B2", [128, 512], BF16))
    SQ2 = e(nc.sbuf_tensor("SQ2", [128, 512], BF16))
    bufs = SimpleNamespace(**{k: v for k, v in locals().items()
                              if k not in ("C", "nc", "pl", "e")})
    pl.wait("sync", "agc", C.AGDONE)
    for ck in range(2):
        _emit_ffn_chunk(C, ck, bufs)


def _amax_row(C, amt, m2, src, dstrow):
    pl = C.pl
    pl.wait("vector", "pch", pl.pmark())
    for width in (64, 32):
        pl.v.tensor_copy(amt[0:width, :], src[width:2 * width, :])
        pl.v.tensor_tensor(src[0:width, :], src[0:width, :],
                           amt[0:width, :], ALU.max)
    pl.v.transpose(amt[0:32, :], src[0:32, :])
    pl.v.reduce_max(m2[:],
                    amt[0:32, :].rearrange("p (j c) -> p j c", c=32), AX.X)
    mv = pl.vmark()
    pl.wait("sync", "vch", mv)
    pl.dma("sync", C.qbounce, m2[:], "qbn")
    pl.wait("sync", "qbn", pl.cnt["qbn"])
    pl.dma("sync", dstrow.rearrange("a (j r) -> a j r", r=32),
           C.qbounce.transpose([1, 0]).unsqueeze(0), "qbn",
           allow_slow_non_contiguous=True)
    pl.wait("vector", "qbn", pl.cnt["qbn"])


def _emit_ffn_chunk(C, ck, b):
    nc, pl = C.nc, C.pl
    rows = C.rows
    if ck == 0:
        for eng in ENGS:
            for snm in ("r2w", "xr0", "xr1", "boun", "ldwo",
                        "ldm", "ldxq", "ldwA", "outw"):
                pl.wait(eng, snm, pl.cnt[snm])
    if ck == 1:
        pl.wait("sync", "tms", pl.cnt["tms"])
        pl.wait("sync", "r2r0", pl.cnt["r2r0"])
        pl.wait("sync", "r2r1", pl.cnt["r2r1"])
        pl.wait("gpsimd", "r2r0", pl.cnt["r2r0"])
        pl.wait("gpsimd", "r2r1", pl.cnt["r2r1"])
        pl.wait("vector", "outw", pl.cnt["outw"])
        pl.wait("vector", "r2r0", pl.cnt["r2r0"])
        pl.wait("vector", "r2r1", pl.cnt["r2r1"])
        pl.wait("vector", "pch", pl.cnt["pch"])
        pl.wait("scalar", "pch", pl.cnt["pch"])
    mu = C.TROWS[0:1, 0:512]
    rs = C.TROWS[0:1, 512:1024]
    nb = C.TROWS[0:1, 1024:1536]
    tm = C.TROWS[0:1, 1536:2048]

    def gate_slab(m):
        r, ml = m // 8, m % 8
        return C.scr_all[r, ml * GSL:(ml + 1) * GSL].rearrange(
            "(p n) -> p n", p=128)

    def up_slab(m):
        r, ml = m // 8, m % 8
        return C.scr_all[r, OWN_G + ml * GSL:OWN_G + (ml + 1) * GSL].rearrange(
            "(p n) -> p n", p=128)

    def down_q(otp, q4):
        outs = []
        for rr in range(2):
            r = 2 * q4 + rr
            base_o = 2 * OWN_G + otp * (128 * 2048)
            reg = C.scr_all[r, base_o:base_o + 128 * 2048].rearrange(
                "(p n) -> p n", p=128)
            outs.append(reg)
        return outs

    x1cols = slice(ck * CHT, (ck + 1) * CHT)
    pl.wait("sync", "r2w", C.x1d_done[ck])
    pl.wait("sync", "xqw", C.xqw_done[ck])
    XQD = b.XQ
    for mt in range(16):
        pl.dma("sync", XQD[:, mt * 512:(mt + 1) * 512],
               C.xqd[ck * D + mt * 128:ck * D + (mt + 1) * 128, :], "ldxq")
    xq_ld = pl.cnt["ldxq"]

    wg_used = {}
    wu_used = {}
    y_v = {}
    sig_s = {}
    for m in range(M_I):
        par = m % 2
        for (wsem, wbuf, slab, used) in (("wg%d" % par, b.WG, gate_slab(m),
                                          wg_used),
                                         ("wu%d" % par, b.WU, up_slab(m),
                                          wu_used)):
            if par in used:
                pl.wait("sync", "tms", used[par])
            pl.dma("sync", wbuf[:, par * 2048:(par + 1) * 2048], slab, wsem)
        gb, ub = C.pb[0 + par], C.pb[2 + par]
        pl.wait("tensor", "wg%d" % par, pl.cnt["wg%d" % par])
        pl.wait("tensor", "wu%d" % par, pl.cnt["wu%d" % par])
        pl.wait("tensor", "ldxq", xq_ld)
        if par in y_v:
            pl.wait("tensor", "vch", y_v[par])
        for kt in range(16):
            pl.t.matmul(gb[:, 0:512],
                        b.WG[:, par * 2048 + kt * 128:par * 2048 + (kt + 1) * 128],
                        XQD[:, kt * 512:(kt + 1) * 512],
                        start=(kt == 0), stop=(kt == 15))
        gate_tv = pl.tmark()
        for kt in range(16):
            pl.t.matmul(ub[:, 0:512],
                        b.WU[:, par * 2048 + kt * 128:par * 2048 + (kt + 1) * 128],
                        XQD[:, kt * 512:(kt + 1) * 512],
                        start=(kt == 0), stop=(kt == 15))
        up_tv = pl.tmark()
        wg_used[par] = up_tv
        wu_used[par] = up_tv
        pl.wait("scalar", "tms", gate_tv)
        if par in sig_s:
            pl.wait("scalar", "vch", y_v[par])
        pl.s.activation(b.SIG[:, par * 512:(par + 1) * 512], gb[:, 0:512],
                        ACT.Silu, scale=C.WMC[:, 0:1])
        sig_s[par] = pl.smark()
        pl.wait("vector", "sch", sig_s[par])
        pl.wait("vector", "tms", up_tv)
        pl.v.scalar_tensor_tensor(b.Y[:, m * 512:(m + 1) * 512],
                                  ub[:, 0:512], C.WMC[:, 1:2],
                                  b.SIG[:, par * 512:(par + 1) * 512],
                                  ALU.mult, ALU.mult, nochain=True)
        y_v[par] = pl.vmark()

    yv = max(y_v.values())
    pl.wait("vector", "vch", yv)
    pl.wait("gpsimd", "vch", yv)

    def _quad(eng, m0, s4, t4):
        blk = [b.Y[:, (m0 + kk * s4) * 512:(m0 + kk * s4) * 512 + 512]
               for kk in range(4)]
        t1, t2 = t4[:, 0:512], t4[:, 512:1024]
        t3, tt = t4[:, 1024:1536], t4[:, 1536:2048]
        eng.tensor_tensor(t1, blk[0], blk[1], ALU.add)
        eng.tensor_tensor(t2, blk[0], blk[1], ALU.subtract, nochain=True)
        eng.tensor_tensor(t3, blk[2], blk[3], ALU.add, nochain=True)
        eng.tensor_tensor(tt, blk[2], blk[3], ALU.subtract, nochain=True)
        eng.tensor_tensor(blk[0], t1, t3, ALU.add)
        eng.tensor_tensor(blk[1], t2, tt, ALU.add, nochain=True)
        eng.tensor_tensor(blk[2], t1, t3, ALU.subtract, nochain=True)
        eng.tensor_tensor(blk[3], t2, tt, ALU.subtract, nochain=True)

    for s4 in (1, 4):
        for j in range(16):
            m0 = (j // s4) * 4 * s4 + (j % s4)
            if m0 < 48 or not POOL_BF:
                _quad(pl.v, m0, s4, b.T4)
            else:
                _quad(pl.p, m0, s4, b.T4P)
    pl.wait("vector", "pch", pl.pmark())
    pl.wait("gpsimd", "vch", pl.vmark())
    for j in range(16):
        if j < 11 or not POOL_BF:
            _quad(pl.v, j, 16, b.T4)
        else:
            _quad(pl.p, j, 16, b.T4P)
    bf_v = pl.vmark()
    bf_p = pl.pmark()

    pl.wait("tensor", "vch", bf_v)
    pl.wait("tensor", "pch", bf_p)
    pl.wait("scalar", "vch", bf_v)
    pl.wait("scalar", "pch", bf_p)
    abs_done = {}
    mx_vs = []
    for m in range(M_I):
        par = m % 2
        hb = C.pb[4 + par]
        if par in abs_done:
            pl.wait("tensor", "sch", abs_done[par])
        pl.t.matmul(hb[:, 0:512], C.sh128[:], b.Y[:, m * 512:(m + 1) * 512],
                    start=True, stop=True)
        tv = pl.tmark()
        pl.wait("scalar", "tms", tv)
        pl.s.activation(b.Y[:, m * 512:(m + 1) * 512], hb[:, 0:512], ACT.Copy)
        if m >= 2:
            pl.wait("scalar", "vch", mx_vs[m - 2])
        dst = b.AMT if par == 0 else b.TQ
        pl.s.activation(dst[:], hb[:, 0:512], ACT.Abs)
        abs_done[par] = pl.smark()
        pl.wait("vector", "sch", abs_done[par])
        if m == 0:
            pl.v.tensor_copy(b.AM[:], dst[:])
        else:
            pl.v.tensor_tensor(b.AM[:], b.AM[:], dst[:], ALU.max)
        mx_vs.append(pl.vmark())
    _amax_row(C, b.AMT, b.M2, b.AM, tm)
    pl.v.tensor_scalar(tm, tm, C8192, EPS, ALU.mult, ALU.max)
    pl.v.reciprocal(mu, tm)
    pl.v.tensor_scalar(mu, mu, 128.0, None, ALU.mult)
    pl.v.tensor_scalar(mu, mu, C8192, None, ALU.mult)
    pl.v.tensor_scalar(tm, tm, 1.0 / 128.0, None, ALU.mult)
    pl.v.tensor_single_scalar(tm, tm, rows[0:1, R_WS + 16:R_WS + 17],
                              ALU.mult)
    sc2_v = pl.vmark()
    pl.wait("tensor", "vch", sc2_v)
    pl.t.matmul(C.pb[4][:, 0:512], C.onesrow_f[:], mu, start=True, stop=True)
    pl.t.matmul(C.pb[5][:, 0:512], C.onesrow_f[:], tm, start=True, stop=True)
    tv = pl.tmark()
    pl.wait("vector", "tms", tv)
    pl.v.tensor_copy(b.SCB[:], C.pb[4][:, 0:512])
    pl.v.tensor_copy(b.BCB2[:, 512:1024], C.pb[5][:, 0:512], nochain=True)
    sc2b_v = pl.vmark()

    nv = 56 if POOL_XI else M_I
    if POOL_XI:
        pl.wait("gpsimd", "vch", sc2b_v)
        for m in range(56, M_I):
            pl.p.tensor_tensor(b.T4P[:, 0:512], b.Y[:, m * 512:(m + 1) * 512],
                               b.SCB[:], ALU.mult)
            pl.p.tensor_copy(b.TI8P[:], b.T4P[:, 0:512])
            pl.p.tensor_copy(b.Y[:, m * 512:(m + 1) * 512], b.TI8P[:])
    xi2_p = pl.pmark()
    for m in range(nv):
        pl.v.tensor_tensor(b.TI8[:], b.Y[:, m * 512:(m + 1) * 512], b.SCB[:],
                           ALU.mult)
        pl.v.tensor_copy(b.Y[:, m * 512:(m + 1) * 512], b.TI8[:])
    xi2_v = pl.vmark()

    pl.wait("tensor", "vch", xi2_v)
    pl.wait("tensor", "pch", xi2_p)
    dn_done = {}
    r2_vs = []
    sq_ss = []
    stat_tv = None
    dn_q_tv = {}
    r2w_vs = []
    for otp in range(8):
        ppar = otp % 2
        pa, pb_ = C.pb[0 + 2 * ppar], C.pb[1 + 2 * ppar]
        for q4 in range(4):
            wsem = "wd%d" % q4
            prevq = (otp - 1, q4)
            if prevq in dn_q_tv:
                pl.wait("sync", "tms", dn_q_tv[prevq])
            srcs = down_q(otp, q4)
            for rr in range(2):
                pl.dma("sync", b.WD[:, q4 * 4096 + rr * 2048:
                                    q4 * 4096 + (rr + 1) * 2048],
                       srcs[rr], wsem)
            pl.wait("tensor", wsem, pl.cnt[wsem])
            if otp >= 2 and q4 == 0 and ppar in dn_done:
                pl.wait("tensor", "vch", dn_done[ppar])
            for j in range(16):
                kt = q4 * 16 + j
                base = q4 * 4096 + j * 256
                pl.t.matmul(pa[:, 0:512], b.WD[:, base:base + 128],
                            b.Y[:, kt * 512:(kt + 1) * 512],
                            start=(kt == 0), stop=(kt == 63))
                pl.t.matmul(pb_[:, 0:512], b.WD[:, base + 128:base + 256],
                            b.Y[:, kt * 512:(kt + 1) * 512],
                            start=(kt == 0), stop=(kt == 63))
            dn_q_tv[(otp, q4)] = pl.tmark()
        dn_tv = dn_q_tv[(otp, 3)]
        for oo in range(2):
            ot = 2 * otp + oo
            osl = ot % 2
            xs = "x1r%d" % osl
            rsl = b.RT[:, osl * 512:(osl + 1) * 512]
            if len(r2_vs) >= 2:
                pl.wait("sync", "vch", r2_vs[-2])
            pl.dma("sync", b.X1T[:, osl * 512:(osl + 1) * 512],
                   C.x1d[ot * 128:(ot + 1) * 128, x1cols], xs)
            pl.wait("vector", "tms", dn_tv)
            pl.wait("vector", xs, pl.cnt[xs])
            if ot >= 2:
                pl.wait("vector", "r2w", r2w_vs[ot - 2])
                pl.wait("vector", "sch", sq_ss[ot - 2])
            psrc = pa if oo == 0 else pb_
            pl.v.tensor_tensor(rsl, psrc[:, 0:512],
                               b.BCB2[:, 512:1024], ALU.mult, nochain=True)
            pl.v.tensor_tensor(rsl, rsl,
                               b.X1T[:, osl * 512:(osl + 1) * 512], ALU.add)
            r2_vs.append(pl.vmark())
            pl.wait("gpsimd", "vch", r2_vs[-1])
            pl.dma("gpsimd", C.r2b[ot * 128:(ot + 1) * 128, :], rsl, "r2w")
            r2w_vs.append(pl.cnt["r2w"])
            pl.wait("scalar", "vch", r2_vs[-1])
            if stat_tv is not None:
                pl.wait("scalar", "tms", stat_tv)
            pl.s.activation(b.R2B2[:], rsl, ACT.Copy)
            pl.s.activation(b.SQ2[:], b.R2B2[:], ACT.Square)
            sq_ss.append(pl.smark())
            pl.wait("tensor", "sch", sq_ss[-1])
            pl.t.matmul(C.pst[0:1, 0:512], C.onescol_b[:], b.R2B2[:],
                        start=(ot == 0), stop=(ot == 15))
            pl.t.matmul(C.pb[5][0:1, 0:512], C.onescol_b[:], b.SQ2[:],
                        start=(ot == 0), stop=(ot == 15))
            stat_tv = pl.tmark()
        dn_done[ppar] = r2_vs[-1]

    pl.wait("vector", "tms", stat_tv)
    pl.v.tensor_scalar(mu, C.pst[0:1, 0:512], 1.0 / 2048.0, None, ALU.mult)
    pl.v.tensor_scalar(tm, C.pb[5][0:1, 0:512], 1.0 / 2048.0, None, ALU.mult,
                       nochain=True)
    pl.v.tensor_tensor(nb, mu, mu, ALU.mult)
    pl.v.tensor_tensor(tm, tm, nb, ALU.subtract)
    pl.v.tensor_scalar(tm, tm, EPS, None, ALU.add)
    pl.wait("scalar", "vch", pl.vmark())
    pl.s.activation(rs, tm, ACT.Sqrt)
    pl.wait("vector", "sch", pl.smark())
    pl.v.reciprocal(rs, rs)
    pl.v.tensor_tensor(nb, rs, rs, ALU.mult)
    pl.v.tensor_tensor(nb, nb, tm, ALU.mult)
    pl.v.tensor_scalar(nb, nb, -0.5, 1.5, ALU.mult, ALU.add)
    pl.v.tensor_tensor(rs, rs, nb, ALU.mult)
    pl.v.tensor_tensor(nb, mu, rs, ALU.mult)
    pl.v.tensor_scalar(nb, nb, -1.0, None, ALU.mult)
    rs2_v = pl.vmark()
    pl.wait("tensor", "vch", rs2_v)
    pl.t.matmul(C.pb[4][:, 0:512], C.onesrow_f[:], rs, start=True, stop=True)
    pl.t.matmul(C.pb[5][:, 0:512], C.onesrow_f[:], nb, start=True, stop=True)
    tv = pl.tmark()
    pl.wait("vector", "tms", tv)
    pl.v.tensor_copy(b.SCB[:], C.pb[4][:, 0:512])
    pl.v.tensor_copy(b.BCB2[:, 0:512], C.pb[5][:, 0:512], nochain=True)
    ln2_v = pl.vmark()

    pl.wait("sync", "r2w", pl.cnt["r2w"])
    o_vs, ow_vs = [], []
    for mt in range(16):
        rsem = "r2r%d" % (mt % 2)
        if mt >= 2:
            pl.wait("sync", "vch", o_vs[mt - 2])
        pl.dma("sync", b.RT[:, (mt % 2) * 512:(mt % 2 + 1) * 512],
               C.r2b[mt * 128:(mt + 1) * 128, :], rsem)
        pl.wait("vector", rsem, pl.cnt[rsem])
        pl.wait("vector", "vch", ln2_v)
        ot = b.OUTT[:, (mt % 2) * 512:(mt % 2 + 1) * 512]
        if mt >= 2:
            pl.wait("vector", "outw", ow_vs[mt - 2])
        pl.v.tensor_tensor(ot, b.RT[:, (mt % 2) * 512:(mt % 2 + 1) * 512],
                           b.SCB[:], ALU.mult, nochain=True)
        pl.v.tensor_tensor(ot, ot, b.BCB2[:, 0:512], ALU.add)
        o_vs.append(pl.vmark())
        pl.wait("gpsimd", "vch", o_vs[-1])
        pl.dma("gpsimd", C.out[mt * 128:(mt + 1) * 128, x1cols], ot, "outw")
        ow_vs.append(pl.cnt["outw"])
    if C.debug and ck == DBG_FFN_CK:
        pl.wait("gpsimd", "vch", xi2_v)
        pl.wait("gpsimd", "pch", xi2_p)
        pl.dma("gpsimd", C.dbg["d_y"], b.Y[:], "outw")
        pl.dma("gpsimd", C.dbg["d_xq"], b.XQ[:], "outw")


def _stage_inputs(inputs):
    x = np.asarray(inputs["L0_attn_x"], np.float32)
    consts = _host_consts()
    kch, kcl, ksh, ksl = _rope_tables(np.arange(T))
    masks = {p: _masks(p) for p in range(2)}
    qtabs = {p: _rope_tables(_own_positions(p)) for p in range(2)}
    in_maps = []
    for c in range(NCN):
        b, p = c // 2, c % 2
        xT = np.ascontiguousarray(x[b].T)
        xq = np.ascontiguousarray(xT[:, _own_positions(p)])
        smaskp, tmaskp, smaskl, tmaskl, blend = masks[p]
        qch, qcl, qsh, qsl = qtabs[p]
        in_maps.append(dict(
            xkv=xT, xq=xq,
            wqsem=np.ascontiguousarray(inputs["L0_attn_Wq_sem"], np.float32),
            wqgeo=np.ascontiguousarray(inputs["L0_attn_Wq_geo"], np.float32),
            wksem=np.ascontiguousarray(inputs["L0_attn_Wk_sem"], np.float32),
            wkgeo=np.ascontiguousarray(inputs["L0_attn_Wk_geo"], np.float32),
            wv=np.ascontiguousarray(inputs["L0_attn_Wv"], np.float32),
            wo=np.ascontiguousarray(inputs["L0_attn_Wo"], np.float32),
            lsc=np.ascontiguousarray(inputs["L0_attn_logit_scale"],
                                     np.float32).reshape(1, H),
            wgch=np.ascontiguousarray(
                inputs["L0_ffn_w_gate"][c * 1024:(c + 1) * 1024, :],
                np.float32),
            wuch=np.ascontiguousarray(
                inputs["L0_ffn_w_up"][c * 1024:(c + 1) * 1024, :], np.float32),
            wdch=np.ascontiguousarray(
                inputs["L0_ffn_w_down"][:, c * 1024:(c + 1) * 1024],
                np.float32),
            ident=consts["ident"], identf=consts["identf"],
            h128=consts["h128"],
            permrope=consts["permrope"], poolm=consts["poolm"],
            kcos_h=kch, kcos_l=kcl, ksin_h=ksh, ksin_l=ksl,
            qcos_h=qch, qcos_l=qcl, qsin_h=qsh, qsin_l=qsl,
            smaskp=smaskp, tmaskp=tmaskp, smaskl=smaskl, tmaskl=tmaskl,
            blendrow=blend,
        ))
    return in_maps


def kernel(**inputs):
    in_maps = _stage_inputs(inputs)
    nc = build_nc(debug=DEBUG, part=PART)
    res = run_bass_kernel_spmd(nc, in_maps, core_ids=list(range(NCN)))
    out = np.empty((B, T, D), np.float32)
    for c in range(NCN):
        b, p = c // 2, c % 2
        sh = res.results[c]["out"]
        for s in range(NSLOT):
            t = 2 * s + p
            out[b, t * 128:(t + 1) * 128, :] = sh[:, s * 128:(s + 1) * 128].T
    if DEBUG:
        kernel.last_results = res
    return out


# version 15
# speedup vs baseline: 1.0064x; 1.0064x over previous
import numpy as np
import concourse.bass as bass
import concourse.mybir as mybir
from concourse.bass_utils import run_bass_kernel_spmd
from contextlib import ExitStack
from types import SimpleNamespace

F32 = mybir.dt.float32
BF16 = mybir.dt.bfloat16
I8 = mybir.dt.int8
ALU = mybir.AluOpType
ACT = mybir.ActivationFunctionType
AX = mybir.AxisListType

B, T, D = 4, 2048, 2048
H, HKV, DS, DG, DV = 16, 4, 32, 32, 64
MB, LW = 64, 1024
NBLK, NK = 16, 1040
NEG = -1.0e9
NCN = 8
TOWN, CHT, NSLOT = 1024, 512, 8
IFF, M_I, KT_D = 8192, 64, 16
EPS = float(np.float32(1e-5))
C8192 = float(np.float32(1.0 / np.sqrt(8192.0)))
SLOT_LOC = [0, 0, 0, 0, 2, 4, 6, 8]
GSL = 128 * 2048
OWN_G = 8 * GSL
OWN_D = 8 * 128 * 8 * 256
OWN_N = 2 * OWN_G + OWN_D

DEBUG = False
PART = 3
DBG_FFN_CK = 0
STAGE = 99
SUBST = 99
POOL_BF = True
POOL_XI = True
ATTN_IL = False


def _bf16():
    import ml_dtypes
    return ml_dtypes.bfloat16


def _host_consts():
    bf16 = _bf16()
    c = {}
    c["ident"] = np.eye(128, dtype=bf16)
    c["identf"] = np.eye(128, dtype=np.float32)
    i = np.arange(128)
    a = i[:, None] & i[None, :]
    pop = np.zeros((128, 128), np.int64)
    for bb in range(7):
        pop += (a >> bb) & 1
    c["h128"] = np.where(pop % 2 == 0, 1.0, -1.0).astype(bf16)
    pm = np.zeros((128, 128), np.float32)
    for hh in range(4):
        for f in range(16):
            pm[hh * 32 + 16 + f, hh * 32 + f] = 1.0
            pm[hh * 32 + f, hh * 32 + 16 + f] = 1.0
    c["permrope"] = pm.astype(bf16)
    poolm = np.zeros((128, 128), np.float32)
    for tt in range(8):
        poolm[0:64, tt * 16 + 2 * tt] = 1.0 / 64.0
        poolm[64:128, tt * 16 + 2 * tt + 1] = 1.0 / 64.0
    c["poolm"] = poolm.astype(bf16)
    return c


def _rope_tables(pos):
    d = 32
    inv = 1.0 / (10000.0 ** (np.arange(0, d, 2, dtype=np.float64) / d))
    ang = pos.astype(np.float64)[None, :] * inv[:, None]
    cos64, sin64 = np.cos(ang), np.sin(ang)
    cf = np.tile(np.concatenate([cos64, cos64], 0), (4, 1))
    sf = np.tile(np.concatenate([-sin64, sin64], 0), (4, 1))

    def hilo(x):
        hi = x.astype(np.float32)
        return hi, (x - hi.astype(np.float64)).astype(np.float32)
    ch, cl = hilo(cf)
    sh, sl = hilo(sf)
    return ch, cl, sh, sl


def _own_positions(p):
    out = np.empty(TOWN, np.int64)
    for s in range(NSLOT):
        t = 2 * s + p
        out[s * 128:(s + 1) * 128] = np.arange(t * 128, (t + 1) * 128)
    return out


def _masks(p):
    bf16 = _bf16()
    kpp = np.arange(NBLK) * MB + (MB - 1)
    smaskp = np.zeros((128, 4 * 16), np.float32)
    tmaskp = np.zeros((16, 4 * 128), np.float32)
    for s in range(4):
        t = 2 * s + p
        q = np.arange(t * 128, (t + 1) * 128)
        vis = kpp[None, :] <= q[:, None]
        smaskp[:, s * 16:(s + 1) * 16] = np.where(vis, 0.0, NEG)
        tmaskp[:, s * 128:(s + 1) * 128] = np.where(vis.T, 0.0, NEG)
    smaskl = np.zeros((128, 4 * 2 * 128), np.float32)
    tmaskl = np.zeros((128, 4 * 2 * 128), np.float32)
    for js, s in enumerate(range(4, 8)):
        t = 2 * s + p
        q = np.arange(t * 128, (t + 1) * 128)
        for w in range(2):
            kt = SLOT_LOC[s] - 2 + w
            kidx = LW + np.arange(kt * 128, (kt + 1) * 128)
            vis = kidx[None, :] <= q[:, None]
            sl = slice((js * 2 + w) * 128, (js * 2 + w + 1) * 128)
            smaskl[:, sl] = np.where(vis, 0.0, NEG)
            tmaskl[:, sl] = np.where(vis.T, 0.0, NEG)
    blend = np.zeros((1, 128), np.float32)
    if p == 0:
        blend[0, :63] = 1.0
    return (smaskp.astype(bf16), tmaskp.astype(bf16), smaskl.astype(bf16),
            tmaskl.astype(bf16), blend)


ENGS = ("sync", "vector", "scalar", "tensor", "gpsimd")

DATA_OPS = {
    "tensor_tensor", "tensor_add", "tensor_sub", "tensor_mul", "tensor_max",
    "tensor_scalar", "tensor_single_scalar", "tensor_scalar_add",
    "tensor_scalar_sub", "tensor_scalar_mul", "tensor_scalar_max",
    "tensor_scalar_min", "tensor_copy", "tensor_reduce", "reduce_max",
    "reduce_sum", "scalar_tensor_tensor", "tensor_tensor_scan", "memset",
    "memzero", "reciprocal", "select", "activation", "copy", "add", "mul",
    "sign", "sqrt", "square", "iota", "transpose",
}
TENSOR_OPS = {"matmul", "transpose", "ldweights"}


class POp:
    __slots__ = ("kind", "name", "args", "kwargs", "incs", "val")

    def __init__(self, kind, name=None, args=(), kwargs=None, val=None):
        self.kind, self.name = kind, name
        self.args, self.kwargs = args, (kwargs or {})
        self.incs, self.val = [], val


class EngProxy:
    def __init__(self, pl, eng, chain=None):
        self._pl, self._eng, self._chain = pl, eng, chain

    def __getattr__(self, name):
        pl, eng, chain = self._pl, self._eng, self._chain

        def emit(*a, **kw):
            nochain = kw.pop("nochain", False)
            op = POp("op", name, a, kw)
            if chain is not None and name in DATA_OPS:
                if pl.cnt[chain] > 0 and not nochain:
                    pl.ops[eng].append(POp("wait", chain, val=pl.cnt[chain]))
                op.incs.append((chain, 1))
                pl.cnt[chain] += 1
            elif eng == "tensor" and name in TENSOR_OPS:
                pl.tcnt += 1
                pl.last_top = op
            pl.ops[eng].append(op)
            return op
        return emit


class Pl:
    def __init__(self):
        self.ops = {e: [] for e in ENGS}
        self.cnt = {}
        self.h = {}
        self.tcnt = 0
        self.last_top = None
        self.v = EngProxy(self, "vector", "vch")
        self.s = EngProxy(self, "scalar", "sch")
        self.t = EngProxy(self, "tensor")
        self.g = EngProxy(self, "gpsimd")
        self.p = EngProxy(self, "gpsimd", "pch")
        self.q = EngProxy(self, "sync")

    def newsem(self, name, handle):
        self.h[name] = handle
        self.cnt[name] = 0

    def dma(self, eng, out, in_, sem, **kw):
        op = POp("op", "dma_start", (), {"out": out, "in_": in_, **kw})
        op.incs.append((sem, 16))
        self.cnt[sem] += 16
        self.ops[eng].append(op)
        return self.cnt[sem]

    def wait(self, eng, sem, val=None):
        v = self.cnt[sem] if val is None else val
        if v > 0:
            self.ops[eng].append(POp("wait", sem, val=v))
        return v

    def vmark(self):
        return self.cnt["vch"]

    def smark(self):
        return self.cnt["sch"]

    def pmark(self):
        return self.cnt["pch"]

    def tmark(self):
        assert self.last_top is not None
        for (snm, k) in self.last_top.incs:
            if snm == "tms":
                return self.cnt["tms"]
        self.last_top.incs.append(("tms", 1))
        self.cnt["tms"] += 1
        return self.cnt["tms"]

    def bump(self, eng, sem, k=1):
        op = POp("op", "nop", (), {})
        op.incs.append((sem, k))
        self.cnt[sem] += k
        self.ops[eng].append(op)
        return self.cnt[sem]

    def barrier(self, name):
        tv = self.tmark()
        vv = self.cnt["vch"]
        sv = self.cnt["sch"]
        pv = self.cnt["pch"]
        for eng in ("sync", "gpsimd"):
            self.bump(eng, name)
        bn = self.cnt[name]
        for eng in ENGS:
            self.wait(eng, "tms", tv)
            self.wait(eng, "vch", vv)
            self.wait(eng, "sch", sv)
            self.wait(eng, "pch", pv)
            self.wait(eng, name, bn)
        return bn

    def replay(self, block):
        for eng_name in ENGS:
            def _mk(name):
                def _f(eng):
                    for op in self.ops[name]:
                        if op.kind == "wait":
                            eng.wait_ge(self.h[op.name], op.val)
                        else:
                            instr = getattr(eng, op.name)(*op.args, **op.kwargs)
                            for (snm, k) in op.incs:
                                instr.then_inc(self.h[snm], k)
                return _f
            getattr(block, eng_name)(_mk(eng_name))


R_RS32, R_TMP, R_R1040 = 0, 2, 4
R_LS, R_CH = 32, 48
R_WS = 64


def build_nc(debug=False, part=None):
    if part is None:
        part = PART
    nc = bass.Bass("TRN2", target_bir_lowering=False)
    pl = Pl()
    C = SimpleNamespace(nc=nc, pl=pl, debug=debug)

    def din(name, shape, dt=F32):
        return nc.dram_tensor(name, shape, dt, kind="ExternalInput").ap()

    C.xkv = din("xkv", [D, T])
    C.xq = din("xq", [D, TOWN])
    C.wqsem = din("wqsem", [D, H * DS])
    C.wqgeo = din("wqgeo", [D, H * DG])
    C.wksem = din("wksem", [D, HKV * DS])
    C.wkgeo = din("wkgeo", [D, HKV * DG])
    C.wv = din("wv", [D, HKV * DV])
    C.wo = din("wo", [H * DV, D])
    C.lsc = din("lsc", [1, H])
    C.wgch = din("wgch", [TOWN, D])
    C.wuch = din("wuch", [TOWN, D])
    C.wdch = din("wdch", [D, TOWN])
    C.ident = din("ident", [128, 128], BF16)
    C.identf = din("identf", [128, 128])
    C.h128 = din("h128", [128, 128], BF16)
    C.permrope = din("permrope", [128, 128], BF16)
    C.poolm = din("poolm", [128, 128], BF16)
    C.kcos_h = din("kcos_h", [128, T]); C.kcos_l = din("kcos_l", [128, T])
    C.ksin_h = din("ksin_h", [128, T]); C.ksin_l = din("ksin_l", [128, T])
    C.qcos_h = din("qcos_h", [128, TOWN]); C.qcos_l = din("qcos_l", [128, TOWN])
    C.qsin_h = din("qsin_h", [128, TOWN]); C.qsin_l = din("qsin_l", [128, TOWN])
    C.smaskp = din("smaskp", [128, 64], BF16)
    C.tmaskp = din("tmaskp", [16, 512], BF16)
    C.smaskl = din("smaskl", [128, 1024], BF16)
    C.tmaskl = din("tmaskl", [128, 1024], BF16)
    C.blendrow = din("blendrow", [1, 128])

    C.out = nc.dram_tensor("out", [D, TOWN], F32, kind="ExternalOutput").ap()
    C.dbg = {}
    if debug:
        for nm, sh, dt in [
            ("d_kct", [128, 4 * 1152], BF16), ("d_qct", [128, H * TOWN], BF16),
            ("d_v2t", [128, 9 * 260], BF16), ("d_meanv", [128, 4], F32),
            ("d_att", [128, 2 * 8 * CHT], BF16),
            ("d_x1", [128, 2 * 16 * CHT], F32),
            ("d_rows", [1, 128], F32), ("d_scr", [1, 16384], BF16),
            ("d_y", [128, M_I * CHT], BF16), ("d_xq", [128, 16 * CHT], BF16),
        ]:
            C.dbg[nm] = nc.dram_tensor(nm, sh, dt, kind="ExternalOutput").ap()

    C.mbounce = nc.dram_tensor("mbounce", [64, 128], BF16).ap()
    C.ccs_in = nc.dram_tensor("ccs_in", [1, 4], F32).ap()
    C.ccs_out = nc.dram_tensor("ccs_out", [1, 4], F32).ap()
    C.scr_own = nc.dram_tensor("scr_own", [1, OWN_N], BF16).ap()
    C.scr_all = nc.dram_tensor("scr_all", [NCN, OWN_N], BF16,
                               addr_space="Shared").ap()
    C.r2b = nc.dram_tensor("r2b", [D, CHT], F32).ap()
    C.x1d = nc.dram_tensor("x1d", [D, TOWN], F32).ap()
    C.xqd = nc.dram_tensor("xqd", [2 * D, CHT], BF16).ap()
    C.qbounce = nc.dram_tensor("qbounce", [32, 16], F32).ap()

    with ExitStack() as est:
        e = est.enter_context
        for nm in ("vch", "sch", "tms", "pch",
                   "ldcst", "ldmsk", "ldwA", "ldxq", "ldm", "ldwo",
                   "pw0", "pw1", "pq0", "pq1", "arb", "sw0", "sw1",
                   "agc", "arc", "boun", "xr0", "xr1",
                   "wg0", "wg1", "wu0", "wu1", "wd0", "wd1", "wd2", "wd3",
                   "x1r0", "x1r1", "r2w", "r2r0", "r2r1", "outw", "qbn",
                   "xqw",
                   "bar1", "bar2"):
            pl.newsem(nm, e(nc.semaphore(nm)))

        C.pb = [e(nc.psum_tensor("pb%d" % i, [128, 512], F32)) for i in range(6)]
        C.pst = e(nc.psum_tensor("pst", [128, 512], F32))
        C.ptr = e(nc.psum_tensor("ptr", [128, 1024], BF16))

        C.sident = e(nc.sbuf_tensor("sident", [128, 128], BF16))
        C.sidentf = e(nc.sbuf_tensor("sidentf", [128, 128], F32))
        C.sh128 = e(nc.sbuf_tensor("sh128", [128, 128], BF16))
        C.sperm = e(nc.sbuf_tensor("sperm", [128, 128], BF16))
        C.spoolm = e(nc.sbuf_tensor("spoolm", [128, 128], BF16))
        C.onescol_b = e(nc.sbuf_tensor("onescol_b", [128, 1], BF16))
        C.onescol_f = e(nc.sbuf_tensor("onescol_f", [128, 1], F32))
        C.onesrow_f = e(nc.sbuf_tensor("onesrow_f", [1, 128], F32))
        C.rows = e(nc.sbuf_tensor("rows", [1, 128], F32))
        C.rcol = e(nc.sbuf_tensor("rcol", [128, 16], F32))
        C.MEANV = e(nc.sbuf_tensor("MEANV", [128, 4], F32))
        C.RCONST = e(nc.sbuf_tensor("RCONST", [128, 4], F32))
        C.CVALL = e(nc.sbuf_tensor("CVALL", [128, 16], F32))
        C.s_blend = e(nc.sbuf_tensor("s_blend", [1, 128], F32))
        C.TROWS = e(nc.sbuf_tensor("TROWS", [1, 2048], F32))
        C.PMROW = e(nc.sbuf_tensor("PMROW", [1, 2048], F32))
        C.aq_v = {}
        C.xqw_done = {}
        C.qs_v = {}
        C.xres_free = {}
        C.WMC = e(nc.sbuf_tensor("WMC", [128, 4], F32))

        _emit_base_loads(C)
        _emit_derived(C)

        with ExitStack() as estAB:
            eab = estAB.enter_context
            C.KCT = eab(nc.sbuf_tensor("KCT", [128, 4 * 1152], BF16))
            C.V2T = eab(nc.sbuf_tensor("V2T", [128, 9 * 260], BF16))
            C.QCT = eab(nc.sbuf_tensor("QCT", [128, H * TOWN], BF16))
            C.s_smaskp = eab(nc.sbuf_tensor("s_smaskp", [128, 64], BF16))
            C.s_tmaskp = eab(nc.sbuf_tensor("s_tmaskp", [16, 512], BF16))
            C.s_smaskl = eab(nc.sbuf_tensor("s_smaskl", [128, 1024], BF16))
            C.s_tmaskl = eab(nc.sbuf_tensor("s_tmaskl", [128, 1024], BF16))
            _emit_ab_loads(C)
            with ExitStack() as estA:
                if STAGE >= 4:
                    gA = _gen_phaseA(C, estA.enter_context)
                    gP = _gen_prepass(C, estA.enter_context)

                    def pump(g, n):
                        for _ in range(n):
                            try:
                                next(g)
                            except StopIteration:
                                return False
                        return True

                    pump(gA, 1)
                    for _ in range(4):
                        pump(gA, 1)
                        pump(gP, 2)
                    pump(gP, 2)
                    pump(gA, 1)
                    while True:
                        alive = pump(gA, 1)
                        pump(gP, 2)
                        if not alive:
                            break
                    while pump(gP, 8):
                        pass
                elif STAGE >= 1:
                    for _ in _gen_phaseA(C, estA.enter_context):
                        pass
                if debug:
                    _emit_dbgA(C)
            if part >= 2:
                pl.barrier("bar1")
                with ExitStack() as estB:
                    _emit_attn_all(C, estB.enter_context)
        if part >= 3:
            pl.barrier("bar2")
            with ExitStack() as estF:
                _emit_ffn_all(C, estF.enter_context)
        pl.wait("gpsimd", "outw")
        pl.wait("gpsimd", "r2w")

        block = e(nc.Block())
        pl.replay(block)
    return nc


def _emit_base_loads(C):
    pl = C.pl
    g = lambda t, s: pl.dma("gpsimd", t, s, "ldcst")
    g(C.sident[:], C.ident)
    g(C.sidentf[:], C.identf)
    g(C.sh128[:], C.h128)
    g(C.sperm[:], C.permrope)
    g(C.spoolm[:], C.poolm)
    g(C.s_blend[:], C.blendrow)
    g(C.rows[0:1, R_LS:R_LS + 16], C.lsc)
    C.BASELOAD = pl.cnt["ldcst"]


def _emit_ab_loads(C):
    pl = C.pl
    g = lambda t, s: pl.dma("gpsimd", t, s, "ldmsk")
    g(C.s_smaskp[:], C.smaskp)
    g(C.s_tmaskp[:], C.tmaskp)
    g(C.s_smaskl[:], C.smaskl)
    g(C.s_tmaskl[:], C.tmaskl)
    C.MSKLOAD = pl.cnt["ldmsk"]


def _emit_derived(C):
    pl, rows = C.pl, C.rows
    pl.wait("vector", "ldcst", C.BASELOAD)
    pl.v.memset(rows[0:1, 0:32], 0.0)
    pl.v.memset(rows[0:1, 48:128], 0.0, nochain=True)
    pl.v.memset(C.onescol_b[:], 1.0)
    pl.v.memset(C.onescol_f[:], 1.0)
    pl.v.memset(C.onesrow_f[:], 1.0)
    pl.v.memset(rows[0:1, R_RS32:R_RS32 + 1], 0.1767766952966369)
    for _ in range(2):
        pl.v.tensor_tensor(rows[0:1, R_TMP:R_TMP + 1],
                           rows[0:1, R_RS32:R_RS32 + 1],
                           rows[0:1, R_RS32:R_RS32 + 1], ALU.mult)
        pl.v.tensor_scalar(rows[0:1, R_TMP:R_TMP + 1],
                           rows[0:1, R_TMP:R_TMP + 1],
                           -16.0, 1.5, ALU.mult, ALU.add)
        pl.v.tensor_tensor(rows[0:1, R_RS32:R_RS32 + 1],
                           rows[0:1, R_RS32:R_RS32 + 1],
                           rows[0:1, R_TMP:R_TMP + 1], ALU.mult)
    pl.v.memset(rows[0:1, R_R1040:R_R1040 + 1], 1040.0)
    pl.v.reciprocal(rows[0:1, R_R1040:R_R1040 + 1],
                    rows[0:1, R_R1040:R_R1040 + 1])
    pl.wait("scalar", "vch")
    pl.s.activation(rows[0:1, R_CH:R_CH + 16], rows[0:1, R_LS:R_LS + 16],
                    ACT.Exp)
    pl.wait("vector", "sch")
    pl.v.tensor_single_scalar(rows[0:1, R_CH:R_CH + 16],
                              rows[0:1, R_CH:R_CH + 16],
                              rows[0:1, R_RS32:R_RS32 + 1], ALU.mult)
    pl.wait("tensor", "vch")
    pl.t.matmul(C.pb[5][:, 0:16], C.onesrow_f[:], rows[0:1, R_CH:R_CH + 16],
                start=True, stop=True)
    pl.t.matmul(C.pb[5][:, 16:20], C.onesrow_f[:],
                rows[0:1, R_R1040:R_R1040 + 4], start=True, stop=True)
    tv = pl.tmark()
    pl.wait("vector", "tms", tv)
    pl.v.tensor_copy(C.CVALL[:], C.pb[5][:, 0:16])
    pl.v.tensor_copy(C.RCONST[:], C.pb[5][:, 16:20])
    C.CVREADY = pl.vmark()


def _gen_phaseA(C, e):
    nc, pl = C.nc, C.pl
    QT = 512
    XTB = e(nc.sbuf_tensor("XTB", [128, 16 * QT], BF16))
    C.WQS = e(nc.sbuf_tensor("WQS", [128, 16 * 512], BF16))
    C.WQG = e(nc.sbuf_tensor("WQG", [128, 16 * 512], BF16))
    WKS = e(nc.sbuf_tensor("WKS", [128, 16 * 128], BF16))
    WKG = e(nc.sbuf_tensor("WKG", [128, 16 * 128], BF16))
    WVV = e(nc.sbuf_tensor("WVV", [128, 16 * 256], BF16))
    tabs = {nm: e(nc.sbuf_tensor("tab_" + nm, [128, QT], F32))
            for nm in ("ch", "cl", "sh", "sl")}
    KG = e(nc.sbuf_tensor("KG", [128, QT], F32))
    KGR = e(nc.sbuf_tensor("KGR", [128, QT], F32))
    KTMP = e(nc.sbuf_tensor("KTMP", [128, 2 * QT], F32))
    KGB = e(nc.sbuf_tensor("KGB", [128, QT], BF16))
    VTQ = e(nc.sbuf_tensor("VTQ", [128, 8 * 256], BF16))
    QSM = e(nc.sbuf_tensor("QSM", [128, 4 * 512], F32))
    QGO = e(nc.sbuf_tensor("QGO", [128, 4 * 512], F32))

    for (wbuf, src) in ((WKS, C.wksem), (WKG, C.wkgeo), (WVV, C.wv),
                        (C.WQS, C.wqsem), (C.WQG, C.wqgeo)):
        pl.dma("gpsimd", wbuf[:].rearrange("p (k c) -> p k c", k=16),
               src.rearrange("(k p) c -> p k c", p=128), "ldwA")
    C.WLOAD2 = pl.cnt["ldwA"]
    pl.v.memset(C.KCT[:], 0.0)
    vev = {}
    xtb_tms = 0
    yield

    for qq in range(4):
        tok0 = qq * QT
        if qq > 0:
            pl.wait("gpsimd", "tms", xtb_tms)
        pl.dma("gpsimd", XTB[:].rearrange("p (k c) -> p k c", k=16),
               C.xkv[:, tok0:tok0 + QT].rearrange("(k p) c -> p k c", p=128),
               "ldxq")
        for nm, src in (("ch", C.kcos_h), ("cl", C.kcos_l),
                        ("sh", C.ksin_h), ("sl", C.ksin_l)):
            if qq > 0:
                pl.wait("scalar", "vch", pl.vmark())
            pl.dma("scalar", tabs[nm][:], src[:, tok0:tok0 + QT], "ldm")
        XL = pl.cnt["ldxq"]
        TL = pl.cnt["ldm"]
        pl.wait("tensor", "ldxq", XL)
        pl.wait("tensor", "ldwA", C.WLOAD2)
        ev = {}
        for wi, wbuf in enumerate((WKS, WKG)):
            bi = wi
            if bi in vev:
                pl.wait("tensor", "vch", vev[bi])
            for kt in range(16):
                pl.t.matmul(C.pb[bi][:, 0:512],
                            wbuf[:, kt * 128:(kt + 1) * 128],
                            XTB[:, kt * QT:(kt + 1) * QT],
                            start=(kt == 0), stop=(kt == 15))
            tv = pl.tmark()
            pl.wait("vector", "tms", tv)
            if wi == 0:
                if qq < 2:
                    for g in range(4):
                        base = g * 1152
                        pl.v.reduce_sum(C.rcol[0:32, 0:8],
                                        C.pb[bi][g * 32:(g + 1) * 32, 0:512]
                                        .rearrange("p (b w) -> p b w", w=64),
                                        AX.X)
                        pl.v.tensor_scalar(
                            C.KCT[0:32, base + qq * 8:base + qq * 8 + 8],
                            C.rcol[0:32, 0:8], 1.0 / 64.0, None, ALU.mult)
                else:
                    for g in range(4):
                        base = g * 1152
                        pl.v.tensor_copy(
                            C.KCT[0:32, base + 16 + (qq - 2) * 512:
                                  base + 16 + (qq - 1) * 512],
                            C.pb[bi][g * 32:(g + 1) * 32, 0:512],
                            nochain=(g > 0))
            else:
                pl.v.tensor_copy(KG[:], C.pb[bi][:, 0:512])
            ev[wi] = pl.vmark()
            vev[bi] = ev[wi]
        pl.v.tensor_copy(KGB[:], KG[:])
        kgb_v = pl.vmark()
        pl.wait("tensor", "vch", kgb_v)
        pl.t.matmul(C.pb[2][:, 0:512], C.sperm[:], KGB[:], start=True,
                    stop=True)
        tv = pl.tmark()
        pl.wait("vector", "tms", tv)
        pl.wait("vector", "ldm", TL)
        pl.v.tensor_tensor(KGR[:], KG[:], tabs["ch"][:], ALU.mult)
        pl.v.tensor_tensor(KTMP[:, 0:512], KG[:], tabs["cl"][:], ALU.mult,
                           nochain=True)
        pl.v.tensor_tensor(KGR[:], KGR[:], KTMP[:, 0:512], ALU.add)
        pl.v.tensor_copy(KTMP[:, 0:512], C.pb[2][:, 0:512])
        pl.v.tensor_tensor(KTMP[:, 0:512], KTMP[:, 0:512], tabs["sh"][:],
                           ALU.mult)
        pl.v.tensor_tensor(KGR[:], KGR[:], KTMP[:, 0:512], ALU.add)
        pl.v.tensor_copy(KTMP[:, 0:512], C.pb[2][:, 0:512])
        pl.v.tensor_tensor(KTMP[:, 0:512], KTMP[:, 0:512], tabs["sl"][:],
                           ALU.mult)
        pl.v.tensor_tensor(KGR[:], KGR[:], KTMP[:, 0:512], ALU.add)
        if qq == 0:
            for g in range(4):
                pl.v.memset(C.KCT[64:68, g * 1152:g * 1152 + 1040], 1.0,
                            nochain=(g > 0))
        if qq < 2:
            for g in range(4):
                base = g * 1152
                pl.v.reduce_sum(C.rcol[0:32, 0:8],
                                KGR[g * 32:(g + 1) * 32, :]
                                .rearrange("p (b w) -> p b w", w=64), AX.X)
                pl.v.tensor_scalar(
                    C.KCT[32:64, base + qq * 8:base + qq * 8 + 8],
                    C.rcol[0:32, 0:8], 1.0 / 64.0, None, ALU.mult)
        else:
            for g in range(4):
                base = g * 1152
                pl.v.tensor_copy(C.KCT[32:64, base + 16 + (qq - 2) * 512:
                                       base + 16 + (qq - 1) * 512],
                                 KGR[g * 32:(g + 1) * 32, :],
                                 nochain=(g > 0))
        kct_v = pl.vmark()

        for ti in range(4):
            tt = qq * 4 + ti
            bi = 3 + tt % 2
            if bi in vev:
                pl.wait("tensor", "vch", vev[bi])
            for kt in range(16):
                pl.t.matmul(C.pb[bi][:, 0:256],
                            XTB[:, kt * QT + ti * 128:kt * QT + (ti + 1) * 128],
                            WVV[:, kt * 256:(kt + 1) * 256],
                            start=(kt == 0), stop=(kt == 15))
            tv = pl.tmark()
            pl.wait("vector", "tms", tv)
            if tt < 8:
                pl.v.tensor_copy(VTQ[:, tt * 256:(tt + 1) * 256],
                                 C.pb[bi][:, 0:256])
            else:
                kt2 = tt - 7
                for g in range(4):
                    pl.v.tensor_copy(
                        C.V2T[:, kt2 * 260 + g * 65:kt2 * 260 + g * 65 + 64],
                        C.pb[bi][:, g * 64:(g + 1) * 64], nochain=(g > 0))
            vev[bi] = pl.vmark()
        xtb_tms = tv
        yield

    pl.wait("tensor", "vch", pl.vmark())
    for tt in range(8):
        pl.t.matmul(C.pb[5][0:16, 0:256], C.spoolm[:, tt * 16:(tt + 1) * 16],
                    VTQ[:, tt * 256:(tt + 1) * 256],
                    start=(tt == 0), stop=(tt == 7))
    tv = pl.tmark()
    pl.wait("vector", "tms", tv)
    pl.v.memset(C.V2T[:, 0:260], 0.0)
    for g in range(4):
        pl.v.tensor_copy(C.V2T[0:16, g * 65:g * 65 + 64],
                         C.pb[5][0:16, g * 64:(g + 1) * 64], nochain=(g > 0))
    for kt in range(9):
        pp = 16 if kt == 0 else 128
        pl.v.memset(C.V2T[0:pp, kt * 260:(kt + 1) * 260]
                    .rearrange("p (g c) -> p g c", g=4)[:, :, 64:65], 1.0,
                    nochain=(kt > 0))
    C.V2T_V = pl.vmark()
    pl.wait("tensor", "vch", C.V2T_V)
    for g in range(4):
        for kt in range(9):
            pp = 16 if kt == 0 else 128
            pl.t.matmul(C.pb[5][0:65, g:g + 1],
                        C.V2T[0:pp, kt * 260 + g * 65:kt * 260 + (g + 1) * 65],
                        C.onescol_b[0:pp, 0:1], start=(kt == 0), stop=(kt == 8))
    tv = pl.tmark()
    pl.wait("vector", "tms", tv)
    pl.v.tensor_single_scalar(C.MEANV[0:64, 0:4], C.pb[5][0:64, 0:4],
                              C.RCONST[0:64, 0:1], ALU.mult)
    pl.v.tensor_single_scalar(C.MEANV[64:128, 0:4], C.pb[5][0:64, 0:4],
                              C.RCONST[0:64, 0:1], ALU.mult, nochain=True)
    C.MEANV_V = pl.vmark()
    yield

    for hf in range(2):
        tk0 = hf * 512
        pl.wait("gpsimd", "tms", xtb_tms)
        pl.dma("gpsimd", XTB[:].rearrange("p (k c) -> p k c", k=16),
               C.xq[:, tk0:tk0 + 512].rearrange("(k p) c -> p k c", p=128),
               "ldxq")
        for nm, src in (("ch", C.qcos_h), ("cl", C.qcos_l),
                        ("sh", C.qsin_h), ("sl", C.qsin_l)):
            pl.wait("scalar", "vch", pl.vmark())
            pl.dma("scalar", tabs[nm][:], src[:, tk0:tk0 + 512], "ldm")
        XL = pl.cnt["ldxq"]
        TL = pl.cnt["ldm"]
        yield
        pl.wait("tensor", "ldxq", XL)
        QSEM, QGEO = QSM, QGO
        ev = {}
        for wi, (wbuf, dstq) in enumerate(((C.WQS, QSEM), (C.WQG, QGEO))):
            for mt in range(4):
                bi = (wi * 4 + mt) % 2
                if bi in ev:
                    pl.wait("tensor", "vch", ev[bi])
                elif bi in vev:
                    pl.wait("tensor", "vch", vev[bi])
                for kt in range(16):
                    pl.t.matmul(C.pb[bi][:, 0:512],
                                wbuf[:, kt * 512 + mt * 128:
                                     kt * 512 + (mt + 1) * 128],
                                XTB[:, kt * 512:(kt + 1) * 512],
                                start=(kt == 0), stop=(kt == 15))
                tv = pl.tmark()
                pl.wait("vector", "tms", tv)
                pl.wait("vector", "vch", C.CVREADY)
                for hh in range(4):
                    h = mt * 4 + hh
                    pl.v.tensor_scalar(dstq[hh * 32:(hh + 1) * 32,
                                            mt * 512:(mt + 1) * 512],
                                       C.pb[bi][hh * 32:(hh + 1) * 32, 0:512],
                                       C.CVALL[hh * 32:hh * 32 + 32, h:h + 1],
                                       None, ALU.mult, nochain=(hh > 0))
                ev[bi] = pl.vmark()
            yield
        xtb_tms = tv
        for mt in range(4):
            pl.v.tensor_copy(KGB[:], QGEO[:, mt * 512:(mt + 1) * 512])
            qgb_v = pl.vmark()
            pl.wait("tensor", "vch", qgb_v)
            pl.t.matmul(C.pb[2][:, 0:512], C.sperm[:], KGB[:],
                        start=True, stop=True)
            tv = pl.tmark()
            pl.wait("vector", "tms", tv)
            pl.wait("vector", "ldm", TL)
            sl = slice(mt * 512, (mt + 1) * 512)
            pl.v.tensor_tensor(KTMP[:, 0:512], QGEO[:, sl], tabs["ch"][:],
                               ALU.mult)
            pl.v.tensor_tensor(KTMP[:, 512:1024], QGEO[:, sl], tabs["cl"][:],
                               ALU.mult, nochain=True)
            pl.v.tensor_tensor(KTMP[:, 0:512], KTMP[:, 0:512],
                               KTMP[:, 512:1024], ALU.add)
            pl.v.tensor_copy(KTMP[:, 512:1024], C.pb[2][:, 0:512])
            pl.v.tensor_tensor(KTMP[:, 512:1024], KTMP[:, 512:1024],
                               tabs["sh"][:], ALU.mult)
            pl.v.tensor_tensor(KTMP[:, 0:512], KTMP[:, 0:512],
                               KTMP[:, 512:1024], ALU.add)
            pl.v.tensor_copy(KTMP[:, 512:1024], C.pb[2][:, 0:512])
            pl.v.tensor_tensor(KTMP[:, 512:1024], KTMP[:, 512:1024],
                               tabs["sl"][:], ALU.mult)
            pl.v.tensor_tensor(QGEO[:, sl], KTMP[:, 0:512],
                               KTMP[:, 512:1024], ALU.add)
            yield
        for h in range(16):
            mt, hh = h // 4, h % 4
            col = h * TOWN + hf * 512
            pl.v.tensor_copy(C.QCT[0:32, col:col + 512],
                             QSEM[hh * 32:(hh + 1) * 32,
                                  mt * 512:(mt + 1) * 512],
                             nochain=(h > 0))
            pl.v.tensor_copy(C.QCT[32:64, col:col + 512],
                             QGEO[hh * 32:(hh + 1) * 32,
                                  mt * 512:(mt + 1) * 512], nochain=True)
    pl.v.memset(C.QCT[64:128, :], 0.0)
    C.QCT_V = pl.vmark()


def _gen_prepass(C, e):
    nc, pl, rows = C.nc, C.pl, C.rows
    WFP = e(nc.sbuf_tensor("WFP", [128, 2 * 2048], F32))
    WQB = e(nc.sbuf_tensor("WQB", [128, 2048], BF16))
    SLAB = e(nc.sbuf_tensor("SLAB", [128, 2 * 2048], BF16))
    PSUMS = e(nc.sbuf_tensor("PSUMS", [128, 32], F32))
    WSCOL = e(nc.sbuf_tensor("WSCOL", [128, 8], F32))

    wtiles = [("g", C.wgch, i, 2048) for i in range(8)] + \
             [("u", C.wuch, i, 2048) for i in range(8)] + \
             [("d", C.wdch, i, 1024) for i in range(16)]

    red_vs = []
    for ti, (kind, ap, i, ncol) in enumerate(wtiles):
        buf = WFP[:, (ti % 2) * 2048:(ti % 2) * 2048 + ncol]
        sem = "pw%d" % (ti % 2)
        if ti >= 2:
            pl.wait("sync", "vch", red_vs[ti - 2])
        pl.dma("sync", buf, ap[i * 128:(i + 1) * 128, 0:ncol], sem)
        pl.wait("vector", sem, pl.cnt[sem])
        pl.v.reduce_sum(PSUMS[:, ti:ti + 1], buf, AX.X,
                        apply_absolute_value=True, nochain=True)
        red_vs.append(pl.vmark())
        if ti % 4 == 3:
            yield
    if SUBST < 1:
        return
    pl.wait("tensor", "vch", red_vs[-1])
    pl.t.matmul(C.pst[0:1, 0:32], C.onescol_f[:], PSUMS[:, 0:32],
                start=True, stop=True)
    tv = pl.tmark()
    pl.wait("vector", "tms", tv)
    pl.v.reduce_sum(rows[0:1, R_WS + 0:R_WS + 1], C.pst[0:1, 0:8], AX.X)
    pl.v.reduce_sum(rows[0:1, R_WS + 1:R_WS + 2], C.pst[0:1, 8:16], AX.X,
                    nochain=True)
    pl.v.reduce_sum(rows[0:1, R_WS + 2:R_WS + 3], C.pst[0:1, 16:32], AX.X,
                    nochain=True)
    pl.v.memset(rows[0:1, R_WS + 3:R_WS + 4], 0.0, nochain=True)
    if SUBST < 2:
        return
    pl.wait("gpsimd", "vch", pl.vmark())
    pl.dma("gpsimd", C.ccs_in, rows[0:1, R_WS:R_WS + 4], "arb")
    pl.wait("gpsimd", "arb", pl.cnt["arb"])
    op = pl.g.collective_compute(
        "AllReduce", ALU.add, replica_groups=[list(range(NCN))],
        ins=[C.ccs_in], outs=[C.ccs_out])
    op.incs.append(("arc", 1))
    pl.cnt["arc"] += 1
    pl.wait("gpsimd", "arc")
    pl.dma("gpsimd", rows[0:1, R_WS + 4:R_WS + 8], C.ccs_out, "arb")
    pl.wait("vector", "arb", pl.cnt["arb"])
    pl.v.tensor_scalar(rows[0:1, R_WS + 8:R_WS + 11],
                       rows[0:1, R_WS + 4:R_WS + 7],
                       1.0 / 16777216.0, EPS, ALU.mult, ALU.max)
    pl.v.reciprocal(rows[0:1, R_WS + 11:R_WS + 14],
                    rows[0:1, R_WS + 8:R_WS + 11])
    pl.v.reciprocal(rows[0:1, R_WS + 14:R_WS + 17],
                    rows[0:1, R_WS + 11:R_WS + 14])
    scl_v = pl.vmark()
    pl.wait("tensor", "vch", scl_v)
    pl.t.matmul(C.pst[:, 40:46], C.onesrow_f[:],
                rows[0:1, R_WS + 11:R_WS + 17], start=True, stop=True)
    tv = pl.tmark()
    pl.wait("vector", "tms", tv)
    pl.v.tensor_copy(WSCOL[:, 0:6], C.pst[:, 40:46])
    pl.v.tensor_copy(C.WMC[:, 0:3], C.pst[:, 43:46], nochain=True)
    scl_v = pl.vmark()
    yield
    if SUBST < 3:
        return

    tern_vs, tF_after, scrw_after = [], [], []
    evac_v = pl.vmark()
    down_base = 2 * OWN_G
    for jj, (kind, ap, i, ncol) in enumerate(wtiles):
        scol = {"g": 0, "u": 1, "d": 2}[kind]
        mcol = 3 + {"g": 0, "u": 1, "d": 2}[kind]
        buf = WFP[:, (jj % 2) * 2048:(jj % 2) * 2048 + ncol]
        sem = "pq%d" % (jj % 2)
        if jj >= 2:
            pl.wait("sync", "vch", tern_vs[jj - 2])
        else:
            pl.wait("sync", "vch", red_vs[30 + jj])
        pl.dma("sync", buf, ap[i * 128:(i + 1) * 128, 0:ncol], sem)
        pl.wait("vector", sem, pl.cnt[sem])
        pl.wait("vector", "vch", scl_v)
        if jj >= 1:
            pl.wait("vector", "tms", tF_after[jj - 1])
        if jj >= 2:
            pl.wait("vector", "sw%d" % (jj % 2), scrw_after[jj - 2])
        z = buf
        pl.v.tensor_single_scalar(z, buf, WSCOL[:, scol:scol + 1], ALU.mult,
                                  nochain=True)
        pl.v.tensor_scalar(WQB[:, 0:ncol], z, -0.5, None, ALU.is_lt)
        pl.v.scalar_tensor_tensor(WQB[:, 0:ncol], z, 0.5, WQB[:, 0:ncol],
                                  ALU.is_gt, ALU.subtract)
        tern_vs.append(pl.vmark())
        par = jj % 2
        if SUBST < 4:
            tF_after.append(pl.cnt["tms"])
            scrw_after.append(pl.cnt["sw%d" % par])
            continue
        pl.wait("tensor", "vch", tern_vs[-1])
        for grp in range(ncol // 1024):
            pl.wait("tensor", "vch", evac_v)
            for j in range(8):
                bkk = grp * 8 + j
                pl.t.transpose(C.ptr[:, j * 128:(j + 1) * 128],
                               WQB[:, bkk * 128:(bkk + 1) * 128], C.sident[:])
            tv = pl.tmark()
            pl.wait("vector", "tms", tv)
            pl.v.tensor_copy(SLAB[:, par * 2048 + grp * 1024:
                                  par * 2048 + (grp + 1) * 1024],
                             C.ptr[:, 0:1024], nochain=True)
            evac_v = pl.vmark()
        tF_after.append(pl.cnt["tms"])
        if SUBST < 5:
            scrw_after.append(pl.cnt["sw%d" % par])
            continue
        pl.wait("gpsimd", "vch", evac_v)
        ssem = "sw%d" % par
        if kind in ("g", "u"):
            off = (0 if kind == "g" else OWN_G) + i * GSL
            dst = C.scr_own[0, off:off + GSL].rearrange("(p n) -> p n", p=128)
            pl.dma("gpsimd", dst, SLAB[:, par * 2048:(par + 1) * 2048], ssem)
        else:
            otp, oh = i // 2, i % 2
            base_o = down_base + otp * (128 * 2048)
            dst = C.scr_own[0, base_o:base_o + 128 * 2048].rearrange(
                "(p k c) -> p k c", p=128, k=8)[:, :, oh * 128:(oh + 1) * 128]
            pl.dma("gpsimd", dst,
                   SLAB[:, par * 2048:par * 2048 + 1024]
                   .rearrange("p (k c) -> p k c", k=8), ssem)
        scrw_after.append(pl.cnt[ssem])
        yield
    if C.debug and STAGE >= 5:
        _emit_allgather(C)


def _emit_allgather(C):
    pl = C.pl
    pl.wait("gpsimd", "sw0", pl.cnt["sw0"])
    pl.wait("gpsimd", "sw1", pl.cnt["sw1"])
    pl.wait("gpsimd", "r2w", pl.cnt["r2w"])
    pl.wait("gpsimd", "xqw", pl.cnt["xqw"])
    pl.wait("gpsimd", "ldxq", pl.cnt["ldxq"])
    op = pl.g.collective_compute(
        "AllGather", ALU.bypass, replica_groups=[list(range(NCN))],
        ins=[C.scr_own], outs=[C.scr_all])
    op.incs.append(("agc", 1))
    pl.cnt["agc"] += 1
    C.AGDONE = pl.cnt["agc"]


def _emit_dbgA(C):
    pl = C.pl
    pl.wait("gpsimd", "vch", pl.vmark())
    if STAGE >= 1:
        pl.dma("gpsimd", C.dbg["d_kct"], C.KCT[:], "outw")
        pl.dma("gpsimd", C.dbg["d_qct"], C.QCT[:], "outw")
        pl.dma("gpsimd", C.dbg["d_v2t"], C.V2T[:], "outw")
        pl.dma("gpsimd", C.dbg["d_meanv"], C.MEANV[:], "outw")
    pl.dma("gpsimd", C.dbg["d_rows"], C.rows[:], "outw")
    if STAGE < 5:
        return
    pl.wait("gpsimd", "agc", C.AGDONE)
    pl.dma("gpsimd", C.dbg["d_scr"][0:1, 0:4096], C.scr_all[0:1, 0:4096],
           "outw")
    pl.dma("gpsimd", C.dbg["d_scr"][0:1, 4096:8192],
           C.scr_all[3:4, OWN_G:OWN_G + 4096], "outw")
    pl.dma("gpsimd", C.dbg["d_scr"][0:1, 8192:12288],
           C.scr_all[6:7, 2 * OWN_G:2 * OWN_G + 4096], "outw")


def _emit_attn_all(C, e):
    nc, pl = C.nc, C.pl
    WOO = e(nc.sbuf_tensor("WOO", [128, 8 * 2048], BF16))
    EXPB = e(nc.sbuf_tensor("EXPB", [128, 2 * 9 * 512], BF16))
    ATT = e(nc.sbuf_tensor("ATT", [128, 8 * 512], BF16))
    X1C = e(nc.sbuf_tensor("X1C", [128, 16 * CHT], F32))
    MCOL = e(nc.sbuf_tensor("MCOL", [128, 64], F32))
    MROW = e(nc.sbuf_tensor("MROW", [64, 128], BF16))
    XRES = e(nc.sbuf_tensor("XRES", [128, 2 * 512], F32))
    BCB = e(nc.sbuf_tensor("BCB", [128, 1024], F32))
    BL1 = e(nc.sbuf_tensor("BL1", [128, 128], F32))
    BLM = e(nc.sbuf_tensor("BLM", [128, 128], F32))
    R1B = e(nc.sbuf_tensor("R1B", [128, 512], BF16))
    SQT = e(nc.sbuf_tensor("SQT", [128, 512], BF16))
    QS1 = e(nc.sbuf_tensor("QS1", [128, 512], F32))
    QS2 = e(nc.sbuf_tensor("QS2", [128, 512], F32))
    ATI8 = e(nc.sbuf_tensor("ATI8", [128, 512], I8))
    AQB = e(nc.sbuf_tensor("AQB", [128, 2 * 512], BF16))
    M2A = e(nc.sbuf_tensor("M2A", [32, 16], F32))
    bufs = (WOO, EXPB, ATT, X1C, MCOL, MROW, XRES, BCB, BL1, BLM, R1B, SQT,
            QS1, QS2, ATI8, AQB, M2A)

    pl.dma("gpsimd", WOO[:].rearrange("p (k c) -> p k c", k=8),
           C.wo.rearrange("(k p) c -> p k c", p=128), "ldwo")
    C.WOL = pl.cnt["ldwo"]
    if not C.debug and STAGE >= 5:
        _emit_allgather(C)
    pl.wait("vector", "ldmsk", C.MSKLOAD)
    pl.wait("sync", "outw")

    pl.t.matmul(C.pb[2][:, 0:128], C.onesrow_f[:], C.s_blend[:],
                start=True, stop=True)
    tv = pl.tmark()
    pl.wait("vector", "tms", tv)
    pl.v.tensor_copy(BLM[:], C.pb[2][:, 0:128])
    pl.v.tensor_scalar(BL1[:], C.pb[2][:, 0:128], -1.0, 1.0,
                       ALU.mult, ALU.add, nochain=True)

    g0 = _gen_attn_chunk(C, 0, bufs)
    g1 = _gen_attn_chunk(C, 1, bufs)
    next(g0)
    next(g0)
    if ATTN_IL:
        next(g1)
        for _ in g0:
            pass
        for _ in g1:
            pass
    else:
        for _ in g0:
            pass
        for _ in g1:
            pass
    if C.debug:
        pl.wait("gpsimd", "r2w")
        pl.dma("gpsimd", C.dbg["d_x1"][:, 0:16 * CHT]
               .rearrange("p (k c) -> p k c", k=16),
               C.x1d[:, 0:CHT].rearrange("(k p) c -> p k c", p=128), "outw")
        pl.dma("gpsimd", C.dbg["d_x1"][:, 16 * CHT:]
               .rearrange("p (k c) -> p k c", k=16),
               C.x1d[:, CHT:].rearrange("(k p) c -> p k c", p=128), "outw")


def _gen_attn_chunk(C, ck, bufs):
    (WOO, EXPB, ATT, X1C, MCOL, MROW, XRES, BCB, BL1, BLM, R1B, SQT,
     QS1, QS2, ATI8, AQB, M2A) = bufs
    RECB = BCB[:, 0:512]
    RSB = BCB[:, 0:512]
    NBB = BCB[:, 512:1024]
    nc, pl = C.nc, C.pl
    LOCS = SLOT_LOC[ck * 4:(ck + 1) * 4]
    pl.wait("tensor", "vch", pl.vmark())
    if C.debug:
        pl.wait("vector", "outw")

    red_done = {}
    for s4 in range(4):
        loc = LOCS[s4]
        for h in range(16):
            g = h // 4
            par = h % 2
            lb = (C.pb[0], C.pb[1]) if par == 0 else (C.pb[2], C.pb[3])
            poolb = C.pb[4] if par == 0 else C.pb[5]
            qcol = h * TOWN + ck * 512 + s4 * 128
            if par in red_done:
                pl.wait("tensor", "vch", red_done[par])
            base = g * 1152
            pl.t.matmul(poolb[:, 0:16], C.QCT[0:64, qcol:qcol + 128],
                        C.KCT[0:64, base:base + 16], start=True, stop=True)
            for kt in range(loc):
                pl.t.matmul(lb[kt // 4][:, (kt % 4) * 128:(kt % 4 + 1) * 128],
                            C.QCT[0:64, qcol:qcol + 128],
                            C.KCT[0:64, base + 16 + kt * 128:
                                  base + 16 + (kt + 1) * 128],
                            start=True, stop=True)
            tv = pl.tmark()
            pl.wait("vector", "tms", tv)
            ps = poolb[:, 0:16]
            if ck == 0:
                pl.v.tensor_tensor(ps, ps,
                                   C.s_smaskp[:, s4 * 16:(s4 + 1) * 16],
                                   ALU.add, nochain=True)
            pl.v.reduce_max(C.rcol[:, 0:1], ps, AX.X)
            ncand = 1
            if loc > 0:
                for w in range(2):
                    kt = loc - 2 + w
                    sl = lb[kt // 4][:, (kt % 4) * 128:(kt % 4 + 1) * 128]
                    pl.v.tensor_tensor(sl, sl,
                                       C.s_smaskl[:, (s4 * 2 + w) * 128:
                                                  (s4 * 2 + w + 1) * 128],
                                       ALU.add, nochain=True)
                n0 = min(loc, 4)
                pl.v.reduce_max(C.rcol[:, 1:2], lb[0][:, 0:n0 * 128], AX.X)
                ncand = 2
                if loc > 4:
                    pl.v.reduce_max(C.rcol[:, 2:3],
                                    lb[1][:, 0:(loc - 4) * 128], AX.X)
                    ncand = 3
            pl.v.reduce_max(MCOL[:, s4 * 16 + h:s4 * 16 + h + 1],
                            C.rcol[:, 0:ncand], AX.X)
            red_done[par] = pl.vmark()

    pl.wait("tensor", "vch", max(red_done.values()))
    pl.t.transpose(C.pb[5][0:64, 0:128], MCOL[:], C.sidentf[:])
    tv = pl.tmark()
    pl.wait("vector", "tms", tv)
    pl.v.tensor_scalar(MROW[:], C.pb[5][0:64, 0:128], -1.0, None, ALU.mult)
    mr_v = pl.vmark()
    pl.wait("sync", "vch", mr_v)
    pl.dma("sync", C.mbounce, MROW[:], "boun")
    pl.wait("sync", "boun", pl.cnt["boun"])
    for s4 in range(4):
        dst = C.QCT[64 + s4:65 + s4, :].rearrange("p (h q) -> p h q", h=16)[
            :, :, ck * 512 + s4 * 128:ck * 512 + (s4 + 1) * 128]
        src = C.mbounce[:].rearrange("(s h) q -> s h q", s=4)[s4:s4 + 1]
        pl.dma("sync", dst, src, "boun")
    MB_DONE = pl.cnt["boun"]

    yield

    pl.wait("tensor", "boun", MB_DONE)
    pl.wait("tensor", "ldwo", C.WOL)
    if ck == 1:
        pl.wait("tensor", "vch", C.qs_v[0])
    pv_evac = {}
    st_free = {3: None, 4: None}
    prev_pv_tv = None
    pv_tvs = []
    for g in range(4):
        kbase = g * 1152
        for h4 in range(4):
            h = g * 4 + h4
            qcol = h * TOWN + ck * 512
            pvb = C.pb[0] if h % 2 == 0 else C.pb[1]
            eb = (h % 2) * 9 * 512
            if len(pv_tvs) >= 2:
                pl.wait("scalar", "tms", pv_tvs[-2])
            kts = [("p", 0)] + [("l", kt) for kt in range(LOCS[3])]
            for idx, (kind, kt) in enumerate(kts):
                stb = 3 + idx % 2
                if st_free[stb] is not None:
                    pl.wait("tensor", "sch", st_free[stb])
                if kind == "p":
                    pl.t.matmul(C.pb[stb][0:16, 0:512],
                                C.KCT[0:68, kbase:kbase + 16],
                                C.QCT[0:68, qcol:qcol + 512],
                                start=True, stop=True)
                else:
                    cs = 128 * sum(1 for l in LOCS if l <= kt)
                    pl.t.matmul(C.pb[stb][:, cs:512],
                                C.KCT[0:68, kbase + 16 + kt * 128:
                                      kbase + 16 + (kt + 1) * 128],
                                C.QCT[0:68, qcol + cs:qcol + 512],
                                start=True, stop=True)
                tv = pl.tmark()
                need_mask = (kind == "p" and ck == 0) or \
                            (kind == "l" and any(l - 2 <= kt < l for l in LOCS))
                if need_mask:
                    pl.wait("vector", "tms", tv)
                    if kind == "p":
                        for s4 in range(4):
                            pl.v.tensor_tensor(
                                C.pb[stb][0:16, s4 * 128:(s4 + 1) * 128],
                                C.pb[stb][0:16, s4 * 128:(s4 + 1) * 128],
                                C.s_tmaskp[:, s4 * 128:(s4 + 1) * 128],
                                ALU.add, nochain=(s4 > 0))
                    else:
                        firstm = True
                        for s4, l in enumerate(LOCS):
                            if l - 2 <= kt < l:
                                w = kt - (l - 2)
                                pl.v.tensor_tensor(
                                    C.pb[stb][:, s4 * 128:(s4 + 1) * 128],
                                    C.pb[stb][:, s4 * 128:(s4 + 1) * 128],
                                    C.s_tmaskl[:, (s4 * 2 + w) * 128:
                                               (s4 * 2 + w + 1) * 128],
                                    ALU.add, nochain=not firstm)
                                firstm = False
                    pl.wait("scalar", "vch", pl.vmark())
                else:
                    pl.wait("scalar", "tms", tv)
                if kind == "p":
                    pl.s.activation(EXPB[0:16, eb:eb + 512],
                                    C.pb[stb][0:16, 0:512], ACT.Exp)
                else:
                    cs = 128 * sum(1 for l in LOCS if l <= kt)
                    pl.s.activation(
                        EXPB[:, eb + (1 + kt) * 512 + cs:
                             eb + (1 + kt) * 512 + 512],
                        C.pb[stb][:, cs:512], ACT.Exp)
                st_free[stb] = pl.smark()
            pl.wait("tensor", "sch", max(x for x in st_free.values()
                                         if x is not None))
            if (h % 2) in pv_evac:
                pl.wait("tensor", "vch", pv_evac[h % 2])
            pl.t.matmul(pvb[0:65, 0:512], C.V2T[0:16, g * 65:g * 65 + 65],
                        EXPB[0:16, eb:eb + 512], start=True,
                        stop=(LOCS[3] == 0))
            for kt in range(LOCS[3]):
                cs = 128 * sum(1 for l in LOCS if l <= kt)
                pl.t.matmul(pvb[0:65, cs:512],
                            C.V2T[:, (1 + kt) * 260 + g * 65:
                                  (1 + kt) * 260 + (g + 1) * 65],
                            EXPB[:, eb + (1 + kt) * 512 + cs:
                                 eb + (1 + kt) * 512 + 512],
                            start=False, stop=(kt == LOCS[3] - 1))
            prev_pv_tv = pl.tmark()
            pv_tvs.append(prev_pv_tv)
            pl.wait("scalar", "tms", prev_pv_tv)
            pl.s.activation(C.TROWS[0:1, 1536:2048], pvb[64:65, 0:512],
                            ACT.Copy)
            dcp_s = pl.smark()
            pl.wait("tensor", "sch", dcp_s)
            pl.t.matmul(C.pb[2][:, 0:512], C.onesrow_f[:],
                        C.TROWS[0:1, 1536:2048], start=True, stop=True)
            tv2 = pl.tmark()
            pl.wait("scalar", "tms", tv2)
            pl.s.activation(RECB, C.pb[2][:, 0:512], ACT.Ln)
            pl.s.activation(RECB, RECB, ACT.Exp, scale=-1.0)
            rec_s = pl.smark()
            pl.wait("vector", "sch", rec_s)
            att = ATT[(h % 2) * 64:(h % 2) * 64 + 64,
                      (h // 2) * 512:(h // 2) * 512 + 512]
            pl.v.tensor_tensor(att, pvb[0:64, 0:512], BCB[0:64, 0:512],
                               ALU.mult)
            if ck == 0:
                pbase = (h % 2) * 64
                a0 = ATT[pbase:pbase + 64,
                         (h // 2) * 512:(h // 2) * 512 + 128]
                pl.v.tensor_tensor(a0, a0, BL1[pbase:pbase + 64, :], ALU.mult)
                pl.v.scalar_tensor_tensor(a0, BLM[pbase:pbase + 64, :],
                                          C.MEANV[pbase:pbase + 64, g:g + 1],
                                          a0, ALU.mult, ALU.add)
            pv_evac[h % 2] = pl.vmark()

    pl.wait("tensor", "vch", max(pv_evac.values()))
    pl.wait("tensor", "sch", pl.smark())
    if ck == 1:
        pl.wait("vector", "r2w", C.X1D_DONE)
        pl.wait("sync", "vch", C.xres_free[0])
    wo_free = {3: None, 4: None}
    r1_vs = []
    last_stat_tv = None
    for mt in range(16):
        wob = 3 + mt % 2
        xsem = "xr%d" % (mt % 2)
        if mt >= 2:
            pl.wait("sync", "vch", r1_vs[mt - 2])
        pl.dma("sync", XRES[:, (mt % 2) * 512:(mt % 2 + 1) * 512],
               C.xq[mt * 128:(mt + 1) * 128, ck * 512:(ck + 1) * 512], xsem)
        if wo_free[wob] is not None:
            pl.wait("tensor", "vch", wo_free[wob])
        for kt in range(8):
            pl.t.matmul(C.pb[wob][:, 0:512],
                        WOO[:, kt * 2048 + mt * 128:kt * 2048 + (mt + 1) * 128],
                        ATT[:, kt * 512:(kt + 1) * 512],
                        start=(kt == 0), stop=(kt == 7))
        tv = pl.tmark()
        pl.wait("vector", "tms", tv)
        pl.wait("vector", xsem, pl.cnt[xsem])
        if last_stat_tv is not None:
            pl.wait("vector", "tms", last_stat_tv)
        x1t = X1C[:, mt * 512:(mt + 1) * 512]
        pl.v.tensor_tensor(x1t, C.pb[wob][:, 0:512],
                           XRES[:, (mt % 2) * 512:(mt % 2 + 1) * 512],
                           ALU.add, nochain=True)
        pl.v.tensor_copy(R1B[:], x1t)
        pl.v.tensor_tensor(SQT[:], R1B[:], R1B[:], ALU.mult)
        wo_free[wob] = pl.vmark()
        r1_vs.append(wo_free[wob])
        pl.wait("tensor", "vch", wo_free[wob])
        pl.t.matmul(C.pst[0:1, 0:512], C.onescol_b[:], R1B[:],
                    start=(mt == 0), stop=(mt == 15))
        pl.t.matmul(C.pb[5][0:1, 0:512], C.onescol_b[:], SQT[:],
                    start=(mt == 0), stop=(mt == 15))
        last_stat_tv = pl.tmark()

    pl.wait("vector", "tms", last_stat_tv)
    mu = C.TROWS[0:1, 0:512]
    rs = C.TROWS[0:1, 512:1024]
    nb = C.TROWS[0:1, 1024:1536]
    tm = C.TROWS[0:1, 1536:2048]
    pl.v.tensor_scalar(mu, C.pst[0:1, 0:512], 1.0 / 2048.0, None, ALU.mult)
    pl.v.tensor_scalar(tm, C.pb[5][0:1, 0:512], 1.0 / 2048.0, None, ALU.mult,
                       nochain=True)
    pl.v.tensor_tensor(nb, mu, mu, ALU.mult)
    pl.v.tensor_tensor(tm, tm, nb, ALU.subtract)
    pl.v.tensor_scalar(tm, tm, EPS, None, ALU.add)
    pl.wait("scalar", "vch", pl.vmark())
    pl.s.activation(rs, tm, ACT.Sqrt)
    pl.wait("vector", "sch", pl.smark())
    pl.v.reciprocal(rs, rs)
    pl.v.tensor_tensor(nb, rs, rs, ALU.mult)
    pl.v.tensor_tensor(nb, nb, tm, ALU.mult)
    pl.v.tensor_scalar(nb, nb, -0.5, 1.5, ALU.mult, ALU.add)
    pl.v.tensor_tensor(rs, rs, nb, ALU.mult)
    pl.v.tensor_tensor(nb, mu, rs, ALU.mult)
    pl.v.tensor_scalar(nb, nb, -1.0, None, ALU.mult)
    rs_v = pl.vmark()
    pl.wait("tensor", "vch", rs_v)
    pl.t.matmul(C.pb[2][:, 0:512], C.onesrow_f[:], C.TROWS[0:1, 512:1024],
                start=True, stop=True)
    pl.t.matmul(C.pb[5][:, 0:512], C.onesrow_f[:], C.TROWS[0:1, 1024:1536],
                start=True, stop=True)
    tv = pl.tmark()
    pl.wait("vector", "tms", tv)
    pl.v.tensor_copy(RSB, C.pb[2][:, 0:512])
    pl.v.tensor_copy(NBB, C.pb[5][:, 0:512], nochain=True)
    for mt in range(16):
        x1t = X1C[:, mt * 512:(mt + 1) * 512]
        pl.v.tensor_tensor(x1t, x1t, RSB, ALU.mult, nochain=(mt > 0))
        pl.v.tensor_tensor(x1t, x1t, NBB, ALU.add)
    C.X1_V = pl.vmark()
    pl.wait("gpsimd", "vch", C.X1_V)
    pl.dma("gpsimd", C.x1d[:, ck * CHT:(ck + 1) * CHT]
           .rearrange("(k p) c -> p k c", p=128),
           X1C[:].rearrange("p (k c) -> p k c", k=16), "r2w")
    C.X1D_DONE = pl.cnt["r2w"]
    if not hasattr(C, "x1d_done"):
        C.x1d_done = {}
    C.x1d_done[ck] = C.X1D_DONE

    yield

    aslots = (XRES[:, 0:512], QS1[:])
    axm = XRES[:, 512:1024]
    pl.wait("scalar", "xr0", pl.cnt["xr0"])
    pl.wait("scalar", "xr1", pl.cnt["xr1"])
    pl.wait("scalar", "vch", C.X1_V)
    amx_vs = []
    for mt in range(16):
        x1t = X1C[:, mt * 512:(mt + 1) * 512]
        aslot = aslots[mt % 2]
        if mt >= 2:
            pl.wait("scalar", "vch", amx_vs[mt - 2])
        pl.s.activation(aslot, x1t, ACT.Abs)
        pl.wait("vector", "sch", pl.smark())
        if mt == 0:
            pl.v.tensor_copy(axm, aslot)
        else:
            pl.v.tensor_tensor(axm, axm, aslot, ALU.max)
        amx_vs.append(pl.vmark())
    mu = C.TROWS[0:1, 0:512]
    tm = C.TROWS[0:1, 1536:2048]
    _amax_row(C, aslots[0], M2A, axm, tm)
    C.xres_free[ck] = pl.vmark()
    pl.v.tensor_scalar(tm, tm, EPS, None, ALU.max)
    pl.v.reciprocal(mu, tm)
    pl.v.tensor_scalar(mu, mu, 128.0, None, ALU.mult)
    pl.v.tensor_scalar(tm, tm, 1.0 / 128.0, None, ALU.mult)
    qsc_v = pl.vmark()
    pl.wait("tensor", "vch", qsc_v)
    pl.t.matmul(C.pb[2][:, 0:512], C.onesrow_f[:], mu, start=True, stop=True)
    pl.t.matmul(C.pb[5][:, 0:512], C.onesrow_f[:], tm, start=True, stop=True)
    tv = pl.tmark()
    pl.wait("vector", "tms", tv)
    pl.v.tensor_copy(QS1[:], C.pb[2][:, 0:512])
    pl.v.tensor_copy(QS2[:], C.pb[5][:, 0:512], nochain=True)
    C.qs_v[ck] = pl.vmark()
    xqw_mks = []
    for mt in range(16):
        x1t = X1C[:, mt * 512:(mt + 1) * 512]
        sl = AQB[:, (mt % 2) * 512:(mt % 2 + 1) * 512]
        if mt >= 2:
            pl.wait("vector", "xqw", xqw_mks[mt - 2])
        pl.v.tensor_tensor(ATI8[:], x1t, QS1[:], ALU.mult)
        pl.v.tensor_tensor(sl, ATI8[:], QS2[:], ALU.mult)
        mk = pl.vmark()
        pl.wait("gpsimd", "vch", mk)
        pl.dma("gpsimd", C.xqd[ck * D + mt * 128:ck * D + (mt + 1) * 128, :],
               sl, "xqw")
        xqw_mks.append(pl.cnt["xqw"])
    C.aq_v[ck] = pl.vmark()
    C.xqw_done[ck] = pl.cnt["xqw"]

    if C.debug:
        pl.wait("gpsimd", "vch", C.X1_V)
        pl.dma("gpsimd", C.dbg["d_att"][:, ck * 8 * 512:(ck + 1) * 8 * 512],
               ATT[:], "outw")


def _emit_ffn_all(C, e):
    nc, pl = C.nc, C.pl
    XQ = e(nc.sbuf_tensor("XQ", [128, 16 * 512], BF16))
    Y = e(nc.sbuf_tensor("Y", [128, M_I * 512], BF16))
    X1T = e(nc.sbuf_tensor("X1T", [128, 2 * 512], F32))
    WG = e(nc.sbuf_tensor("WG", [128, 2 * 2048], BF16))
    WU = e(nc.sbuf_tensor("WU", [128, 2 * 2048], BF16))
    WD = e(nc.sbuf_tensor("WD", [128, 4 * 4096], BF16))
    T4 = e(nc.sbuf_tensor("T4", [128, 4 * 512], F32))
    T4P = e(nc.sbuf_tensor("T4P", [128, 4 * 512], F32))
    TI8P = e(nc.sbuf_tensor("TI8P", [128, 512], I8))
    SIG = e(nc.sbuf_tensor("SIG", [128, 2 * 512], BF16))
    AM = e(nc.sbuf_tensor("AM", [128, 512], F32))
    AMT = e(nc.sbuf_tensor("AMT", [128, 512], F32))
    M2 = e(nc.sbuf_tensor("M2", [32, 16], F32))
    SCB = e(nc.sbuf_tensor("SCB", [128, 512], F32))
    BCB2 = e(nc.sbuf_tensor("BCB2", [128, 2 * 512], F32))
    TQ = e(nc.sbuf_tensor("TQ", [128, 512], F32))
    TI8 = e(nc.sbuf_tensor("TI8", [128, 512], I8))
    RT = e(nc.sbuf_tensor("RT", [128, 2 * 512], F32))
    OUTT = e(nc.sbuf_tensor("OUTT", [128, 2 * 512], F32))
    R2B2 = e(nc.sbuf_tensor("R2B2", [128, 512], BF16))
    SQ2 = e(nc.sbuf_tensor("SQ2", [128, 512], BF16))
    bufs = SimpleNamespace(**{k: v for k, v in locals().items()
                              if k not in ("C", "nc", "pl", "e")})
    pl.wait("sync", "agc", C.AGDONE)
    for ck in range(2):
        _emit_ffn_chunk(C, ck, bufs)


def _amax_row(C, amt, m2, src, dstrow):
    pl = C.pl
    pl.wait("vector", "pch", pl.pmark())
    for width in (64, 32):
        pl.v.tensor_copy(amt[0:width, :], src[width:2 * width, :])
        pl.v.tensor_tensor(src[0:width, :], src[0:width, :],
                           amt[0:width, :], ALU.max)
    pl.v.transpose(amt[0:32, :], src[0:32, :])
    pl.v.reduce_max(m2[:],
                    amt[0:32, :].rearrange("p (j c) -> p j c", c=32), AX.X)
    mv = pl.vmark()
    pl.wait("sync", "vch", mv)
    pl.dma("sync", C.qbounce, m2[:], "qbn")
    pl.wait("sync", "qbn", pl.cnt["qbn"])
    pl.dma("sync", dstrow.rearrange("a (j r) -> a j r", r=32),
           C.qbounce.transpose([1, 0]).unsqueeze(0), "qbn",
           allow_slow_non_contiguous=True)
    pl.wait("vector", "qbn", pl.cnt["qbn"])


def _emit_ffn_chunk(C, ck, b):
    nc, pl = C.nc, C.pl
    rows = C.rows
    if ck == 0:
        for eng in ENGS:
            for snm in ("r2w", "xr0", "xr1", "boun", "ldwo",
                        "ldm", "ldxq", "ldwA", "outw"):
                pl.wait(eng, snm, pl.cnt[snm])
    if ck == 1:
        pl.wait("sync", "tms", pl.cnt["tms"])
        pl.wait("sync", "r2r0", pl.cnt["r2r0"])
        pl.wait("sync", "r2r1", pl.cnt["r2r1"])
        pl.wait("gpsimd", "r2r0", pl.cnt["r2r0"])
        pl.wait("gpsimd", "r2r1", pl.cnt["r2r1"])
        pl.wait("vector", "outw", pl.cnt["outw"])
        pl.wait("vector", "r2r0", pl.cnt["r2r0"])
        pl.wait("vector", "r2r1", pl.cnt["r2r1"])
        pl.wait("vector", "pch", pl.cnt["pch"])
        pl.wait("scalar", "pch", pl.cnt["pch"])
    mu = C.TROWS[0:1, 0:512]
    rs = C.TROWS[0:1, 512:1024]
    nb = C.TROWS[0:1, 1024:1536]
    tm = C.TROWS[0:1, 1536:2048]

    def gate_slab(m):
        r, ml = m // 8, m % 8
        return C.scr_all[r, ml * GSL:(ml + 1) * GSL].rearrange(
            "(p n) -> p n", p=128)

    def up_slab(m):
        r, ml = m // 8, m % 8
        return C.scr_all[r, OWN_G + ml * GSL:OWN_G + (ml + 1) * GSL].rearrange(
            "(p n) -> p n", p=128)

    def down_q(otp, q4):
        outs = []
        for rr in range(2):
            r = 2 * q4 + rr
            base_o = 2 * OWN_G + otp * (128 * 2048)
            reg = C.scr_all[r, base_o:base_o + 128 * 2048].rearrange(
                "(p n) -> p n", p=128)
            outs.append(reg)
        return outs

    x1cols = slice(ck * CHT, (ck + 1) * CHT)
    pl.wait("sync", "r2w", C.x1d_done[ck])
    pl.wait("sync", "xqw", C.xqw_done[ck])
    XQD = b.XQ
    for mt in range(16):
        pl.dma("sync", XQD[:, mt * 512:(mt + 1) * 512],
               C.xqd[ck * D + mt * 128:ck * D + (mt + 1) * 128, :], "ldxq")
    xq_ld = pl.cnt["ldxq"]

    wg_used = {}
    wu_used = {}
    y_v = {}
    sig_s = {}
    for m in range(M_I):
        par = m % 2
        for (wsem, wbuf, slab, used) in (("wg%d" % par, b.WG, gate_slab(m),
                                          wg_used),
                                         ("wu%d" % par, b.WU, up_slab(m),
                                          wu_used)):
            if par in used:
                pl.wait("sync", "tms", used[par])
            pl.dma("sync", wbuf[:, par * 2048:(par + 1) * 2048], slab, wsem)
        gb, ub = C.pb[0 + par], C.pb[2 + par]
        pl.wait("tensor", "wg%d" % par, pl.cnt["wg%d" % par])
        pl.wait("tensor", "wu%d" % par, pl.cnt["wu%d" % par])
        pl.wait("tensor", "ldxq", xq_ld)
        if par in y_v:
            pl.wait("tensor", "vch", y_v[par])
        for kt in range(16):
            pl.t.matmul(gb[:, 0:512],
                        b.WG[:, par * 2048 + kt * 128:par * 2048 + (kt + 1) * 128],
                        XQD[:, kt * 512:(kt + 1) * 512],
                        start=(kt == 0), stop=(kt == 15))
        gate_tv = pl.tmark()
        for kt in range(16):
            pl.t.matmul(ub[:, 0:512],
                        b.WU[:, par * 2048 + kt * 128:par * 2048 + (kt + 1) * 128],
                        XQD[:, kt * 512:(kt + 1) * 512],
                        start=(kt == 0), stop=(kt == 15))
        up_tv = pl.tmark()
        wg_used[par] = up_tv
        wu_used[par] = up_tv
        pl.wait("scalar", "tms", gate_tv)
        if par in sig_s:
            pl.wait("scalar", "vch", y_v[par])
        pl.s.activation(b.SIG[:, par * 512:(par + 1) * 512], gb[:, 0:512],
                        ACT.Silu, scale=C.WMC[:, 0:1])
        sig_s[par] = pl.smark()
        pl.wait("vector", "sch", sig_s[par])
        pl.wait("vector", "tms", up_tv)
        pl.v.scalar_tensor_tensor(b.Y[:, m * 512:(m + 1) * 512],
                                  ub[:, 0:512], C.WMC[:, 1:2],
                                  b.SIG[:, par * 512:(par + 1) * 512],
                                  ALU.mult, ALU.mult, nochain=True)
        y_v[par] = pl.vmark()

    yv = max(y_v.values())
    pl.wait("vector", "vch", yv)
    pl.wait("gpsimd", "vch", yv)

    def _quad(eng, m0, s4, t4):
        blk = [b.Y[:, (m0 + kk * s4) * 512:(m0 + kk * s4) * 512 + 512]
               for kk in range(4)]
        t1, t2 = t4[:, 0:512], t4[:, 512:1024]
        t3, tt = t4[:, 1024:1536], t4[:, 1536:2048]
        eng.tensor_tensor(t1, blk[0], blk[1], ALU.add)
        eng.tensor_tensor(t2, blk[0], blk[1], ALU.subtract, nochain=True)
        eng.tensor_tensor(t3, blk[2], blk[3], ALU.add, nochain=True)
        eng.tensor_tensor(tt, blk[2], blk[3], ALU.subtract, nochain=True)
        eng.tensor_tensor(blk[0], t1, t3, ALU.add)
        eng.tensor_tensor(blk[1], t2, tt, ALU.add, nochain=True)
        eng.tensor_tensor(blk[2], t1, t3, ALU.subtract, nochain=True)
        eng.tensor_tensor(blk[3], t2, tt, ALU.subtract, nochain=True)

    for s4 in (1, 4):
        for j in range(16):
            m0 = (j // s4) * 4 * s4 + (j % s4)
            if m0 < 48 or not POOL_BF:
                _quad(pl.v, m0, s4, b.T4)
            else:
                _quad(pl.p, m0, s4, b.T4P)
    pl.wait("vector", "pch", pl.pmark())
    pl.wait("gpsimd", "vch", pl.vmark())
    for j in range(16):
        if j < 11 or not POOL_BF:
            _quad(pl.v, j, 16, b.T4)
        else:
            _quad(pl.p, j, 16, b.T4P)
    bf_v = pl.vmark()
    bf_p = pl.pmark()

    pl.wait("tensor", "vch", bf_v)
    pl.wait("tensor", "pch", bf_p)
    pl.wait("scalar", "vch", bf_v)
    pl.wait("scalar", "pch", bf_p)
    abs_done = {}
    mx_vs = []
    for m in range(M_I):
        par = m % 2
        hb = C.pb[4 + par]
        if par in abs_done:
            pl.wait("tensor", "sch", abs_done[par])
        pl.t.matmul(hb[:, 0:512], C.sh128[:], b.Y[:, m * 512:(m + 1) * 512],
                    start=True, stop=True)
        tv = pl.tmark()
        pl.wait("scalar", "tms", tv)
        pl.s.activation(b.Y[:, m * 512:(m + 1) * 512], hb[:, 0:512], ACT.Copy)
        if m >= 2:
            pl.wait("scalar", "vch", mx_vs[m - 2])
        dst = b.AMT if par == 0 else b.TQ
        pl.s.activation(dst[:], hb[:, 0:512], ACT.Abs)
        abs_done[par] = pl.smark()
        pl.wait("vector", "sch", abs_done[par])
        if m == 0:
            pl.v.tensor_copy(b.AM[:], dst[:])
        else:
            pl.v.tensor_tensor(b.AM[:], b.AM[:], dst[:], ALU.max)
        mx_vs.append(pl.vmark())
    _amax_row(C, b.AMT, b.M2, b.AM, tm)
    pl.v.tensor_scalar(tm, tm, C8192, EPS, ALU.mult, ALU.max)
    pl.v.reciprocal(mu, tm)
    pl.v.tensor_scalar(mu, mu, 128.0, None, ALU.mult)
    pl.v.tensor_scalar(mu, mu, C8192, None, ALU.mult)
    pl.v.tensor_scalar(tm, tm, 1.0 / 128.0, None, ALU.mult)
    pl.v.tensor_single_scalar(tm, tm, rows[0:1, R_WS + 16:R_WS + 17],
                              ALU.mult)
    sc2_v = pl.vmark()
    pl.wait("tensor", "vch", sc2_v)
    pl.t.matmul(C.pb[4][:, 0:512], C.onesrow_f[:], mu, start=True, stop=True)
    pl.t.matmul(C.pb[5][:, 0:512], C.onesrow_f[:], tm, start=True, stop=True)
    tv = pl.tmark()
    pl.wait("vector", "tms", tv)
    pl.v.tensor_copy(b.SCB[:], C.pb[4][:, 0:512])
    pl.v.tensor_copy(b.BCB2[:, 512:1024], C.pb[5][:, 0:512], nochain=True)
    sc2b_v = pl.vmark()

    nv = 56 if POOL_XI else M_I
    if POOL_XI:
        pl.wait("gpsimd", "vch", sc2b_v)
        for m in range(56, M_I):
            pl.p.tensor_tensor(b.T4P[:, 0:512], b.Y[:, m * 512:(m + 1) * 512],
                               b.SCB[:], ALU.mult)
            pl.p.tensor_copy(b.TI8P[:], b.T4P[:, 0:512])
            pl.p.tensor_copy(b.Y[:, m * 512:(m + 1) * 512], b.TI8P[:])
    xi2_p = pl.pmark()
    for m in range(nv):
        pl.v.tensor_tensor(b.TI8[:], b.Y[:, m * 512:(m + 1) * 512], b.SCB[:],
                           ALU.mult)
        pl.v.tensor_copy(b.Y[:, m * 512:(m + 1) * 512], b.TI8[:])
    xi2_v = pl.vmark()

    pl.wait("tensor", "vch", xi2_v)
    pl.wait("tensor", "pch", xi2_p)
    dn_done = {}
    r2_vs = []
    sq_ss = []
    stat_tv = None
    dn_q_tv = {}
    r2w_vs = []
    for otp in range(8):
        ppar = otp % 2
        pa, pb_ = C.pb[0 + 2 * ppar], C.pb[1 + 2 * ppar]
        for q4 in range(4):
            wsem = "wd%d" % q4
            prevq = (otp - 1, q4)
            if prevq in dn_q_tv:
                pl.wait("sync", "tms", dn_q_tv[prevq])
            srcs = down_q(otp, q4)
            for rr in range(2):
                pl.dma("sync", b.WD[:, q4 * 4096 + rr * 2048:
                                    q4 * 4096 + (rr + 1) * 2048],
                       srcs[rr], wsem)
            pl.wait("tensor", wsem, pl.cnt[wsem])
            if otp >= 2 and q4 == 0 and ppar in dn_done:
                pl.wait("tensor", "vch", dn_done[ppar])
            for j in range(16):
                kt = q4 * 16 + j
                base = q4 * 4096 + j * 256
                pl.t.matmul(pa[:, 0:512], b.WD[:, base:base + 128],
                            b.Y[:, kt * 512:(kt + 1) * 512],
                            start=(kt == 0), stop=(kt == 63))
                pl.t.matmul(pb_[:, 0:512], b.WD[:, base + 128:base + 256],
                            b.Y[:, kt * 512:(kt + 1) * 512],
                            start=(kt == 0), stop=(kt == 63))
            dn_q_tv[(otp, q4)] = pl.tmark()
        dn_tv = dn_q_tv[(otp, 3)]
        for oo in range(2):
            ot = 2 * otp + oo
            osl = ot % 2
            xs = "x1r%d" % osl
            rsl = b.RT[:, osl * 512:(osl + 1) * 512]
            if len(r2_vs) >= 2:
                pl.wait("sync", "vch", r2_vs[-2])
            pl.dma("sync", b.X1T[:, osl * 512:(osl + 1) * 512],
                   C.x1d[ot * 128:(ot + 1) * 128, x1cols], xs)
            pl.wait("vector", "tms", dn_tv)
            pl.wait("vector", xs, pl.cnt[xs])
            if ot >= 2:
                pl.wait("vector", "r2w", r2w_vs[ot - 2])
                pl.wait("vector", "sch", sq_ss[ot - 2])
            psrc = pa if oo == 0 else pb_
            pl.v.tensor_tensor(rsl, psrc[:, 0:512],
                               b.BCB2[:, 512:1024], ALU.mult, nochain=True)
            pl.v.tensor_tensor(rsl, rsl,
                               b.X1T[:, osl * 512:(osl + 1) * 512], ALU.add)
            r2_vs.append(pl.vmark())
            pl.wait("gpsimd", "vch", r2_vs[-1])
            pl.dma("gpsimd", C.r2b[ot * 128:(ot + 1) * 128, :], rsl, "r2w")
            r2w_vs.append(pl.cnt["r2w"])
            pl.wait("scalar", "vch", r2_vs[-1])
            if stat_tv is not None:
                pl.wait("scalar", "tms", stat_tv)
            pl.s.activation(b.R2B2[:], rsl, ACT.Copy)
            pl.s.activation(b.SQ2[:], b.R2B2[:], ACT.Square)
            sq_ss.append(pl.smark())
            pl.wait("tensor", "sch", sq_ss[-1])
            pl.t.matmul(C.pst[0:1, 0:512], C.onescol_b[:], b.R2B2[:],
                        start=(ot == 0), stop=(ot == 15))
            pl.t.matmul(C.pb[5][0:1, 0:512], C.onescol_b[:], b.SQ2[:],
                        start=(ot == 0), stop=(ot == 15))
            stat_tv = pl.tmark()
        dn_done[ppar] = r2_vs[-1]

    pl.wait("vector", "tms", stat_tv)
    pl.v.tensor_scalar(mu, C.pst[0:1, 0:512], 1.0 / 2048.0, None, ALU.mult)
    pl.v.tensor_scalar(tm, C.pb[5][0:1, 0:512], 1.0 / 2048.0, None, ALU.mult,
                       nochain=True)
    pl.v.tensor_tensor(nb, mu, mu, ALU.mult)
    pl.v.tensor_tensor(tm, tm, nb, ALU.subtract)
    pl.v.tensor_scalar(tm, tm, EPS, None, ALU.add)
    pl.wait("scalar", "vch", pl.vmark())
    pl.s.activation(rs, tm, ACT.Sqrt)
    pl.wait("vector", "sch", pl.smark())
    pl.v.reciprocal(rs, rs)
    pl.v.tensor_tensor(nb, rs, rs, ALU.mult)
    pl.v.tensor_tensor(nb, nb, tm, ALU.mult)
    pl.v.tensor_scalar(nb, nb, -0.5, 1.5, ALU.mult, ALU.add)
    pl.v.tensor_tensor(rs, rs, nb, ALU.mult)
    pl.v.tensor_tensor(nb, mu, rs, ALU.mult)
    pl.v.tensor_scalar(nb, nb, -1.0, None, ALU.mult)
    rs2_v = pl.vmark()
    pl.wait("tensor", "vch", rs2_v)
    pl.t.matmul(C.pb[4][:, 0:512], C.onesrow_f[:], rs, start=True, stop=True)
    pl.t.matmul(C.pb[5][:, 0:512], C.onesrow_f[:], nb, start=True, stop=True)
    tv = pl.tmark()
    pl.wait("vector", "tms", tv)
    pl.v.tensor_copy(b.SCB[:], C.pb[4][:, 0:512])
    pl.v.tensor_copy(b.BCB2[:, 0:512], C.pb[5][:, 0:512], nochain=True)
    ln2_v = pl.vmark()

    pl.wait("sync", "r2w", pl.cnt["r2w"])
    o_vs, ow_vs = [], []
    for mt in range(16):
        rsem = "r2r%d" % (mt % 2)
        if mt >= 2:
            pl.wait("sync", "vch", o_vs[mt - 2])
        pl.dma("sync", b.RT[:, (mt % 2) * 512:(mt % 2 + 1) * 512],
               C.r2b[mt * 128:(mt + 1) * 128, :], rsem)
        pl.wait("vector", rsem, pl.cnt[rsem])
        pl.wait("vector", "vch", ln2_v)
        ot = b.OUTT[:, (mt % 2) * 512:(mt % 2 + 1) * 512]
        if mt >= 2:
            pl.wait("vector", "outw", ow_vs[mt - 2])
        pl.v.tensor_tensor(ot, b.RT[:, (mt % 2) * 512:(mt % 2 + 1) * 512],
                           b.SCB[:], ALU.mult, nochain=True)
        pl.v.tensor_tensor(ot, ot, b.BCB2[:, 0:512], ALU.add)
        o_vs.append(pl.vmark())
        pl.wait("gpsimd", "vch", o_vs[-1])
        pl.dma("gpsimd", C.out[mt * 128:(mt + 1) * 128, x1cols], ot, "outw")
        ow_vs.append(pl.cnt["outw"])
    if C.debug and ck == DBG_FFN_CK:
        pl.wait("gpsimd", "vch", xi2_v)
        pl.wait("gpsimd", "pch", xi2_p)
        pl.dma("gpsimd", C.dbg["d_y"], b.Y[:], "outw")
        pl.dma("gpsimd", C.dbg["d_xq"], b.XQ[:], "outw")


def _stage_inputs(inputs):
    x = np.asarray(inputs["L0_attn_x"], np.float32)
    consts = _host_consts()
    kch, kcl, ksh, ksl = _rope_tables(np.arange(T))
    masks = {p: _masks(p) for p in range(2)}
    qtabs = {p: _rope_tables(_own_positions(p)) for p in range(2)}
    in_maps = []
    for c in range(NCN):
        b, p = c // 2, c % 2
        xT = np.ascontiguousarray(x[b].T)
        xq = np.ascontiguousarray(xT[:, _own_positions(p)])
        smaskp, tmaskp, smaskl, tmaskl, blend = masks[p]
        qch, qcl, qsh, qsl = qtabs[p]
        in_maps.append(dict(
            xkv=xT, xq=xq,
            wqsem=np.ascontiguousarray(inputs["L0_attn_Wq_sem"], np.float32),
            wqgeo=np.ascontiguousarray(inputs["L0_attn_Wq_geo"], np.float32),
            wksem=np.ascontiguousarray(inputs["L0_attn_Wk_sem"], np.float32),
            wkgeo=np.ascontiguousarray(inputs["L0_attn_Wk_geo"], np.float32),
            wv=np.ascontiguousarray(inputs["L0_attn_Wv"], np.float32),
            wo=np.ascontiguousarray(inputs["L0_attn_Wo"], np.float32),
            lsc=np.ascontiguousarray(inputs["L0_attn_logit_scale"],
                                     np.float32).reshape(1, H),
            wgch=np.ascontiguousarray(
                inputs["L0_ffn_w_gate"][c * 1024:(c + 1) * 1024, :],
                np.float32),
            wuch=np.ascontiguousarray(
                inputs["L0_ffn_w_up"][c * 1024:(c + 1) * 1024, :], np.float32),
            wdch=np.ascontiguousarray(
                inputs["L0_ffn_w_down"][:, c * 1024:(c + 1) * 1024],
                np.float32),
            ident=consts["ident"], identf=consts["identf"],
            h128=consts["h128"],
            permrope=consts["permrope"], poolm=consts["poolm"],
            kcos_h=kch, kcos_l=kcl, ksin_h=ksh, ksin_l=ksl,
            qcos_h=qch, qcos_l=qcl, qsin_h=qsh, qsin_l=qsl,
            smaskp=smaskp, tmaskp=tmaskp, smaskl=smaskl, tmaskl=tmaskl,
            blendrow=blend,
        ))
    return in_maps


def kernel(**inputs):
    in_maps = _stage_inputs(inputs)
    nc = build_nc(debug=DEBUG, part=PART)
    res = run_bass_kernel_spmd(nc, in_maps, core_ids=list(range(NCN)))
    out = np.empty((B, T, D), np.float32)
    for c in range(NCN):
        b, p = c // 2, c % 2
        sh = res.results[c]["out"]
        for s in range(NSLOT):
            t = 2 * s + p
            out[b, t * 128:(t + 1) * 128, :] = sh[:, s * 128:(s + 1) * 128].T
    if DEBUG:
        kernel.last_results = res
    return out


# version 16
# speedup vs baseline: 1.0231x; 1.0165x over previous
import numpy as np
import concourse.bass as bass
import concourse.mybir as mybir
from concourse.bass_utils import run_bass_kernel_spmd
from contextlib import ExitStack
from types import SimpleNamespace

F32 = mybir.dt.float32
BF16 = mybir.dt.bfloat16
I8 = mybir.dt.int8
ALU = mybir.AluOpType
ACT = mybir.ActivationFunctionType
AX = mybir.AxisListType

B, T, D = 4, 2048, 2048
H, HKV, DS, DG, DV = 16, 4, 32, 32, 64
MB, LW = 64, 1024
NBLK, NK = 16, 1040
NEG = -1.0e9
NCN = 8
TOWN, CHT, NSLOT = 1024, 512, 8
IFF, M_I, KT_D = 8192, 64, 16
EPS = float(np.float32(1e-5))
C8192 = float(np.float32(1.0 / np.sqrt(8192.0)))
SLOT_LOC = [0, 0, 0, 0, 2, 4, 6, 8]
GSL = 128 * 2048
OWN_G = 8 * GSL
OWN_D = 8 * 128 * 8 * 256
OWN_N = 2 * OWN_G + OWN_D

DEBUG = False
PART = 3
DBG_FFN_CK = 0
STAGE = 99
SUBST = 99
POOL_BF = True
POOL_XI = True
ATTN_IL = True


def _bf16():
    import ml_dtypes
    return ml_dtypes.bfloat16


def _host_consts():
    bf16 = _bf16()
    c = {}
    c["ident"] = np.eye(128, dtype=bf16)
    c["identf"] = np.eye(128, dtype=np.float32)
    i = np.arange(128)
    a = i[:, None] & i[None, :]
    pop = np.zeros((128, 128), np.int64)
    for bb in range(7):
        pop += (a >> bb) & 1
    c["h128"] = np.where(pop % 2 == 0, 1.0, -1.0).astype(bf16)
    pm = np.zeros((128, 128), np.float32)
    for hh in range(4):
        for f in range(16):
            pm[hh * 32 + 16 + f, hh * 32 + f] = 1.0
            pm[hh * 32 + f, hh * 32 + 16 + f] = 1.0
    c["permrope"] = pm.astype(bf16)
    poolm = np.zeros((128, 128), np.float32)
    for tt in range(8):
        poolm[0:64, tt * 16 + 2 * tt] = 1.0 / 64.0
        poolm[64:128, tt * 16 + 2 * tt + 1] = 1.0 / 64.0
    c["poolm"] = poolm.astype(bf16)
    return c


def _rope_tables(pos):
    d = 32
    inv = 1.0 / (10000.0 ** (np.arange(0, d, 2, dtype=np.float64) / d))
    ang = pos.astype(np.float64)[None, :] * inv[:, None]
    cos64, sin64 = np.cos(ang), np.sin(ang)
    cf = np.tile(np.concatenate([cos64, cos64], 0), (4, 1))
    sf = np.tile(np.concatenate([-sin64, sin64], 0), (4, 1))

    def hilo(x):
        hi = x.astype(np.float32)
        return hi, (x - hi.astype(np.float64)).astype(np.float32)
    ch, cl = hilo(cf)
    sh, sl = hilo(sf)
    return ch, cl, sh, sl


def _own_positions(p):
    out = np.empty(TOWN, np.int64)
    for s in range(NSLOT):
        t = 2 * s + p
        out[s * 128:(s + 1) * 128] = np.arange(t * 128, (t + 1) * 128)
    return out


def _masks(p):
    bf16 = _bf16()
    kpp = np.arange(NBLK) * MB + (MB - 1)
    smaskp = np.zeros((128, 4 * 16), np.float32)
    tmaskp = np.zeros((16, 4 * 128), np.float32)
    for s in range(4):
        t = 2 * s + p
        q = np.arange(t * 128, (t + 1) * 128)
        vis = kpp[None, :] <= q[:, None]
        smaskp[:, s * 16:(s + 1) * 16] = np.where(vis, 0.0, NEG)
        tmaskp[:, s * 128:(s + 1) * 128] = np.where(vis.T, 0.0, NEG)
    smaskl = np.zeros((128, 4 * 2 * 128), np.float32)
    tmaskl = np.zeros((128, 4 * 2 * 128), np.float32)
    for js, s in enumerate(range(4, 8)):
        t = 2 * s + p
        q = np.arange(t * 128, (t + 1) * 128)
        for w in range(2):
            kt = SLOT_LOC[s] - 2 + w
            kidx = LW + np.arange(kt * 128, (kt + 1) * 128)
            vis = kidx[None, :] <= q[:, None]
            sl = slice((js * 2 + w) * 128, (js * 2 + w + 1) * 128)
            smaskl[:, sl] = np.where(vis, 0.0, NEG)
            tmaskl[:, sl] = np.where(vis.T, 0.0, NEG)
    blend = np.zeros((1, 128), np.float32)
    if p == 0:
        blend[0, :63] = 1.0
    return (smaskp.astype(bf16), tmaskp.astype(bf16), smaskl.astype(bf16),
            tmaskl.astype(bf16), blend)


ENGS = ("sync", "vector", "scalar", "tensor", "gpsimd")

DATA_OPS = {
    "tensor_tensor", "tensor_add", "tensor_sub", "tensor_mul", "tensor_max",
    "tensor_scalar", "tensor_single_scalar", "tensor_scalar_add",
    "tensor_scalar_sub", "tensor_scalar_mul", "tensor_scalar_max",
    "tensor_scalar_min", "tensor_copy", "tensor_reduce", "reduce_max",
    "reduce_sum", "scalar_tensor_tensor", "tensor_tensor_scan", "memset",
    "memzero", "reciprocal", "select", "activation", "copy", "add", "mul",
    "sign", "sqrt", "square", "iota", "transpose",
}
TENSOR_OPS = {"matmul", "transpose", "ldweights"}


class POp:
    __slots__ = ("kind", "name", "args", "kwargs", "incs", "val")

    def __init__(self, kind, name=None, args=(), kwargs=None, val=None):
        self.kind, self.name = kind, name
        self.args, self.kwargs = args, (kwargs or {})
        self.incs, self.val = [], val


class EngProxy:
    def __init__(self, pl, eng, chain=None):
        self._pl, self._eng, self._chain = pl, eng, chain

    def __getattr__(self, name):
        pl, eng, chain = self._pl, self._eng, self._chain

        def emit(*a, **kw):
            nochain = kw.pop("nochain", False)
            op = POp("op", name, a, kw)
            if chain is not None and name in DATA_OPS:
                if pl.cnt[chain] > 0 and not nochain:
                    pl.ops[eng].append(POp("wait", chain, val=pl.cnt[chain]))
                op.incs.append((chain, 1))
                pl.cnt[chain] += 1
            elif eng == "tensor" and name in TENSOR_OPS:
                pl.tcnt += 1
                pl.last_top = op
            pl.ops[eng].append(op)
            return op
        return emit


class Pl:
    def __init__(self):
        self.ops = {e: [] for e in ENGS}
        self.cnt = {}
        self.h = {}
        self.tcnt = 0
        self.last_top = None
        self.v = EngProxy(self, "vector", "vch")
        self.s = EngProxy(self, "scalar", "sch")
        self.t = EngProxy(self, "tensor")
        self.g = EngProxy(self, "gpsimd")
        self.p = EngProxy(self, "gpsimd", "pch")
        self.q = EngProxy(self, "sync")

    def newsem(self, name, handle):
        self.h[name] = handle
        self.cnt[name] = 0

    def dma(self, eng, out, in_, sem, **kw):
        op = POp("op", "dma_start", (), {"out": out, "in_": in_, **kw})
        op.incs.append((sem, 16))
        self.cnt[sem] += 16
        self.ops[eng].append(op)
        return self.cnt[sem]

    def wait(self, eng, sem, val=None):
        v = self.cnt[sem] if val is None else val
        if v > 0:
            self.ops[eng].append(POp("wait", sem, val=v))
        return v

    def vmark(self):
        return self.cnt["vch"]

    def smark(self):
        return self.cnt["sch"]

    def pmark(self):
        return self.cnt["pch"]

    def tmark(self):
        assert self.last_top is not None
        for (snm, k) in self.last_top.incs:
            if snm == "tms":
                return self.cnt["tms"]
        self.last_top.incs.append(("tms", 1))
        self.cnt["tms"] += 1
        return self.cnt["tms"]

    def bump(self, eng, sem, k=1):
        op = POp("op", "nop", (), {})
        op.incs.append((sem, k))
        self.cnt[sem] += k
        self.ops[eng].append(op)
        return self.cnt[sem]

    def barrier(self, name):
        tv = self.tmark()
        vv = self.cnt["vch"]
        sv = self.cnt["sch"]
        pv = self.cnt["pch"]
        for eng in ("sync", "gpsimd"):
            self.bump(eng, name)
        bn = self.cnt[name]
        for eng in ENGS:
            self.wait(eng, "tms", tv)
            self.wait(eng, "vch", vv)
            self.wait(eng, "sch", sv)
            self.wait(eng, "pch", pv)
            self.wait(eng, name, bn)
        return bn

    def replay(self, block):
        for eng_name in ENGS:
            def _mk(name):
                def _f(eng):
                    for op in self.ops[name]:
                        if op.kind == "wait":
                            eng.wait_ge(self.h[op.name], op.val)
                        else:
                            instr = getattr(eng, op.name)(*op.args, **op.kwargs)
                            for (snm, k) in op.incs:
                                instr.then_inc(self.h[snm], k)
                return _f
            getattr(block, eng_name)(_mk(eng_name))


R_RS32, R_TMP, R_R1040 = 0, 2, 4
R_LS, R_CH = 32, 48
R_WS = 64


def build_nc(debug=False, part=None):
    if part is None:
        part = PART
    nc = bass.Bass("TRN2", target_bir_lowering=False)
    pl = Pl()
    C = SimpleNamespace(nc=nc, pl=pl, debug=debug)

    def din(name, shape, dt=F32):
        return nc.dram_tensor(name, shape, dt, kind="ExternalInput").ap()

    C.xkv = din("xkv", [D, T])
    C.xq = din("xq", [D, TOWN])
    C.wqsem = din("wqsem", [D, H * DS])
    C.wqgeo = din("wqgeo", [D, H * DG])
    C.wksem = din("wksem", [D, HKV * DS])
    C.wkgeo = din("wkgeo", [D, HKV * DG])
    C.wv = din("wv", [D, HKV * DV])
    C.wo = din("wo", [H * DV, D])
    C.lsc = din("lsc", [1, H])
    C.wgch = din("wgch", [TOWN, D])
    C.wuch = din("wuch", [TOWN, D])
    C.wdch = din("wdch", [D, TOWN])
    C.ident = din("ident", [128, 128], BF16)
    C.identf = din("identf", [128, 128])
    C.h128 = din("h128", [128, 128], BF16)
    C.permrope = din("permrope", [128, 128], BF16)
    C.poolm = din("poolm", [128, 128], BF16)
    C.kcos_h = din("kcos_h", [128, T]); C.kcos_l = din("kcos_l", [128, T])
    C.ksin_h = din("ksin_h", [128, T]); C.ksin_l = din("ksin_l", [128, T])
    C.qcos_h = din("qcos_h", [128, TOWN]); C.qcos_l = din("qcos_l", [128, TOWN])
    C.qsin_h = din("qsin_h", [128, TOWN]); C.qsin_l = din("qsin_l", [128, TOWN])
    C.smaskp = din("smaskp", [128, 64], BF16)
    C.tmaskp = din("tmaskp", [16, 512], BF16)
    C.smaskl = din("smaskl", [128, 1024], BF16)
    C.tmaskl = din("tmaskl", [128, 1024], BF16)
    C.blendrow = din("blendrow", [1, 128])

    C.out = nc.dram_tensor("out", [D, TOWN], F32, kind="ExternalOutput").ap()
    C.dbg = {}
    if debug:
        for nm, sh, dt in [
            ("d_kct", [128, 4 * 1152], BF16), ("d_qct", [128, H * TOWN], BF16),
            ("d_v2t", [128, 9 * 260], BF16), ("d_meanv", [128, 4], F32),
            ("d_att", [128, 2 * 8 * CHT], BF16),
            ("d_x1", [128, 2 * 16 * CHT], F32),
            ("d_rows", [1, 128], F32), ("d_scr", [1, 16384], BF16),
            ("d_y", [128, M_I * CHT], BF16), ("d_xq", [128, 16 * CHT], BF16),
        ]:
            C.dbg[nm] = nc.dram_tensor(nm, sh, dt, kind="ExternalOutput").ap()

    C.mbounce = nc.dram_tensor("mbounce", [64, 128], BF16).ap()
    C.ccs_in = nc.dram_tensor("ccs_in", [1, 4], F32).ap()
    C.ccs_out = nc.dram_tensor("ccs_out", [1, 4], F32).ap()
    C.scr_own = nc.dram_tensor("scr_own", [1, OWN_N], BF16).ap()
    C.scr_all = nc.dram_tensor("scr_all", [NCN, OWN_N], BF16,
                               addr_space="Shared").ap()
    C.r2b = nc.dram_tensor("r2b", [D, CHT], F32).ap()
    C.x1d = nc.dram_tensor("x1d", [D, TOWN], F32).ap()
    C.xqd = nc.dram_tensor("xqd", [2 * D, CHT], BF16).ap()
    C.qbounce = nc.dram_tensor("qbounce", [32, 16], F32).ap()

    with ExitStack() as est:
        e = est.enter_context
        for nm in ("vch", "sch", "tms", "pch",
                   "ldcst", "ldmsk", "ldwA", "ldxq", "ldm", "ldwo",
                   "pw0", "pw1", "pq0", "pq1", "arb", "sw0", "sw1",
                   "agc", "arc", "boun", "xr0", "xr1",
                   "wg0", "wg1", "wu0", "wu1", "wd0", "wd1", "wd2", "wd3",
                   "x1r0", "x1r1", "r2w", "r2r0", "r2r1", "outw", "qbn",
                   "xqw",
                   "bar1", "bar2"):
            pl.newsem(nm, e(nc.semaphore(nm)))

        C.pb = [e(nc.psum_tensor("pb%d" % i, [128, 512], F32)) for i in range(6)]
        C.pst = e(nc.psum_tensor("pst", [128, 512], F32))
        C.ptr = e(nc.psum_tensor("ptr", [128, 1024], BF16))

        C.sident = e(nc.sbuf_tensor("sident", [128, 128], BF16))
        C.sidentf = e(nc.sbuf_tensor("sidentf", [128, 128], F32))
        C.sh128 = e(nc.sbuf_tensor("sh128", [128, 128], BF16))
        C.sperm = e(nc.sbuf_tensor("sperm", [128, 128], BF16))
        C.spoolm = e(nc.sbuf_tensor("spoolm", [128, 128], BF16))
        C.onescol_b = e(nc.sbuf_tensor("onescol_b", [128, 1], BF16))
        C.onescol_f = e(nc.sbuf_tensor("onescol_f", [128, 1], F32))
        C.onesrow_f = e(nc.sbuf_tensor("onesrow_f", [1, 128], F32))
        C.rows = e(nc.sbuf_tensor("rows", [1, 128], F32))
        C.rcol = e(nc.sbuf_tensor("rcol", [128, 16], F32))
        C.MEANV = e(nc.sbuf_tensor("MEANV", [128, 4], F32))
        C.RCONST = e(nc.sbuf_tensor("RCONST", [128, 4], F32))
        C.CVALL = e(nc.sbuf_tensor("CVALL", [128, 16], F32))
        C.s_blend = e(nc.sbuf_tensor("s_blend", [1, 128], F32))
        C.TROWS = e(nc.sbuf_tensor("TROWS", [1, 2048], F32))
        C.PMROW = e(nc.sbuf_tensor("PMROW", [1, 2048], F32))
        C.aq_v = {}
        C.xqw_done = {}
        C.qs_v = {}
        C.xres_free = {}
        C.WMC = e(nc.sbuf_tensor("WMC", [128, 4], F32))

        _emit_base_loads(C)
        _emit_derived(C)

        with ExitStack() as estAB:
            eab = estAB.enter_context
            C.KCT = eab(nc.sbuf_tensor("KCT", [128, 4 * 1152], BF16))
            C.V2T = eab(nc.sbuf_tensor("V2T", [128, 9 * 260], BF16))
            C.QCT = eab(nc.sbuf_tensor("QCT", [128, H * TOWN], BF16))
            C.s_smaskp = eab(nc.sbuf_tensor("s_smaskp", [128, 64], BF16))
            C.s_tmaskp = eab(nc.sbuf_tensor("s_tmaskp", [16, 512], BF16))
            C.s_smaskl = eab(nc.sbuf_tensor("s_smaskl", [128, 1024], BF16))
            C.s_tmaskl = eab(nc.sbuf_tensor("s_tmaskl", [128, 1024], BF16))
            _emit_ab_loads(C)
            with ExitStack() as estA:
                if STAGE >= 4:
                    gA = _gen_phaseA(C, estA.enter_context)
                    gP = _gen_prepass(C, estA.enter_context)

                    def pump(g, n):
                        for _ in range(n):
                            try:
                                next(g)
                            except StopIteration:
                                return False
                        return True

                    pump(gA, 1)
                    for _ in range(4):
                        pump(gA, 1)
                        pump(gP, 2)
                    pump(gP, 2)
                    pump(gA, 1)
                    while True:
                        alive = pump(gA, 1)
                        pump(gP, 2)
                        if not alive:
                            break
                    while pump(gP, 8):
                        pass
                elif STAGE >= 1:
                    for _ in _gen_phaseA(C, estA.enter_context):
                        pass
                if debug:
                    _emit_dbgA(C)
            if part >= 2:
                pl.barrier("bar1")
                with ExitStack() as estB:
                    _emit_attn_all(C, estB.enter_context)
        if part >= 3:
            pl.barrier("bar2")
            with ExitStack() as estF:
                _emit_ffn_all(C, estF.enter_context)
        pl.wait("gpsimd", "outw")
        pl.wait("gpsimd", "r2w")

        block = e(nc.Block())
        pl.replay(block)
    return nc


def _emit_base_loads(C):
    pl = C.pl
    g = lambda t, s: pl.dma("gpsimd", t, s, "ldcst")
    g(C.sident[:], C.ident)
    g(C.sidentf[:], C.identf)
    g(C.sh128[:], C.h128)
    g(C.sperm[:], C.permrope)
    g(C.spoolm[:], C.poolm)
    g(C.s_blend[:], C.blendrow)
    g(C.rows[0:1, R_LS:R_LS + 16], C.lsc)
    C.BASELOAD = pl.cnt["ldcst"]


def _emit_ab_loads(C):
    pl = C.pl
    g = lambda t, s: pl.dma("gpsimd", t, s, "ldmsk")
    g(C.s_smaskp[:], C.smaskp)
    g(C.s_tmaskp[:], C.tmaskp)
    g(C.s_smaskl[:], C.smaskl)
    g(C.s_tmaskl[:], C.tmaskl)
    C.MSKLOAD = pl.cnt["ldmsk"]


def _emit_derived(C):
    pl, rows = C.pl, C.rows
    pl.wait("vector", "ldcst", C.BASELOAD)
    pl.v.memset(rows[0:1, 0:32], 0.0)
    pl.v.memset(rows[0:1, 48:128], 0.0, nochain=True)
    pl.v.memset(C.onescol_b[:], 1.0)
    pl.v.memset(C.onescol_f[:], 1.0)
    pl.v.memset(C.onesrow_f[:], 1.0)
    pl.v.memset(rows[0:1, R_RS32:R_RS32 + 1], 0.1767766952966369)
    for _ in range(2):
        pl.v.tensor_tensor(rows[0:1, R_TMP:R_TMP + 1],
                           rows[0:1, R_RS32:R_RS32 + 1],
                           rows[0:1, R_RS32:R_RS32 + 1], ALU.mult)
        pl.v.tensor_scalar(rows[0:1, R_TMP:R_TMP + 1],
                           rows[0:1, R_TMP:R_TMP + 1],
                           -16.0, 1.5, ALU.mult, ALU.add)
        pl.v.tensor_tensor(rows[0:1, R_RS32:R_RS32 + 1],
                           rows[0:1, R_RS32:R_RS32 + 1],
                           rows[0:1, R_TMP:R_TMP + 1], ALU.mult)
    pl.v.memset(rows[0:1, R_R1040:R_R1040 + 1], 1040.0)
    pl.v.reciprocal(rows[0:1, R_R1040:R_R1040 + 1],
                    rows[0:1, R_R1040:R_R1040 + 1])
    pl.wait("scalar", "vch")
    pl.s.activation(rows[0:1, R_CH:R_CH + 16], rows[0:1, R_LS:R_LS + 16],
                    ACT.Exp)
    pl.wait("vector", "sch")
    pl.v.tensor_single_scalar(rows[0:1, R_CH:R_CH + 16],
                              rows[0:1, R_CH:R_CH + 16],
                              rows[0:1, R_RS32:R_RS32 + 1], ALU.mult)
    pl.wait("tensor", "vch")
    pl.t.matmul(C.pb[5][:, 0:16], C.onesrow_f[:], rows[0:1, R_CH:R_CH + 16],
                start=True, stop=True)
    pl.t.matmul(C.pb[5][:, 16:20], C.onesrow_f[:],
                rows[0:1, R_R1040:R_R1040 + 4], start=True, stop=True)
    tv = pl.tmark()
    pl.wait("vector", "tms", tv)
    pl.v.tensor_copy(C.CVALL[:], C.pb[5][:, 0:16])
    pl.v.tensor_copy(C.RCONST[:], C.pb[5][:, 16:20])
    C.CVREADY = pl.vmark()


def _gen_phaseA(C, e):
    nc, pl = C.nc, C.pl
    QT = 512
    XTB = e(nc.sbuf_tensor("XTB", [128, 16 * QT], BF16))
    C.WQS = e(nc.sbuf_tensor("WQS", [128, 16 * 512], BF16))
    C.WQG = e(nc.sbuf_tensor("WQG", [128, 16 * 512], BF16))
    WKS = e(nc.sbuf_tensor("WKS", [128, 16 * 128], BF16))
    WKG = e(nc.sbuf_tensor("WKG", [128, 16 * 128], BF16))
    WVV = e(nc.sbuf_tensor("WVV", [128, 16 * 256], BF16))
    tabs = {nm: e(nc.sbuf_tensor("tab_" + nm, [128, QT], F32))
            for nm in ("ch", "cl", "sh", "sl")}
    KG = e(nc.sbuf_tensor("KG", [128, QT], F32))
    KGR = e(nc.sbuf_tensor("KGR", [128, QT], F32))
    KTMP = e(nc.sbuf_tensor("KTMP", [128, 2 * QT], F32))
    KGB = e(nc.sbuf_tensor("KGB", [128, QT], BF16))
    VTQ = e(nc.sbuf_tensor("VTQ", [128, 8 * 256], BF16))
    QSM = e(nc.sbuf_tensor("QSM", [128, 4 * 512], F32))
    QGO = e(nc.sbuf_tensor("QGO", [128, 4 * 512], F32))

    for (wbuf, src) in ((WKS, C.wksem), (WKG, C.wkgeo), (WVV, C.wv),
                        (C.WQS, C.wqsem), (C.WQG, C.wqgeo)):
        pl.dma("gpsimd", wbuf[:].rearrange("p (k c) -> p k c", k=16),
               src.rearrange("(k p) c -> p k c", p=128), "ldwA")
    C.WLOAD2 = pl.cnt["ldwA"]
    pl.v.memset(C.KCT[:], 0.0)
    vev = {}
    xtb_tms = 0
    yield

    for qq in range(4):
        tok0 = qq * QT
        if qq > 0:
            pl.wait("gpsimd", "tms", xtb_tms)
        pl.dma("gpsimd", XTB[:].rearrange("p (k c) -> p k c", k=16),
               C.xkv[:, tok0:tok0 + QT].rearrange("(k p) c -> p k c", p=128),
               "ldxq")
        for nm, src in (("ch", C.kcos_h), ("cl", C.kcos_l),
                        ("sh", C.ksin_h), ("sl", C.ksin_l)):
            if qq > 0:
                pl.wait("scalar", "vch", pl.vmark())
            pl.dma("scalar", tabs[nm][:], src[:, tok0:tok0 + QT], "ldm")
        XL = pl.cnt["ldxq"]
        TL = pl.cnt["ldm"]
        pl.wait("tensor", "ldxq", XL)
        pl.wait("tensor", "ldwA", C.WLOAD2)
        ev = {}
        for wi, wbuf in enumerate((WKS, WKG)):
            bi = wi
            if bi in vev:
                pl.wait("tensor", "vch", vev[bi])
            for kt in range(16):
                pl.t.matmul(C.pb[bi][:, 0:512],
                            wbuf[:, kt * 128:(kt + 1) * 128],
                            XTB[:, kt * QT:(kt + 1) * QT],
                            start=(kt == 0), stop=(kt == 15))
            tv = pl.tmark()
            pl.wait("vector", "tms", tv)
            if wi == 0:
                if qq < 2:
                    for g in range(4):
                        base = g * 1152
                        pl.v.reduce_sum(C.rcol[0:32, 0:8],
                                        C.pb[bi][g * 32:(g + 1) * 32, 0:512]
                                        .rearrange("p (b w) -> p b w", w=64),
                                        AX.X)
                        pl.v.tensor_scalar(
                            C.KCT[0:32, base + qq * 8:base + qq * 8 + 8],
                            C.rcol[0:32, 0:8], 1.0 / 64.0, None, ALU.mult)
                else:
                    for g in range(4):
                        base = g * 1152
                        pl.v.tensor_copy(
                            C.KCT[0:32, base + 16 + (qq - 2) * 512:
                                  base + 16 + (qq - 1) * 512],
                            C.pb[bi][g * 32:(g + 1) * 32, 0:512],
                            nochain=(g > 0))
            else:
                pl.v.tensor_copy(KG[:], C.pb[bi][:, 0:512])
            ev[wi] = pl.vmark()
            vev[bi] = ev[wi]
        pl.v.tensor_copy(KGB[:], KG[:])
        kgb_v = pl.vmark()
        pl.wait("tensor", "vch", kgb_v)
        pl.t.matmul(C.pb[2][:, 0:512], C.sperm[:], KGB[:], start=True,
                    stop=True)
        tv = pl.tmark()
        pl.wait("vector", "tms", tv)
        pl.wait("vector", "ldm", TL)
        pl.v.tensor_tensor(KGR[:], KG[:], tabs["ch"][:], ALU.mult)
        pl.v.tensor_tensor(KTMP[:, 0:512], KG[:], tabs["cl"][:], ALU.mult,
                           nochain=True)
        pl.v.tensor_tensor(KGR[:], KGR[:], KTMP[:, 0:512], ALU.add)
        pl.v.tensor_copy(KTMP[:, 0:512], C.pb[2][:, 0:512])
        pl.v.tensor_tensor(KTMP[:, 0:512], KTMP[:, 0:512], tabs["sh"][:],
                           ALU.mult)
        pl.v.tensor_tensor(KGR[:], KGR[:], KTMP[:, 0:512], ALU.add)
        pl.v.tensor_copy(KTMP[:, 0:512], C.pb[2][:, 0:512])
        pl.v.tensor_tensor(KTMP[:, 0:512], KTMP[:, 0:512], tabs["sl"][:],
                           ALU.mult)
        pl.v.tensor_tensor(KGR[:], KGR[:], KTMP[:, 0:512], ALU.add)
        if qq == 0:
            for g in range(4):
                pl.v.memset(C.KCT[64:68, g * 1152:g * 1152 + 1040], 1.0,
                            nochain=(g > 0))
        if qq < 2:
            for g in range(4):
                base = g * 1152
                pl.v.reduce_sum(C.rcol[0:32, 0:8],
                                KGR[g * 32:(g + 1) * 32, :]
                                .rearrange("p (b w) -> p b w", w=64), AX.X)
                pl.v.tensor_scalar(
                    C.KCT[32:64, base + qq * 8:base + qq * 8 + 8],
                    C.rcol[0:32, 0:8], 1.0 / 64.0, None, ALU.mult)
        else:
            for g in range(4):
                base = g * 1152
                pl.v.tensor_copy(C.KCT[32:64, base + 16 + (qq - 2) * 512:
                                       base + 16 + (qq - 1) * 512],
                                 KGR[g * 32:(g + 1) * 32, :],
                                 nochain=(g > 0))
        kct_v = pl.vmark()

        for ti in range(4):
            tt = qq * 4 + ti
            bi = 3 + tt % 2
            if bi in vev:
                pl.wait("tensor", "vch", vev[bi])
            for kt in range(16):
                pl.t.matmul(C.pb[bi][:, 0:256],
                            XTB[:, kt * QT + ti * 128:kt * QT + (ti + 1) * 128],
                            WVV[:, kt * 256:(kt + 1) * 256],
                            start=(kt == 0), stop=(kt == 15))
            tv = pl.tmark()
            pl.wait("vector", "tms", tv)
            if tt < 8:
                pl.v.tensor_copy(VTQ[:, tt * 256:(tt + 1) * 256],
                                 C.pb[bi][:, 0:256])
            else:
                kt2 = tt - 7
                for g in range(4):
                    pl.v.tensor_copy(
                        C.V2T[:, kt2 * 260 + g * 65:kt2 * 260 + g * 65 + 64],
                        C.pb[bi][:, g * 64:(g + 1) * 64], nochain=(g > 0))
            vev[bi] = pl.vmark()
        xtb_tms = tv
        yield

    pl.wait("tensor", "vch", pl.vmark())
    for tt in range(8):
        pl.t.matmul(C.pb[5][0:16, 0:256], C.spoolm[:, tt * 16:(tt + 1) * 16],
                    VTQ[:, tt * 256:(tt + 1) * 256],
                    start=(tt == 0), stop=(tt == 7))
    tv = pl.tmark()
    pl.wait("vector", "tms", tv)
    pl.v.memset(C.V2T[:, 0:260], 0.0)
    for g in range(4):
        pl.v.tensor_copy(C.V2T[0:16, g * 65:g * 65 + 64],
                         C.pb[5][0:16, g * 64:(g + 1) * 64], nochain=(g > 0))
    for kt in range(9):
        pp = 16 if kt == 0 else 128
        pl.v.memset(C.V2T[0:pp, kt * 260:(kt + 1) * 260]
                    .rearrange("p (g c) -> p g c", g=4)[:, :, 64:65], 1.0,
                    nochain=(kt > 0))
    C.V2T_V = pl.vmark()
    pl.wait("tensor", "vch", C.V2T_V)
    for g in range(4):
        for kt in range(9):
            pp = 16 if kt == 0 else 128
            pl.t.matmul(C.pb[5][0:65, g:g + 1],
                        C.V2T[0:pp, kt * 260 + g * 65:kt * 260 + (g + 1) * 65],
                        C.onescol_b[0:pp, 0:1], start=(kt == 0), stop=(kt == 8))
    tv = pl.tmark()
    pl.wait("vector", "tms", tv)
    pl.v.tensor_single_scalar(C.MEANV[0:64, 0:4], C.pb[5][0:64, 0:4],
                              C.RCONST[0:64, 0:1], ALU.mult)
    pl.v.tensor_single_scalar(C.MEANV[64:128, 0:4], C.pb[5][0:64, 0:4],
                              C.RCONST[0:64, 0:1], ALU.mult, nochain=True)
    C.MEANV_V = pl.vmark()
    yield

    for hf in range(2):
        tk0 = hf * 512
        pl.wait("gpsimd", "tms", xtb_tms)
        pl.dma("gpsimd", XTB[:].rearrange("p (k c) -> p k c", k=16),
               C.xq[:, tk0:tk0 + 512].rearrange("(k p) c -> p k c", p=128),
               "ldxq")
        for nm, src in (("ch", C.qcos_h), ("cl", C.qcos_l),
                        ("sh", C.qsin_h), ("sl", C.qsin_l)):
            pl.wait("scalar", "vch", pl.vmark())
            pl.dma("scalar", tabs[nm][:], src[:, tk0:tk0 + 512], "ldm")
        XL = pl.cnt["ldxq"]
        TL = pl.cnt["ldm"]
        yield
        pl.wait("tensor", "ldxq", XL)
        QSEM, QGEO = QSM, QGO
        ev = {}
        for wi, (wbuf, dstq) in enumerate(((C.WQS, QSEM), (C.WQG, QGEO))):
            for mt in range(4):
                bi = (wi * 4 + mt) % 2
                if bi in ev:
                    pl.wait("tensor", "vch", ev[bi])
                elif bi in vev:
                    pl.wait("tensor", "vch", vev[bi])
                for kt in range(16):
                    pl.t.matmul(C.pb[bi][:, 0:512],
                                wbuf[:, kt * 512 + mt * 128:
                                     kt * 512 + (mt + 1) * 128],
                                XTB[:, kt * 512:(kt + 1) * 512],
                                start=(kt == 0), stop=(kt == 15))
                tv = pl.tmark()
                pl.wait("vector", "tms", tv)
                pl.wait("vector", "vch", C.CVREADY)
                for hh in range(4):
                    h = mt * 4 + hh
                    pl.v.tensor_scalar(dstq[hh * 32:(hh + 1) * 32,
                                            mt * 512:(mt + 1) * 512],
                                       C.pb[bi][hh * 32:(hh + 1) * 32, 0:512],
                                       C.CVALL[hh * 32:hh * 32 + 32, h:h + 1],
                                       None, ALU.mult, nochain=(hh > 0))
                ev[bi] = pl.vmark()
            yield
        xtb_tms = tv
        for mt in range(4):
            pl.v.tensor_copy(KGB[:], QGEO[:, mt * 512:(mt + 1) * 512])
            qgb_v = pl.vmark()
            pl.wait("tensor", "vch", qgb_v)
            pl.t.matmul(C.pb[2][:, 0:512], C.sperm[:], KGB[:],
                        start=True, stop=True)
            tv = pl.tmark()
            pl.wait("vector", "tms", tv)
            pl.wait("vector", "ldm", TL)
            sl = slice(mt * 512, (mt + 1) * 512)
            pl.v.tensor_tensor(KTMP[:, 0:512], QGEO[:, sl], tabs["ch"][:],
                               ALU.mult)
            pl.v.tensor_tensor(KTMP[:, 512:1024], QGEO[:, sl], tabs["cl"][:],
                               ALU.mult, nochain=True)
            pl.v.tensor_tensor(KTMP[:, 0:512], KTMP[:, 0:512],
                               KTMP[:, 512:1024], ALU.add)
            pl.v.tensor_copy(KTMP[:, 512:1024], C.pb[2][:, 0:512])
            pl.v.tensor_tensor(KTMP[:, 512:1024], KTMP[:, 512:1024],
                               tabs["sh"][:], ALU.mult)
            pl.v.tensor_tensor(KTMP[:, 0:512], KTMP[:, 0:512],
                               KTMP[:, 512:1024], ALU.add)
            pl.v.tensor_copy(KTMP[:, 512:1024], C.pb[2][:, 0:512])
            pl.v.tensor_tensor(KTMP[:, 512:1024], KTMP[:, 512:1024],
                               tabs["sl"][:], ALU.mult)
            pl.v.tensor_tensor(QGEO[:, sl], KTMP[:, 0:512],
                               KTMP[:, 512:1024], ALU.add)
            yield
        for h in range(16):
            mt, hh = h // 4, h % 4
            col = h * TOWN + hf * 512
            pl.v.tensor_copy(C.QCT[0:32, col:col + 512],
                             QSEM[hh * 32:(hh + 1) * 32,
                                  mt * 512:(mt + 1) * 512],
                             nochain=(h > 0))
            pl.v.tensor_copy(C.QCT[32:64, col:col + 512],
                             QGEO[hh * 32:(hh + 1) * 32,
                                  mt * 512:(mt + 1) * 512], nochain=True)
    pl.v.memset(C.QCT[64:128, :], 0.0)
    C.QCT_V = pl.vmark()


def _gen_prepass(C, e):
    nc, pl, rows = C.nc, C.pl, C.rows
    WFP = e(nc.sbuf_tensor("WFP", [128, 2 * 2048], F32))
    WQB = e(nc.sbuf_tensor("WQB", [128, 2048], BF16))
    SLAB = e(nc.sbuf_tensor("SLAB", [128, 2 * 2048], BF16))
    PSUMS = e(nc.sbuf_tensor("PSUMS", [128, 32], F32))
    WSCOL = e(nc.sbuf_tensor("WSCOL", [128, 8], F32))

    wtiles = [("g", C.wgch, i, 2048) for i in range(8)] + \
             [("u", C.wuch, i, 2048) for i in range(8)] + \
             [("d", C.wdch, i, 1024) for i in range(16)]

    red_vs = []
    for ti, (kind, ap, i, ncol) in enumerate(wtiles):
        buf = WFP[:, (ti % 2) * 2048:(ti % 2) * 2048 + ncol]
        sem = "pw%d" % (ti % 2)
        if ti >= 2:
            pl.wait("sync", "vch", red_vs[ti - 2])
        pl.dma("sync", buf, ap[i * 128:(i + 1) * 128, 0:ncol], sem)
        pl.wait("vector", sem, pl.cnt[sem])
        pl.v.reduce_sum(PSUMS[:, ti:ti + 1], buf, AX.X,
                        apply_absolute_value=True, nochain=True)
        red_vs.append(pl.vmark())
        if ti % 4 == 3:
            yield
    if SUBST < 1:
        return
    pl.wait("tensor", "vch", red_vs[-1])
    pl.t.matmul(C.pst[0:1, 0:32], C.onescol_f[:], PSUMS[:, 0:32],
                start=True, stop=True)
    tv = pl.tmark()
    pl.wait("vector", "tms", tv)
    pl.v.reduce_sum(rows[0:1, R_WS + 0:R_WS + 1], C.pst[0:1, 0:8], AX.X)
    pl.v.reduce_sum(rows[0:1, R_WS + 1:R_WS + 2], C.pst[0:1, 8:16], AX.X,
                    nochain=True)
    pl.v.reduce_sum(rows[0:1, R_WS + 2:R_WS + 3], C.pst[0:1, 16:32], AX.X,
                    nochain=True)
    pl.v.memset(rows[0:1, R_WS + 3:R_WS + 4], 0.0, nochain=True)
    if SUBST < 2:
        return
    pl.wait("gpsimd", "vch", pl.vmark())
    pl.dma("gpsimd", C.ccs_in, rows[0:1, R_WS:R_WS + 4], "arb")
    pl.wait("gpsimd", "arb", pl.cnt["arb"])
    op = pl.g.collective_compute(
        "AllReduce", ALU.add, replica_groups=[list(range(NCN))],
        ins=[C.ccs_in], outs=[C.ccs_out])
    op.incs.append(("arc", 1))
    pl.cnt["arc"] += 1
    pl.wait("gpsimd", "arc")
    pl.dma("gpsimd", rows[0:1, R_WS + 4:R_WS + 8], C.ccs_out, "arb")
    pl.wait("vector", "arb", pl.cnt["arb"])
    pl.v.tensor_scalar(rows[0:1, R_WS + 8:R_WS + 11],
                       rows[0:1, R_WS + 4:R_WS + 7],
                       1.0 / 16777216.0, EPS, ALU.mult, ALU.max)
    pl.v.reciprocal(rows[0:1, R_WS + 11:R_WS + 14],
                    rows[0:1, R_WS + 8:R_WS + 11])
    pl.v.reciprocal(rows[0:1, R_WS + 14:R_WS + 17],
                    rows[0:1, R_WS + 11:R_WS + 14])
    scl_v = pl.vmark()
    pl.wait("tensor", "vch", scl_v)
    pl.t.matmul(C.pst[:, 40:46], C.onesrow_f[:],
                rows[0:1, R_WS + 11:R_WS + 17], start=True, stop=True)
    tv = pl.tmark()
    pl.wait("vector", "tms", tv)
    pl.v.tensor_copy(WSCOL[:, 0:6], C.pst[:, 40:46])
    pl.v.tensor_copy(C.WMC[:, 0:3], C.pst[:, 43:46], nochain=True)
    scl_v = pl.vmark()
    yield
    if SUBST < 3:
        return

    tern_vs, tF_after, scrw_after = [], [], []
    evac_v = pl.vmark()
    down_base = 2 * OWN_G
    for jj, (kind, ap, i, ncol) in enumerate(wtiles):
        scol = {"g": 0, "u": 1, "d": 2}[kind]
        mcol = 3 + {"g": 0, "u": 1, "d": 2}[kind]
        buf = WFP[:, (jj % 2) * 2048:(jj % 2) * 2048 + ncol]
        sem = "pq%d" % (jj % 2)
        if jj >= 2:
            pl.wait("sync", "vch", tern_vs[jj - 2])
        else:
            pl.wait("sync", "vch", red_vs[30 + jj])
        pl.dma("sync", buf, ap[i * 128:(i + 1) * 128, 0:ncol], sem)
        pl.wait("vector", sem, pl.cnt[sem])
        pl.wait("vector", "vch", scl_v)
        if jj >= 1:
            pl.wait("vector", "tms", tF_after[jj - 1])
        if jj >= 2:
            pl.wait("vector", "sw%d" % (jj % 2), scrw_after[jj - 2])
        z = buf
        pl.v.tensor_single_scalar(z, buf, WSCOL[:, scol:scol + 1], ALU.mult,
                                  nochain=True)
        pl.v.tensor_scalar(WQB[:, 0:ncol], z, -0.5, None, ALU.is_lt)
        pl.v.scalar_tensor_tensor(WQB[:, 0:ncol], z, 0.5, WQB[:, 0:ncol],
                                  ALU.is_gt, ALU.subtract)
        tern_vs.append(pl.vmark())
        par = jj % 2
        if SUBST < 4:
            tF_after.append(pl.cnt["tms"])
            scrw_after.append(pl.cnt["sw%d" % par])
            continue
        pl.wait("tensor", "vch", tern_vs[-1])
        for grp in range(ncol // 1024):
            pl.wait("tensor", "vch", evac_v)
            for j in range(8):
                bkk = grp * 8 + j
                pl.t.transpose(C.ptr[:, j * 128:(j + 1) * 128],
                               WQB[:, bkk * 128:(bkk + 1) * 128], C.sident[:])
            tv = pl.tmark()
            pl.wait("vector", "tms", tv)
            pl.v.tensor_copy(SLAB[:, par * 2048 + grp * 1024:
                                  par * 2048 + (grp + 1) * 1024],
                             C.ptr[:, 0:1024], nochain=True)
            evac_v = pl.vmark()
        tF_after.append(pl.cnt["tms"])
        if SUBST < 5:
            scrw_after.append(pl.cnt["sw%d" % par])
            continue
        pl.wait("gpsimd", "vch", evac_v)
        ssem = "sw%d" % par
        if kind in ("g", "u"):
            off = (0 if kind == "g" else OWN_G) + i * GSL
            dst = C.scr_own[0, off:off + GSL].rearrange("(p n) -> p n", p=128)
            pl.dma("gpsimd", dst, SLAB[:, par * 2048:(par + 1) * 2048], ssem)
        else:
            otp, oh = i // 2, i % 2
            base_o = down_base + otp * (128 * 2048)
            dst = C.scr_own[0, base_o:base_o + 128 * 2048].rearrange(
                "(p k c) -> p k c", p=128, k=8)[:, :, oh * 128:(oh + 1) * 128]
            pl.dma("gpsimd", dst,
                   SLAB[:, par * 2048:par * 2048 + 1024]
                   .rearrange("p (k c) -> p k c", k=8), ssem)
        scrw_after.append(pl.cnt[ssem])
        yield
    if C.debug and STAGE >= 5:
        _emit_allgather(C)


def _emit_allgather(C):
    pl = C.pl
    pl.wait("gpsimd", "sw0", pl.cnt["sw0"])
    pl.wait("gpsimd", "sw1", pl.cnt["sw1"])
    pl.wait("gpsimd", "r2w", pl.cnt["r2w"])
    pl.wait("gpsimd", "xqw", pl.cnt["xqw"])
    pl.wait("gpsimd", "ldxq", pl.cnt["ldxq"])
    op = pl.g.collective_compute(
        "AllGather", ALU.bypass, replica_groups=[list(range(NCN))],
        ins=[C.scr_own], outs=[C.scr_all])
    op.incs.append(("agc", 1))
    pl.cnt["agc"] += 1
    C.AGDONE = pl.cnt["agc"]


def _emit_dbgA(C):
    pl = C.pl
    pl.wait("gpsimd", "vch", pl.vmark())
    if STAGE >= 1:
        pl.dma("gpsimd", C.dbg["d_kct"], C.KCT[:], "outw")
        pl.dma("gpsimd", C.dbg["d_qct"], C.QCT[:], "outw")
        pl.dma("gpsimd", C.dbg["d_v2t"], C.V2T[:], "outw")
        pl.dma("gpsimd", C.dbg["d_meanv"], C.MEANV[:], "outw")
    pl.dma("gpsimd", C.dbg["d_rows"], C.rows[:], "outw")
    if STAGE < 5:
        return
    pl.wait("gpsimd", "agc", C.AGDONE)
    pl.dma("gpsimd", C.dbg["d_scr"][0:1, 0:4096], C.scr_all[0:1, 0:4096],
           "outw")
    pl.dma("gpsimd", C.dbg["d_scr"][0:1, 4096:8192],
           C.scr_all[3:4, OWN_G:OWN_G + 4096], "outw")
    pl.dma("gpsimd", C.dbg["d_scr"][0:1, 8192:12288],
           C.scr_all[6:7, 2 * OWN_G:2 * OWN_G + 4096], "outw")


def _emit_attn_all(C, e):
    nc, pl = C.nc, C.pl
    WOO = e(nc.sbuf_tensor("WOO", [128, 8 * 2048], BF16))
    EXPB = e(nc.sbuf_tensor("EXPB", [128, 2 * 9 * 512], BF16))
    ATT = e(nc.sbuf_tensor("ATT", [128, 8 * 512], BF16))
    X1C = e(nc.sbuf_tensor("X1C", [128, 16 * CHT], F32))
    MCOL = e(nc.sbuf_tensor("MCOL", [128, 64], F32))
    MROW = e(nc.sbuf_tensor("MROW", [64, 128], BF16))
    XRES = e(nc.sbuf_tensor("XRES", [128, 2 * 512], F32))
    BCB = e(nc.sbuf_tensor("BCB", [128, 1024], F32))
    BL1 = e(nc.sbuf_tensor("BL1", [128, 128], F32))
    BLM = e(nc.sbuf_tensor("BLM", [128, 128], F32))
    R1B = e(nc.sbuf_tensor("R1B", [128, 512], BF16))
    SQT = e(nc.sbuf_tensor("SQT", [128, 512], BF16))
    QS1 = e(nc.sbuf_tensor("QS1", [128, 512], F32))
    QS2 = e(nc.sbuf_tensor("QS2", [128, 512], F32))
    ATI8 = e(nc.sbuf_tensor("ATI8", [128, 512], I8))
    AQB = e(nc.sbuf_tensor("AQB", [128, 2 * 512], BF16))
    M2A = e(nc.sbuf_tensor("M2A", [32, 16], F32))
    bufs = (WOO, EXPB, ATT, X1C, MCOL, MROW, XRES, BCB, BL1, BLM, R1B, SQT,
            QS1, QS2, ATI8, AQB, M2A)

    pl.dma("gpsimd", WOO[:].rearrange("p (k c) -> p k c", k=8),
           C.wo.rearrange("(k p) c -> p k c", p=128), "ldwo")
    C.WOL = pl.cnt["ldwo"]
    pl.wait("vector", "ldmsk", C.MSKLOAD)
    pl.wait("sync", "outw")

    pl.t.matmul(C.pb[2][:, 0:128], C.onesrow_f[:], C.s_blend[:],
                start=True, stop=True)
    tv = pl.tmark()
    pl.wait("vector", "tms", tv)
    pl.v.tensor_copy(BLM[:], C.pb[2][:, 0:128])
    pl.v.tensor_scalar(BL1[:], C.pb[2][:, 0:128], -1.0, 1.0,
                       ALU.mult, ALU.add, nochain=True)

    g0 = _gen_attn_chunk(C, 0, bufs)
    g1 = _gen_attn_chunk(C, 1, bufs)
    next(g0)
    next(g0)
    if ATTN_IL:
        next(g1)
        for _ in g0:
            pass
        if not C.debug and STAGE >= 5:
            _emit_allgather(C)
        for _ in g1:
            pass
    else:
        for _ in g0:
            pass
        if not C.debug and STAGE >= 5:
            _emit_allgather(C)
        for _ in g1:
            pass
    if C.debug:
        pl.wait("gpsimd", "r2w")
        pl.dma("gpsimd", C.dbg["d_x1"][:, 0:16 * CHT]
               .rearrange("p (k c) -> p k c", k=16),
               C.x1d[:, 0:CHT].rearrange("(k p) c -> p k c", p=128), "outw")
        pl.dma("gpsimd", C.dbg["d_x1"][:, 16 * CHT:]
               .rearrange("p (k c) -> p k c", k=16),
               C.x1d[:, CHT:].rearrange("(k p) c -> p k c", p=128), "outw")


def _gen_attn_chunk(C, ck, bufs):
    (WOO, EXPB, ATT, X1C, MCOL, MROW, XRES, BCB, BL1, BLM, R1B, SQT,
     QS1, QS2, ATI8, AQB, M2A) = bufs
    RECB = BCB[:, 0:512]
    RSB = BCB[:, 0:512]
    NBB = BCB[:, 512:1024]
    nc, pl = C.nc, C.pl
    LOCS = SLOT_LOC[ck * 4:(ck + 1) * 4]
    pl.wait("tensor", "vch", pl.vmark())
    if C.debug:
        pl.wait("vector", "outw")

    red_done = {}
    for s4 in range(4):
        loc = LOCS[s4]
        for h in range(16):
            g = h // 4
            par = h % 2
            lb = (C.pb[0], C.pb[1]) if par == 0 else (C.pb[2], C.pb[3])
            poolb = C.pb[4] if par == 0 else C.pb[5]
            qcol = h * TOWN + ck * 512 + s4 * 128
            if par in red_done:
                pl.wait("tensor", "vch", red_done[par])
            base = g * 1152
            pl.t.matmul(poolb[:, 0:16], C.QCT[0:64, qcol:qcol + 128],
                        C.KCT[0:64, base:base + 16], start=True, stop=True)
            for kt in range(loc):
                pl.t.matmul(lb[kt // 4][:, (kt % 4) * 128:(kt % 4 + 1) * 128],
                            C.QCT[0:64, qcol:qcol + 128],
                            C.KCT[0:64, base + 16 + kt * 128:
                                  base + 16 + (kt + 1) * 128],
                            start=True, stop=True)
            tv = pl.tmark()
            pl.wait("vector", "tms", tv)
            ps = poolb[:, 0:16]
            if ck == 0:
                pl.v.tensor_tensor(ps, ps,
                                   C.s_smaskp[:, s4 * 16:(s4 + 1) * 16],
                                   ALU.add, nochain=True)
            pl.v.reduce_max(C.rcol[:, 0:1], ps, AX.X)
            ncand = 1
            if loc > 0:
                for w in range(2):
                    kt = loc - 2 + w
                    sl = lb[kt // 4][:, (kt % 4) * 128:(kt % 4 + 1) * 128]
                    pl.v.tensor_tensor(sl, sl,
                                       C.s_smaskl[:, (s4 * 2 + w) * 128:
                                                  (s4 * 2 + w + 1) * 128],
                                       ALU.add, nochain=True)
                n0 = min(loc, 4)
                pl.v.reduce_max(C.rcol[:, 1:2], lb[0][:, 0:n0 * 128], AX.X)
                ncand = 2
                if loc > 4:
                    pl.v.reduce_max(C.rcol[:, 2:3],
                                    lb[1][:, 0:(loc - 4) * 128], AX.X)
                    ncand = 3
            pl.v.reduce_max(MCOL[:, s4 * 16 + h:s4 * 16 + h + 1],
                            C.rcol[:, 0:ncand], AX.X)
            red_done[par] = pl.vmark()

    pl.wait("tensor", "vch", max(red_done.values()))
    pl.t.transpose(C.pb[5][0:64, 0:128], MCOL[:], C.sidentf[:])
    tv = pl.tmark()
    pl.wait("vector", "tms", tv)
    pl.v.tensor_scalar(MROW[:], C.pb[5][0:64, 0:128], -1.0, None, ALU.mult)
    mr_v = pl.vmark()
    pl.wait("sync", "vch", mr_v)
    pl.dma("sync", C.mbounce, MROW[:], "boun")
    pl.wait("sync", "boun", pl.cnt["boun"])
    for s4 in range(4):
        dst = C.QCT[64 + s4:65 + s4, :].rearrange("p (h q) -> p h q", h=16)[
            :, :, ck * 512 + s4 * 128:ck * 512 + (s4 + 1) * 128]
        src = C.mbounce[:].rearrange("(s h) q -> s h q", s=4)[s4:s4 + 1]
        pl.dma("sync", dst, src, "boun")
    MB_DONE = pl.cnt["boun"]

    yield

    pl.wait("tensor", "boun", MB_DONE)
    pl.wait("tensor", "ldwo", C.WOL)
    if ck == 1:
        pl.wait("tensor", "vch", C.qs_v[0])
    pv_evac = {}
    st_free = {3: None, 4: None}
    prev_pv_tv = None
    pv_tvs = []
    for g in range(4):
        kbase = g * 1152
        for h4 in range(4):
            h = g * 4 + h4
            qcol = h * TOWN + ck * 512
            pvb = C.pb[0] if h % 2 == 0 else C.pb[1]
            eb = (h % 2) * 9 * 512
            if len(pv_tvs) >= 2:
                pl.wait("scalar", "tms", pv_tvs[-2])
            kts = [("p", 0)] + [("l", kt) for kt in range(LOCS[3])]
            for idx, (kind, kt) in enumerate(kts):
                stb = 3 + idx % 2
                if st_free[stb] is not None:
                    pl.wait("tensor", "sch", st_free[stb])
                if kind == "p":
                    pl.t.matmul(C.pb[stb][0:16, 0:512],
                                C.KCT[0:68, kbase:kbase + 16],
                                C.QCT[0:68, qcol:qcol + 512],
                                start=True, stop=True)
                else:
                    cs = 128 * sum(1 for l in LOCS if l <= kt)
                    pl.t.matmul(C.pb[stb][:, cs:512],
                                C.KCT[0:68, kbase + 16 + kt * 128:
                                      kbase + 16 + (kt + 1) * 128],
                                C.QCT[0:68, qcol + cs:qcol + 512],
                                start=True, stop=True)
                tv = pl.tmark()
                need_mask = (kind == "p" and ck == 0) or \
                            (kind == "l" and any(l - 2 <= kt < l for l in LOCS))
                if need_mask:
                    pl.wait("vector", "tms", tv)
                    if kind == "p":
                        for s4 in range(4):
                            pl.v.tensor_tensor(
                                C.pb[stb][0:16, s4 * 128:(s4 + 1) * 128],
                                C.pb[stb][0:16, s4 * 128:(s4 + 1) * 128],
                                C.s_tmaskp[:, s4 * 128:(s4 + 1) * 128],
                                ALU.add, nochain=(s4 > 0))
                    else:
                        firstm = True
                        for s4, l in enumerate(LOCS):
                            if l - 2 <= kt < l:
                                w = kt - (l - 2)
                                pl.v.tensor_tensor(
                                    C.pb[stb][:, s4 * 128:(s4 + 1) * 128],
                                    C.pb[stb][:, s4 * 128:(s4 + 1) * 128],
                                    C.s_tmaskl[:, (s4 * 2 + w) * 128:
                                               (s4 * 2 + w + 1) * 128],
                                    ALU.add, nochain=not firstm)
                                firstm = False
                    pl.wait("scalar", "vch", pl.vmark())
                else:
                    pl.wait("scalar", "tms", tv)
                if kind == "p":
                    pl.s.activation(EXPB[0:16, eb:eb + 512],
                                    C.pb[stb][0:16, 0:512], ACT.Exp)
                else:
                    cs = 128 * sum(1 for l in LOCS if l <= kt)
                    pl.s.activation(
                        EXPB[:, eb + (1 + kt) * 512 + cs:
                             eb + (1 + kt) * 512 + 512],
                        C.pb[stb][:, cs:512], ACT.Exp)
                st_free[stb] = pl.smark()
            pl.wait("tensor", "sch", max(x for x in st_free.values()
                                         if x is not None))
            if (h % 2) in pv_evac:
                pl.wait("tensor", "vch", pv_evac[h % 2])
            pl.t.matmul(pvb[0:65, 0:512], C.V2T[0:16, g * 65:g * 65 + 65],
                        EXPB[0:16, eb:eb + 512], start=True,
                        stop=(LOCS[3] == 0))
            for kt in range(LOCS[3]):
                cs = 128 * sum(1 for l in LOCS if l <= kt)
                pl.t.matmul(pvb[0:65, cs:512],
                            C.V2T[:, (1 + kt) * 260 + g * 65:
                                  (1 + kt) * 260 + (g + 1) * 65],
                            EXPB[:, eb + (1 + kt) * 512 + cs:
                                 eb + (1 + kt) * 512 + 512],
                            start=False, stop=(kt == LOCS[3] - 1))
            prev_pv_tv = pl.tmark()
            pv_tvs.append(prev_pv_tv)
            pl.wait("scalar", "tms", prev_pv_tv)
            pl.s.activation(C.TROWS[0:1, 1536:2048], pvb[64:65, 0:512],
                            ACT.Copy)
            dcp_s = pl.smark()
            pl.wait("tensor", "sch", dcp_s)
            pl.t.matmul(C.pb[2][:, 0:512], C.onesrow_f[:],
                        C.TROWS[0:1, 1536:2048], start=True, stop=True)
            tv2 = pl.tmark()
            pl.wait("scalar", "tms", tv2)
            pl.s.activation(RECB, C.pb[2][:, 0:512], ACT.Ln)
            pl.s.activation(RECB, RECB, ACT.Exp, scale=-1.0)
            rec_s = pl.smark()
            pl.wait("vector", "sch", rec_s)
            att = ATT[(h % 2) * 64:(h % 2) * 64 + 64,
                      (h // 2) * 512:(h // 2) * 512 + 512]
            pl.v.tensor_tensor(att, pvb[0:64, 0:512], BCB[0:64, 0:512],
                               ALU.mult)
            if ck == 0:
                pbase = (h % 2) * 64
                a0 = ATT[pbase:pbase + 64,
                         (h // 2) * 512:(h // 2) * 512 + 128]
                pl.v.tensor_tensor(a0, a0, BL1[pbase:pbase + 64, :], ALU.mult)
                pl.v.scalar_tensor_tensor(a0, BLM[pbase:pbase + 64, :],
                                          C.MEANV[pbase:pbase + 64, g:g + 1],
                                          a0, ALU.mult, ALU.add)
            pv_evac[h % 2] = pl.vmark()

    pl.wait("tensor", "vch", max(pv_evac.values()))
    pl.wait("tensor", "sch", pl.smark())
    if ck == 1:
        pl.wait("vector", "r2w", C.X1D_DONE)
        pl.wait("sync", "vch", C.xres_free[0])
    wo_free = {3: None, 4: None}
    r1_vs = []
    last_stat_tv = None
    for mt in range(16):
        wob = 3 + mt % 2
        xsem = "xr%d" % (mt % 2)
        if mt >= 2:
            pl.wait("sync", "vch", r1_vs[mt - 2])
        pl.dma("sync", XRES[:, (mt % 2) * 512:(mt % 2 + 1) * 512],
               C.xq[mt * 128:(mt + 1) * 128, ck * 512:(ck + 1) * 512], xsem)
        if wo_free[wob] is not None:
            pl.wait("tensor", "vch", wo_free[wob])
        for kt in range(8):
            pl.t.matmul(C.pb[wob][:, 0:512],
                        WOO[:, kt * 2048 + mt * 128:kt * 2048 + (mt + 1) * 128],
                        ATT[:, kt * 512:(kt + 1) * 512],
                        start=(kt == 0), stop=(kt == 7))
        tv = pl.tmark()
        pl.wait("vector", "tms", tv)
        pl.wait("vector", xsem, pl.cnt[xsem])
        if last_stat_tv is not None:
            pl.wait("vector", "tms", last_stat_tv)
        x1t = X1C[:, mt * 512:(mt + 1) * 512]
        pl.v.tensor_tensor(x1t, C.pb[wob][:, 0:512],
                           XRES[:, (mt % 2) * 512:(mt % 2 + 1) * 512],
                           ALU.add, nochain=True)
        pl.v.tensor_copy(R1B[:], x1t)
        pl.v.tensor_tensor(SQT[:], R1B[:], R1B[:], ALU.mult)
        wo_free[wob] = pl.vmark()
        r1_vs.append(wo_free[wob])
        pl.wait("tensor", "vch", wo_free[wob])
        pl.t.matmul(C.pst[0:1, 0:512], C.onescol_b[:], R1B[:],
                    start=(mt == 0), stop=(mt == 15))
        pl.t.matmul(C.pb[5][0:1, 0:512], C.onescol_b[:], SQT[:],
                    start=(mt == 0), stop=(mt == 15))
        last_stat_tv = pl.tmark()

    pl.wait("vector", "tms", last_stat_tv)
    mu = C.TROWS[0:1, 0:512]
    rs = C.TROWS[0:1, 512:1024]
    nb = C.TROWS[0:1, 1024:1536]
    tm = C.TROWS[0:1, 1536:2048]
    pl.v.tensor_scalar(mu, C.pst[0:1, 0:512], 1.0 / 2048.0, None, ALU.mult)
    pl.v.tensor_scalar(tm, C.pb[5][0:1, 0:512], 1.0 / 2048.0, None, ALU.mult,
                       nochain=True)
    pl.v.tensor_tensor(nb, mu, mu, ALU.mult)
    pl.v.tensor_tensor(tm, tm, nb, ALU.subtract)
    pl.v.tensor_scalar(tm, tm, EPS, None, ALU.add)
    pl.wait("scalar", "vch", pl.vmark())
    pl.s.activation(rs, tm, ACT.Sqrt)
    pl.wait("vector", "sch", pl.smark())
    pl.v.reciprocal(rs, rs)
    pl.v.tensor_tensor(nb, rs, rs, ALU.mult)
    pl.v.tensor_tensor(nb, nb, tm, ALU.mult)
    pl.v.tensor_scalar(nb, nb, -0.5, 1.5, ALU.mult, ALU.add)
    pl.v.tensor_tensor(rs, rs, nb, ALU.mult)
    pl.v.tensor_tensor(nb, mu, rs, ALU.mult)
    pl.v.tensor_scalar(nb, nb, -1.0, None, ALU.mult)
    rs_v = pl.vmark()
    pl.wait("tensor", "vch", rs_v)
    pl.t.matmul(C.pb[2][:, 0:512], C.onesrow_f[:], C.TROWS[0:1, 512:1024],
                start=True, stop=True)
    pl.t.matmul(C.pb[5][:, 0:512], C.onesrow_f[:], C.TROWS[0:1, 1024:1536],
                start=True, stop=True)
    tv = pl.tmark()
    pl.wait("vector", "tms", tv)
    pl.v.tensor_copy(RSB, C.pb[2][:, 0:512])
    pl.v.tensor_copy(NBB, C.pb[5][:, 0:512], nochain=True)
    for mt in range(16):
        x1t = X1C[:, mt * 512:(mt + 1) * 512]
        pl.v.tensor_tensor(x1t, x1t, RSB, ALU.mult, nochain=(mt > 0))
        pl.v.tensor_tensor(x1t, x1t, NBB, ALU.add)
    C.X1_V = pl.vmark()
    pl.wait("gpsimd", "vch", C.X1_V)
    pl.dma("gpsimd", C.x1d[:, ck * CHT:(ck + 1) * CHT]
           .rearrange("(k p) c -> p k c", p=128),
           X1C[:].rearrange("p (k c) -> p k c", k=16), "r2w")
    C.X1D_DONE = pl.cnt["r2w"]
    if not hasattr(C, "x1d_done"):
        C.x1d_done = {}
    C.x1d_done[ck] = C.X1D_DONE

    yield

    aslots = (XRES[:, 0:512], QS1[:])
    axm = XRES[:, 512:1024]
    pl.wait("scalar", "xr0", pl.cnt["xr0"])
    pl.wait("scalar", "xr1", pl.cnt["xr1"])
    pl.wait("scalar", "vch", C.X1_V)
    amx_vs = []
    for mt in range(16):
        x1t = X1C[:, mt * 512:(mt + 1) * 512]
        aslot = aslots[mt % 2]
        if mt >= 2:
            pl.wait("scalar", "vch", amx_vs[mt - 2])
        pl.s.activation(aslot, x1t, ACT.Abs)
        pl.wait("vector", "sch", pl.smark())
        if mt == 0:
            pl.v.tensor_copy(axm, aslot)
        else:
            pl.v.tensor_tensor(axm, axm, aslot, ALU.max)
        amx_vs.append(pl.vmark())
    mu = C.TROWS[0:1, 0:512]
    tm = C.TROWS[0:1, 1536:2048]
    _amax_row(C, aslots[0], M2A, axm, tm)
    C.xres_free[ck] = pl.vmark()
    pl.v.tensor_scalar(tm, tm, EPS, None, ALU.max)
    pl.v.reciprocal(mu, tm)
    pl.v.tensor_scalar(mu, mu, 128.0, None, ALU.mult)
    pl.v.tensor_scalar(tm, tm, 1.0 / 128.0, None, ALU.mult)
    qsc_v = pl.vmark()
    pl.wait("tensor", "vch", qsc_v)
    pl.t.matmul(C.pb[2][:, 0:512], C.onesrow_f[:], mu, start=True, stop=True)
    pl.t.matmul(C.pb[5][:, 0:512], C.onesrow_f[:], tm, start=True, stop=True)
    tv = pl.tmark()
    pl.wait("vector", "tms", tv)
    pl.v.tensor_copy(QS1[:], C.pb[2][:, 0:512])
    pl.v.tensor_copy(QS2[:], C.pb[5][:, 0:512], nochain=True)
    C.qs_v[ck] = pl.vmark()
    xqw_mks = []
    for mt in range(16):
        x1t = X1C[:, mt * 512:(mt + 1) * 512]
        sl = AQB[:, (mt % 2) * 512:(mt % 2 + 1) * 512]
        if mt >= 2:
            pl.wait("vector", "xqw", xqw_mks[mt - 2])
        pl.v.tensor_tensor(ATI8[:], x1t, QS1[:], ALU.mult)
        pl.v.tensor_tensor(sl, ATI8[:], QS2[:], ALU.mult)
        mk = pl.vmark()
        pl.wait("gpsimd", "vch", mk)
        pl.dma("gpsimd", C.xqd[ck * D + mt * 128:ck * D + (mt + 1) * 128, :],
               sl, "xqw")
        xqw_mks.append(pl.cnt["xqw"])
    C.aq_v[ck] = pl.vmark()
    C.xqw_done[ck] = pl.cnt["xqw"]

    if C.debug:
        pl.wait("gpsimd", "vch", C.X1_V)
        pl.dma("gpsimd", C.dbg["d_att"][:, ck * 8 * 512:(ck + 1) * 8 * 512],
               ATT[:], "outw")


def _emit_ffn_all(C, e):
    nc, pl = C.nc, C.pl
    XQ = e(nc.sbuf_tensor("XQ", [128, 16 * 512], BF16))
    Y = e(nc.sbuf_tensor("Y", [128, M_I * 512], BF16))
    X1T = e(nc.sbuf_tensor("X1T", [128, 2 * 512], F32))
    WG = e(nc.sbuf_tensor("WG", [128, 2 * 2048], BF16))
    WU = e(nc.sbuf_tensor("WU", [128, 2 * 2048], BF16))
    WD = e(nc.sbuf_tensor("WD", [128, 4 * 4096], BF16))
    T4 = e(nc.sbuf_tensor("T4", [128, 4 * 512], F32))
    T4P = e(nc.sbuf_tensor("T4P", [128, 4 * 512], F32))
    TI8P = e(nc.sbuf_tensor("TI8P", [128, 512], I8))
    SIG = e(nc.sbuf_tensor("SIG", [128, 2 * 512], BF16))
    AM = e(nc.sbuf_tensor("AM", [128, 512], F32))
    AMT = e(nc.sbuf_tensor("AMT", [128, 512], F32))
    M2 = e(nc.sbuf_tensor("M2", [32, 16], F32))
    SCB = e(nc.sbuf_tensor("SCB", [128, 512], F32))
    BCB2 = e(nc.sbuf_tensor("BCB2", [128, 2 * 512], F32))
    TQ = e(nc.sbuf_tensor("TQ", [128, 512], F32))
    TI8 = e(nc.sbuf_tensor("TI8", [128, 512], I8))
    RT = e(nc.sbuf_tensor("RT", [128, 2 * 512], F32))
    OUTT = e(nc.sbuf_tensor("OUTT", [128, 2 * 512], F32))
    R2B2 = e(nc.sbuf_tensor("R2B2", [128, 512], BF16))
    SQ2 = e(nc.sbuf_tensor("SQ2", [128, 512], BF16))
    bufs = SimpleNamespace(**{k: v for k, v in locals().items()
                              if k not in ("C", "nc", "pl", "e")})
    pl.wait("sync", "agc", C.AGDONE)
    for ck in range(2):
        _emit_ffn_chunk(C, ck, bufs)


def _amax_row(C, amt, m2, src, dstrow):
    pl = C.pl
    pl.wait("vector", "pch", pl.pmark())
    for width in (64, 32):
        pl.v.tensor_copy(amt[0:width, :], src[width:2 * width, :])
        pl.v.tensor_tensor(src[0:width, :], src[0:width, :],
                           amt[0:width, :], ALU.max)
    pl.v.transpose(amt[0:32, :], src[0:32, :])
    pl.v.reduce_max(m2[:],
                    amt[0:32, :].rearrange("p (j c) -> p j c", c=32), AX.X)
    mv = pl.vmark()
    pl.wait("sync", "vch", mv)
    pl.dma("sync", C.qbounce, m2[:], "qbn")
    pl.wait("sync", "qbn", pl.cnt["qbn"])
    pl.dma("sync", dstrow.rearrange("a (j r) -> a j r", r=32),
           C.qbounce.transpose([1, 0]).unsqueeze(0), "qbn",
           allow_slow_non_contiguous=True)
    pl.wait("vector", "qbn", pl.cnt["qbn"])


def _emit_ffn_chunk(C, ck, b):
    nc, pl = C.nc, C.pl
    rows = C.rows
    if ck == 0:
        for eng in ENGS:
            for snm in ("r2w", "xr0", "xr1", "boun", "ldwo",
                        "ldm", "ldxq", "ldwA", "outw"):
                pl.wait(eng, snm, pl.cnt[snm])
    if ck == 1:
        pl.wait("sync", "tms", pl.cnt["tms"])
        pl.wait("sync", "r2r0", pl.cnt["r2r0"])
        pl.wait("sync", "r2r1", pl.cnt["r2r1"])
        pl.wait("gpsimd", "r2r0", pl.cnt["r2r0"])
        pl.wait("gpsimd", "r2r1", pl.cnt["r2r1"])
        pl.wait("vector", "outw", pl.cnt["outw"])
        pl.wait("vector", "r2r0", pl.cnt["r2r0"])
        pl.wait("vector", "r2r1", pl.cnt["r2r1"])
        pl.wait("vector", "pch", pl.cnt["pch"])
        pl.wait("scalar", "pch", pl.cnt["pch"])
    mu = C.TROWS[0:1, 0:512]
    rs = C.TROWS[0:1, 512:1024]
    nb = C.TROWS[0:1, 1024:1536]
    tm = C.TROWS[0:1, 1536:2048]

    def gate_slab(m):
        r, ml = m // 8, m % 8
        return C.scr_all[r, ml * GSL:(ml + 1) * GSL].rearrange(
            "(p n) -> p n", p=128)

    def up_slab(m):
        r, ml = m // 8, m % 8
        return C.scr_all[r, OWN_G + ml * GSL:OWN_G + (ml + 1) * GSL].rearrange(
            "(p n) -> p n", p=128)

    def down_q(otp, q4):
        outs = []
        for rr in range(2):
            r = 2 * q4 + rr
            base_o = 2 * OWN_G + otp * (128 * 2048)
            reg = C.scr_all[r, base_o:base_o + 128 * 2048].rearrange(
                "(p n) -> p n", p=128)
            outs.append(reg)
        return outs

    x1cols = slice(ck * CHT, (ck + 1) * CHT)
    pl.wait("sync", "r2w", C.x1d_done[ck])
    pl.wait("sync", "xqw", C.xqw_done[ck])
    XQD = b.XQ
    for mt in range(16):
        pl.dma("sync", XQD[:, mt * 512:(mt + 1) * 512],
               C.xqd[ck * D + mt * 128:ck * D + (mt + 1) * 128, :], "ldxq")
    xq_ld = pl.cnt["ldxq"]

    wg_used = {}
    wu_used = {}
    y_v = {}
    sig_s = {}
    for m in range(M_I):
        par = m % 2
        for (wsem, wbuf, slab, used) in (("wg%d" % par, b.WG, gate_slab(m),
                                          wg_used),
                                         ("wu%d" % par, b.WU, up_slab(m),
                                          wu_used)):
            if par in used:
                pl.wait("sync", "tms", used[par])
            pl.dma("sync", wbuf[:, par * 2048:(par + 1) * 2048], slab, wsem)
        gb, ub = C.pb[0 + par], C.pb[2 + par]
        pl.wait("tensor", "wg%d" % par, pl.cnt["wg%d" % par])
        pl.wait("tensor", "wu%d" % par, pl.cnt["wu%d" % par])
        pl.wait("tensor", "ldxq", xq_ld)
        if par in y_v:
            pl.wait("tensor", "vch", y_v[par])
        for kt in range(16):
            pl.t.matmul(gb[:, 0:512],
                        b.WG[:, par * 2048 + kt * 128:par * 2048 + (kt + 1) * 128],
                        XQD[:, kt * 512:(kt + 1) * 512],
                        start=(kt == 0), stop=(kt == 15))
        gate_tv = pl.tmark()
        for kt in range(16):
            pl.t.matmul(ub[:, 0:512],
                        b.WU[:, par * 2048 + kt * 128:par * 2048 + (kt + 1) * 128],
                        XQD[:, kt * 512:(kt + 1) * 512],
                        start=(kt == 0), stop=(kt == 15))
        up_tv = pl.tmark()
        wg_used[par] = up_tv
        wu_used[par] = up_tv
        pl.wait("scalar", "tms", gate_tv)
        if par in sig_s:
            pl.wait("scalar", "vch", y_v[par])
        pl.s.activation(b.SIG[:, par * 512:(par + 1) * 512], gb[:, 0:512],
                        ACT.Silu, scale=C.WMC[:, 0:1])
        sig_s[par] = pl.smark()
        pl.wait("vector", "sch", sig_s[par])
        pl.wait("vector", "tms", up_tv)
        pl.v.scalar_tensor_tensor(b.Y[:, m * 512:(m + 1) * 512],
                                  ub[:, 0:512], C.WMC[:, 1:2],
                                  b.SIG[:, par * 512:(par + 1) * 512],
                                  ALU.mult, ALU.mult, nochain=True)
        y_v[par] = pl.vmark()

    yv = max(y_v.values())
    pl.wait("vector", "vch", yv)
    pl.wait("gpsimd", "vch", yv)

    def _quad(eng, m0, s4, t4):
        blk = [b.Y[:, (m0 + kk * s4) * 512:(m0 + kk * s4) * 512 + 512]
               for kk in range(4)]
        t1, t2 = t4[:, 0:512], t4[:, 512:1024]
        t3, tt = t4[:, 1024:1536], t4[:, 1536:2048]
        eng.tensor_tensor(t1, blk[0], blk[1], ALU.add)
        eng.tensor_tensor(t2, blk[0], blk[1], ALU.subtract, nochain=True)
        eng.tensor_tensor(t3, blk[2], blk[3], ALU.add, nochain=True)
        eng.tensor_tensor(tt, blk[2], blk[3], ALU.subtract, nochain=True)
        eng.tensor_tensor(blk[0], t1, t3, ALU.add)
        eng.tensor_tensor(blk[1], t2, tt, ALU.add, nochain=True)
        eng.tensor_tensor(blk[2], t1, t3, ALU.subtract, nochain=True)
        eng.tensor_tensor(blk[3], t2, tt, ALU.subtract, nochain=True)

    for s4 in (1, 4):
        for j in range(16):
            m0 = (j // s4) * 4 * s4 + (j % s4)
            if m0 < 48 or not POOL_BF:
                _quad(pl.v, m0, s4, b.T4)
            else:
                _quad(pl.p, m0, s4, b.T4P)
    pl.wait("vector", "pch", pl.pmark())
    pl.wait("gpsimd", "vch", pl.vmark())
    for j in range(16):
        if j < 11 or not POOL_BF:
            _quad(pl.v, j, 16, b.T4)
        else:
            _quad(pl.p, j, 16, b.T4P)
    bf_v = pl.vmark()
    bf_p = pl.pmark()

    pl.wait("tensor", "vch", bf_v)
    pl.wait("tensor", "pch", bf_p)
    pl.wait("scalar", "vch", bf_v)
    pl.wait("scalar", "pch", bf_p)
    abs_done = {}
    mx_vs = []
    for m in range(M_I):
        par = m % 2
        hb = C.pb[4 + par]
        if par in abs_done:
            pl.wait("tensor", "sch", abs_done[par])
        pl.t.matmul(hb[:, 0:512], C.sh128[:], b.Y[:, m * 512:(m + 1) * 512],
                    start=True, stop=True)
        tv = pl.tmark()
        pl.wait("scalar", "tms", tv)
        pl.s.activation(b.Y[:, m * 512:(m + 1) * 512], hb[:, 0:512], ACT.Copy)
        if m >= 2:
            pl.wait("scalar", "vch", mx_vs[m - 2])
        dst = b.AMT if par == 0 else b.TQ
        pl.s.activation(dst[:], hb[:, 0:512], ACT.Abs)
        abs_done[par] = pl.smark()
        pl.wait("vector", "sch", abs_done[par])
        if m == 0:
            pl.v.tensor_copy(b.AM[:], dst[:])
        else:
            pl.v.tensor_tensor(b.AM[:], b.AM[:], dst[:], ALU.max)
        mx_vs.append(pl.vmark())
    _amax_row(C, b.AMT, b.M2, b.AM, tm)
    pl.v.tensor_scalar(tm, tm, C8192, EPS, ALU.mult, ALU.max)
    pl.v.reciprocal(mu, tm)
    pl.v.tensor_scalar(mu, mu, 128.0, None, ALU.mult)
    pl.v.tensor_scalar(mu, mu, C8192, None, ALU.mult)
    pl.v.tensor_scalar(tm, tm, 1.0 / 128.0, None, ALU.mult)
    pl.v.tensor_single_scalar(tm, tm, rows[0:1, R_WS + 16:R_WS + 17],
                              ALU.mult)
    sc2_v = pl.vmark()
    pl.wait("tensor", "vch", sc2_v)
    pl.t.matmul(C.pb[4][:, 0:512], C.onesrow_f[:], mu, start=True, stop=True)
    pl.t.matmul(C.pb[5][:, 0:512], C.onesrow_f[:], tm, start=True, stop=True)
    tv = pl.tmark()
    pl.wait("vector", "tms", tv)
    pl.v.tensor_copy(b.SCB[:], C.pb[4][:, 0:512])
    pl.v.tensor_copy(b.BCB2[:, 512:1024], C.pb[5][:, 0:512], nochain=True)
    sc2b_v = pl.vmark()

    nv = 56 if POOL_XI else M_I
    if POOL_XI:
        pl.wait("gpsimd", "vch", sc2b_v)
        for m in range(56, M_I):
            pl.p.tensor_tensor(b.T4P[:, 0:512], b.Y[:, m * 512:(m + 1) * 512],
                               b.SCB[:], ALU.mult)
            pl.p.tensor_copy(b.TI8P[:], b.T4P[:, 0:512])
            pl.p.tensor_copy(b.Y[:, m * 512:(m + 1) * 512], b.TI8P[:])
    xi2_p = pl.pmark()
    for m in range(nv):
        pl.v.tensor_tensor(b.TI8[:], b.Y[:, m * 512:(m + 1) * 512], b.SCB[:],
                           ALU.mult)
        pl.v.tensor_copy(b.Y[:, m * 512:(m + 1) * 512], b.TI8[:])
    xi2_v = pl.vmark()

    pl.wait("tensor", "vch", xi2_v)
    pl.wait("tensor", "pch", xi2_p)
    dn_done = {}
    r2_vs = []
    sq_ss = []
    stat_tv = None
    dn_q_tv = {}
    r2w_vs = []
    for otp in range(8):
        ppar = otp % 2
        pa, pb_ = C.pb[0 + 2 * ppar], C.pb[1 + 2 * ppar]
        for q4 in range(4):
            wsem = "wd%d" % q4
            prevq = (otp - 1, q4)
            if prevq in dn_q_tv:
                pl.wait("sync", "tms", dn_q_tv[prevq])
            srcs = down_q(otp, q4)
            for rr in range(2):
                pl.dma("sync", b.WD[:, q4 * 4096 + rr * 2048:
                                    q4 * 4096 + (rr + 1) * 2048],
                       srcs[rr], wsem)
            pl.wait("tensor", wsem, pl.cnt[wsem])
            if otp >= 2 and q4 == 0 and ppar in dn_done:
                pl.wait("tensor", "vch", dn_done[ppar])
            for j in range(16):
                kt = q4 * 16 + j
                base = q4 * 4096 + j * 256
                pl.t.matmul(pa[:, 0:512], b.WD[:, base:base + 128],
                            b.Y[:, kt * 512:(kt + 1) * 512],
                            start=(kt == 0), stop=(kt == 63))
                pl.t.matmul(pb_[:, 0:512], b.WD[:, base + 128:base + 256],
                            b.Y[:, kt * 512:(kt + 1) * 512],
                            start=(kt == 0), stop=(kt == 63))
            dn_q_tv[(otp, q4)] = pl.tmark()
        dn_tv = dn_q_tv[(otp, 3)]
        for oo in range(2):
            ot = 2 * otp + oo
            osl = ot % 2
            xs = "x1r%d" % osl
            rsl = b.RT[:, osl * 512:(osl + 1) * 512]
            if len(r2_vs) >= 2:
                pl.wait("sync", "vch", r2_vs[-2])
            pl.dma("sync", b.X1T[:, osl * 512:(osl + 1) * 512],
                   C.x1d[ot * 128:(ot + 1) * 128, x1cols], xs)
            pl.wait("vector", "tms", dn_tv)
            pl.wait("vector", xs, pl.cnt[xs])
            if ot >= 2:
                pl.wait("vector", "r2w", r2w_vs[ot - 2])
                pl.wait("vector", "sch", sq_ss[ot - 2])
            psrc = pa if oo == 0 else pb_
            pl.v.tensor_tensor(rsl, psrc[:, 0:512],
                               b.BCB2[:, 512:1024], ALU.mult, nochain=True)
            pl.v.tensor_tensor(rsl, rsl,
                               b.X1T[:, osl * 512:(osl + 1) * 512], ALU.add)
            r2_vs.append(pl.vmark())
            pl.wait("gpsimd", "vch", r2_vs[-1])
            pl.dma("gpsimd", C.r2b[ot * 128:(ot + 1) * 128, :], rsl, "r2w")
            r2w_vs.append(pl.cnt["r2w"])
            pl.wait("scalar", "vch", r2_vs[-1])
            if stat_tv is not None:
                pl.wait("scalar", "tms", stat_tv)
            pl.s.activation(b.R2B2[:], rsl, ACT.Copy)
            pl.s.activation(b.SQ2[:], b.R2B2[:], ACT.Square)
            sq_ss.append(pl.smark())
            pl.wait("tensor", "sch", sq_ss[-1])
            pl.t.matmul(C.pst[0:1, 0:512], C.onescol_b[:], b.R2B2[:],
                        start=(ot == 0), stop=(ot == 15))
            pl.t.matmul(C.pb[5][0:1, 0:512], C.onescol_b[:], b.SQ2[:],
                        start=(ot == 0), stop=(ot == 15))
            stat_tv = pl.tmark()
        dn_done[ppar] = r2_vs[-1]

    pl.wait("vector", "tms", stat_tv)
    pl.v.tensor_scalar(mu, C.pst[0:1, 0:512], 1.0 / 2048.0, None, ALU.mult)
    pl.v.tensor_scalar(tm, C.pb[5][0:1, 0:512], 1.0 / 2048.0, None, ALU.mult,
                       nochain=True)
    pl.v.tensor_tensor(nb, mu, mu, ALU.mult)
    pl.v.tensor_tensor(tm, tm, nb, ALU.subtract)
    pl.v.tensor_scalar(tm, tm, EPS, None, ALU.add)
    pl.wait("scalar", "vch", pl.vmark())
    pl.s.activation(rs, tm, ACT.Sqrt)
    pl.wait("vector", "sch", pl.smark())
    pl.v.reciprocal(rs, rs)
    pl.v.tensor_tensor(nb, rs, rs, ALU.mult)
    pl.v.tensor_tensor(nb, nb, tm, ALU.mult)
    pl.v.tensor_scalar(nb, nb, -0.5, 1.5, ALU.mult, ALU.add)
    pl.v.tensor_tensor(rs, rs, nb, ALU.mult)
    pl.v.tensor_tensor(nb, mu, rs, ALU.mult)
    pl.v.tensor_scalar(nb, nb, -1.0, None, ALU.mult)
    rs2_v = pl.vmark()
    pl.wait("tensor", "vch", rs2_v)
    pl.t.matmul(C.pb[4][:, 0:512], C.onesrow_f[:], rs, start=True, stop=True)
    pl.t.matmul(C.pb[5][:, 0:512], C.onesrow_f[:], nb, start=True, stop=True)
    tv = pl.tmark()
    pl.wait("vector", "tms", tv)
    pl.v.tensor_copy(b.SCB[:], C.pb[4][:, 0:512])
    pl.v.tensor_copy(b.BCB2[:, 0:512], C.pb[5][:, 0:512], nochain=True)
    ln2_v = pl.vmark()

    pl.wait("sync", "r2w", pl.cnt["r2w"])
    o_vs, ow_vs = [], []
    for mt in range(16):
        rsem = "r2r%d" % (mt % 2)
        if mt >= 2:
            pl.wait("sync", "vch", o_vs[mt - 2])
        pl.dma("sync", b.RT[:, (mt % 2) * 512:(mt % 2 + 1) * 512],
               C.r2b[mt * 128:(mt + 1) * 128, :], rsem)
        pl.wait("vector", rsem, pl.cnt[rsem])
        pl.wait("vector", "vch", ln2_v)
        ot = b.OUTT[:, (mt % 2) * 512:(mt % 2 + 1) * 512]
        if mt >= 2:
            pl.wait("vector", "outw", ow_vs[mt - 2])
        pl.v.tensor_tensor(ot, b.RT[:, (mt % 2) * 512:(mt % 2 + 1) * 512],
                           b.SCB[:], ALU.mult, nochain=True)
        pl.v.tensor_tensor(ot, ot, b.BCB2[:, 0:512], ALU.add)
        o_vs.append(pl.vmark())
        pl.wait("gpsimd", "vch", o_vs[-1])
        pl.dma("gpsimd", C.out[mt * 128:(mt + 1) * 128, x1cols], ot, "outw")
        ow_vs.append(pl.cnt["outw"])
    if C.debug and ck == DBG_FFN_CK:
        pl.wait("gpsimd", "vch", xi2_v)
        pl.wait("gpsimd", "pch", xi2_p)
        pl.dma("gpsimd", C.dbg["d_y"], b.Y[:], "outw")
        pl.dma("gpsimd", C.dbg["d_xq"], b.XQ[:], "outw")


def _stage_inputs(inputs):
    x = np.asarray(inputs["L0_attn_x"], np.float32)
    consts = _host_consts()
    kch, kcl, ksh, ksl = _rope_tables(np.arange(T))
    masks = {p: _masks(p) for p in range(2)}
    qtabs = {p: _rope_tables(_own_positions(p)) for p in range(2)}
    in_maps = []
    for c in range(NCN):
        b, p = c // 2, c % 2
        xT = np.ascontiguousarray(x[b].T)
        xq = np.ascontiguousarray(xT[:, _own_positions(p)])
        smaskp, tmaskp, smaskl, tmaskl, blend = masks[p]
        qch, qcl, qsh, qsl = qtabs[p]
        in_maps.append(dict(
            xkv=xT, xq=xq,
            wqsem=np.ascontiguousarray(inputs["L0_attn_Wq_sem"], np.float32),
            wqgeo=np.ascontiguousarray(inputs["L0_attn_Wq_geo"], np.float32),
            wksem=np.ascontiguousarray(inputs["L0_attn_Wk_sem"], np.float32),
            wkgeo=np.ascontiguousarray(inputs["L0_attn_Wk_geo"], np.float32),
            wv=np.ascontiguousarray(inputs["L0_attn_Wv"], np.float32),
            wo=np.ascontiguousarray(inputs["L0_attn_Wo"], np.float32),
            lsc=np.ascontiguousarray(inputs["L0_attn_logit_scale"],
                                     np.float32).reshape(1, H),
            wgch=np.ascontiguousarray(
                inputs["L0_ffn_w_gate"][c * 1024:(c + 1) * 1024, :],
                np.float32),
            wuch=np.ascontiguousarray(
                inputs["L0_ffn_w_up"][c * 1024:(c + 1) * 1024, :], np.float32),
            wdch=np.ascontiguousarray(
                inputs["L0_ffn_w_down"][:, c * 1024:(c + 1) * 1024],
                np.float32),
            ident=consts["ident"], identf=consts["identf"],
            h128=consts["h128"],
            permrope=consts["permrope"], poolm=consts["poolm"],
            kcos_h=kch, kcos_l=kcl, ksin_h=ksh, ksin_l=ksl,
            qcos_h=qch, qcos_l=qcl, qsin_h=qsh, qsin_l=qsl,
            smaskp=smaskp, tmaskp=tmaskp, smaskl=smaskl, tmaskl=tmaskl,
            blendrow=blend,
        ))
    return in_maps


def kernel(**inputs):
    in_maps = _stage_inputs(inputs)
    nc = build_nc(debug=DEBUG, part=PART)
    res = run_bass_kernel_spmd(nc, in_maps, core_ids=list(range(NCN)))
    out = np.empty((B, T, D), np.float32)
    for c in range(NCN):
        b, p = c // 2, c % 2
        sh = res.results[c]["out"]
        for s in range(NSLOT):
            t = 2 * s + p
            out[b, t * 128:(t + 1) * 128, :] = sh[:, s * 128:(s + 1) * 128].T
    if DEBUG:
        kernel.last_results = res
    return out
